# version 14
# baseline (speedup 1.0000x reference)
import numpy as np
import ml_dtypes
from contextlib import ExitStack
import concourse.bass as bass
import concourse.mybir as mybir
from concourse.bass_utils import run_bass_kernel_spmd

F32 = mybir.dt.float32
BF16 = mybir.dt.bfloat16
AF = mybir.ActivationFunctionType
ALU = mybir.AluOpType
AX = mybir.AxisListType

NC = 8
D = 1024
TPC = 2048
NT = TPC // 128
ND = D // 128
L = 8192
LN_EPS = 1e-5
QSCALE = float(D) ** -0.5
BF = ml_dtypes.bfloat16


class REng:

    def __init__(self, sem, dsem):
        self.sem = sem
        self.dsem = dsem
        self.n = 0
        self.nd = 0
        self.ops = []
        self.prog = None
        self.dbounds = set()

    def wait(self, tok):
        if tok is not None:
            self.ops.append(("w", tok))
            if self.prog is not None:
                own = self.prog.dsem_owner.get(id(tok[0]))
                if own is not None and own is not self:
                    own.dbounds.add(tok[1])

    def op(self, fn, inc=1):
        self.ops.append(("o", fn, inc, self.sem))
        if inc:
            self.n += inc
            return (self.sem, self.n)
        return None

    def op_sem(self, fn, sem, cum, inc=1):
        self.ops.append(("o", fn, inc, sem))
        return (sem, cum)

    def dma(self, out, in_):
        self.ops.append(("o", lambda e, out=out, in_=in_: e.dma_start(out=out, in_=in_),
                         16, self.dsem))
        self.nd += 16
        return (self.dsem, self.nd)

    def dma_sem(self, out, in_, sem):
        self.ops.append(("o", lambda e, out=out, in_=in_: e.dma_start(out=out, in_=in_),
                         16, sem))

    def apply_dma_fences(self):
        if not self.dbounds:
            return
        bounds = sorted(self.dbounds)
        out = []
        pre = 0
        fenced = 0
        for item in self.ops:
            if item[0] == "o" and item[3] is self.dsem and item[2]:
                need = 0
                for b in bounds:
                    if b <= pre:
                        need = b
                    else:
                        break
                if need > fenced:
                    out.append(("w", (self.dsem, need)))
                    fenced = need
                pre += item[2]
            out.append(item)
        self.ops = out

    def tok(self):
        return (self.sem, self.n)

    def dtok(self):
        return (self.dsem, self.nd)

    def emit(self, e):
        for item in self.ops:
            if item[0] == "w":
                e.wait_ge(item[1][0], item[1][1])
            else:
                ins = item[1](e)
                if item[2]:
                    ins.then_inc(item[3], item[2])


class Prog:

    def __init__(self, nc, stack):
        self.nc = nc
        self.stack = stack
        self.sems = {nm: stack.enter_context(nc.semaphore("s_" + nm))
                     for nm in ("Y", "G", "T", "V", "S", "Yd", "Gd", "Td", "Vd", "Sd")}
        self.Y = REng(self.sems["Y"], self.sems["Yd"])
        self.G = REng(self.sems["G"], self.sems["Gd"])
        self.T = REng(self.sems["T"], self.sems["Td"])
        self.V = REng(self.sems["V"], self.sems["Vd"])
        self.S = REng(self.sems["S"], self.sems["Sd"])
        self.all = [self.Y, self.G, self.T, self.V, self.S]
        self.dsem_owner = {}
        for r in self.all:
            r.prog = self
            self.dsem_owner[id(r.dsem)] = r
        self.phase_stacks = []

    def barrier(self, engs=None):
        engs = engs or self.all
        toks = []
        for r in engs:
            if r.n > 0:
                toks.append((r, (r.sem, r.n)))
            if r.nd > 0:
                toks.append((r, (r.dsem, r.nd)))
        for r in engs:
            for owner, t in toks:
                r.wait(t)

    def sb(self, name, shape, dt):
        st = self.phase_stacks[-1] if self.phase_stacks else self.stack
        return st.enter_context(self.nc.sbuf_tensor(name, shape, dt))

    def ps(self, name, shape, dt=F32):
        st = self.phase_stacks[-1] if self.phase_stacks else self.stack
        return st.enter_context(self.nc.psum_tensor(name, shape, dt))

    def push_phase(self):
        st = ExitStack()
        self.phase_stacks.append(st)
        return st

    def pop_phase(self):
        self.phase_stacks.pop().close()

    def finish(self):
        for r in self.all:
            r.apply_dma_fences()
        for r in self.all:
            if r.n > 0:
                r.wait((r.sem, r.n))
            if r.nd > 0:
                r.wait((r.dsem, r.nd))
        with self.nc.Block() as blk:
            blk.sync(lambda e: self.Y.emit(e))
            blk.gpsimd(lambda e: self.G.emit(e))
            blk.tensor(lambda e: self.T.emit(e))
            blk.vector(lambda e: self.V.emit(e))
            blk.scalar(lambda e: self.S.emit(e))


def make_ident(P, t):
    tz = P.G.op(lambda e, t=t: e.memset(t, 0.0))
    P.G.wait(tz)
    return P.G.op(lambda e, t=t: e.affine_select(
        out=t, in_=t, compare_op=ALU.not_equal, fill=1.0,
        base=0, pattern=[[-1, t.shape[-1]]], channel_multiplier=1))


def make_tri(P, t, kind, fill):
    tz = P.G.op(lambda e, t=t: e.memset(t, 0.0))
    P.G.wait(tz)
    if kind == "su":
        args = dict(compare_op=ALU.is_ge, pattern=[[-1, t.shape[-1]]], channel_multiplier=1)
    elif kind == "ui":
        args = dict(compare_op=ALU.is_gt, pattern=[[-1, t.shape[-1]]], channel_multiplier=1)
    elif kind == "sl":
        args = dict(compare_op=ALU.is_ge, pattern=[[1, t.shape[-1]]], channel_multiplier=-1)
    return P.G.op(lambda e, t=t, a=args: e.affine_select(out=t, in_=t, fill=fill, base=0, **a))


def build_A():
    nc = bass.Bass("TRN2", target_bir_lowering=False, debug=False)
    io = {}
    for nm, shape in [("x", [TPC, D]), ("Wq", [D, D]), ("Wk", [D, D]),
                      ("Wv", [D, D]), ("Wb", [D]), ("l0g", [D]), ("l0b", [D]),
                      ("Wr", [D, 8]), ("br", [8])]:
        io[nm] = nc.dram_tensor(nm, shape, F32, kind="ExternalInput").ap()
    for nm, shape, dt in [("qfm", [D, TPC], BF16), ("kbar", [D, TPC], BF16),
                          ("ktok", [TPC, D], BF16), ("bv", [TPC, D], BF16),
                          ("beta", [TPC, 1], F32), ("Tt", [NT, 128, 128], BF16),
                          ("KQu", [NT, 128, 128], BF16), ("xn2fm", [D, TPC], BF16),
                          ("stats", [TPC, 2], F32), ("w", [TPC, 8], F32)]:
        io[nm] = nc.dram_tensor(nm, shape, dt, kind="ExternalOutput").ap()
    with ExitStack() as stack:
        P = Prog(nc, stack)
        emit_A(P, io)
        P.finish()
    return nc


def emit_A(P, io):
    nc = P.nc
    Y, G, T, V, S = P.Y, P.G, P.T, P.V, P.S
    fused = "xsend" in io

    x_sb = P.sb("a_x", [128, 2, D], F32)
    sqs = P.sb("a_sqs", [128, D], F32)
    sx = P.sb("a_sx", [128, NT], F32)
    sq2 = P.sb("a_sq2", [128, NT], F32)
    mu = P.sb("a_mu", [128, NT], F32)
    var = P.sb("a_var", [128, NT], F32)
    rstd = P.sb("a_rstd", [128, NT], F32)
    rr2 = P.sb("a_rr2", [128, NT], F32)
    tca = P.sb("a_tca", [128, NT], F32)
    tcb = P.sb("a_tcb", [128, NT], F32)
    muvar = P.sb("a_muvar", [128, NT, 2], F32)
    xn0_tm = P.sb("a_xn0tm", [128, D], BF16)
    xn2_pre = P.sb("a_xn2pre", [128, D], F32)
    xn2_t2 = P.sb("a_xn2t2", [128, D], F32)
    xn2_tm = P.sb("a_xn2tm", [128, D], BF16)
    xn0_fm = P.sb("a_xn0fm", [128, ND, TPC], BF16)
    fm_st = P.sb("a_fmst", [128, ND, 128], BF16)
    q_st = [P.sb(f"a_qst{g}", [128, ND, 512], BF16) for g in range(2)]
    ksc = P.sb("a_ksc", [128, D], BF16)
    ksq = P.sb("a_ksq", [128, D], BF16)
    ktok_t = P.sb("a_ktokt", [128, D], BF16)
    vsc = P.sb("a_vsc", [128, D], BF16)
    bv_t = P.sb("a_bvt", [128, D], BF16)
    k2c = P.sb("a_k2c", [128, 1], F32)
    k2h = P.sb("a_k2h", [128, 1], F32)
    rinv = P.sb("a_rinv", [128, 1], F32)
    beta_c = P.sb("a_betac", [128, NT], F32)
    beta_r = P.sb("a_betar", [1, TPC], F32)
    gb_bc = P.sb("a_gbbc", [128, D], F32)
    bb_bc = P.sb("a_bbbc", [128, D], F32)
    grow = P.sb("a_grow", [1, D], F32)
    brow = P.sb("a_brow", [1, D], F32)
    Wq_sb = P.sb("a_Wq", [128, ND, D], BF16)
    Wk_sb = P.sb("a_Wk", [128, ND, D], BF16)
    Wv_sb = P.sb("a_Wv", [128, ND, D], BF16)
    Wb_pad = P.sb("a_Wbp", [128, 128], BF16)
    Wb_sb = P.sb("a_Wb", [128, ND], BF16)
    ident = P.sb("a_id", [128, 128], BF16)
    ones_f1 = P.sb("a_o1", [1, 512], F32)
    mSUn = P.sb("a_msun", [128, 128], F32)
    mSLn = P.sb("a_msln", [128, 128], F32)
    mUI = P.sb("a_mui", [128, 128], F32)
    bbc = P.sb("a_bbc", [128, 128], F32)
    Xt = [P.sb(f"a_X{i}", [128, 128], BF16) for i in range(2)]
    Yt = [P.sb(f"a_Y{i}", [128, 128], BF16) for i in range(2)]
    Pt = [P.sb(f"a_P{i}", [128, 128], BF16) for i in range(2)]
    kq_sb = P.sb("a_kq", [128, 128], BF16)
    x0_sb = P.sb("a_x0", [128, 128], F32)

    eps_c = P.sb("a_eps", [128, 1], F32)
    ident_f = P.sb("a_idf", [128, 128], F32)
    xn2f = P.sb("a_xn2f", [128, D], F32)
    x2f_st = P.sb("a_x2fst", [128, ND, 128], F32)
    Wrf = P.sb("a_Wrf", [128, ND, 8], F32)
    brf = P.sb("a_brf", [1, 8], F32)
    probs = P.sb("a_probs", [128, 8], F32)
    Px = P.sb("a_Px", [128, 8], F32)
    msk = P.sb("a_msk", [128, 8], F32)
    m1 = P.sb("a_m1", [128, 1], F32)
    m2 = P.sb("a_m2", [128, 1], F32)
    nm_ = P.sb("a_nm", [128, 1], F32)
    rs_ = P.sb("a_rs", [128, 1], F32)
    w_t = P.sb("a_wt", [128, 8], F32)
    ppA = P.ps("a_ppA", [128, 512])
    ppB = P.ps("a_ppB", [128, 512])
    ptA = P.ps("a_ptA", [128, 128], BF16)
    ptB = P.ps("a_ptB", [128, 128], BF16)
    pkA = P.ps("a_pkA", [128, 128])
    pkB = P.ps("a_pkB", [128, 128])
    prow = P.ps("a_prow", [1, 512])
    pcol = P.ps("a_pcol", [128, 1])

    make_ident(P, ident[:])
    make_tri(P, mSUn[:], "su", -1.0)
    make_tri(P, mSLn[:], "sl", -1.0)
    make_tri(P, mUI[:], "ui", 1.0)
    G.op(lambda e: e.memset(Wb_pad[:], 0.0))
    G.op(lambda e: e.memset(ones_f1[:], 1.0))
    G.op(lambda e: e.memset(eps_c[:], LN_EPS))
    tzf = G.op(lambda e: e.memset(ident_f[:], 0.0))
    G.wait(tzf)
    G.op(lambda e: e.affine_select(out=ident_f[:], in_=ident_f[:],
                                   compare_op=ALU.not_equal, fill=1.0,
                                   base=0, pattern=[[-1, 128]], channel_multiplier=1))
    for r in range(ND):
        G.dma(Wq_sb[:, r, :], io["Wq"][128 * r:128 * (r + 1), :])
        G.dma(Wk_sb[:, r, :], io["Wk"][128 * r:128 * (r + 1), :])
        G.dma(Wv_sb[:, r, :], io["Wv"][128 * r:128 * (r + 1), :])
        Y.dma(Wrf[:, r, :], io["Wr"][128 * r:128 * (r + 1), :])
    G.dma(Wb_pad[0:ND, :], io["Wb"].rearrange("(r p) -> r p", p=128))
    Y.dma(grow[:], io["l0g"][None, :])
    Y.dma(brow[:], io["l0b"][None, :])
    Y.dma(brf[:], io["br"][None, :])
    P.barrier()
    T.op(lambda e: e.transpose(ptA[:], Wb_pad[:], ident[:]))
    P.barrier([T, V])
    V.op(lambda e: e.tensor_copy(Wb_sb[:], ptA[:, 0:ND]))
    for h in range(2):
        T.op(lambda e, h=h: e.matmul(ppA[:], ones_f1[0:1, 0:128],
                                     grow[:, 512 * h:512 * (h + 1)], start=True, stop=True))
        T.op(lambda e, h=h: e.matmul(ppB[:], ones_f1[0:1, 0:128],
                                     brow[:, 512 * h:512 * (h + 1)], start=True, stop=True))
        P.barrier([T, S])
        S.op(lambda e, h=h: e.copy(gb_bc[:, 512 * h:512 * (h + 1)], ppA[:]))
        S.op(lambda e, h=h: e.copy(bb_bc[:, 512 * h:512 * (h + 1)], ppB[:]))
        P.barrier([T, S])

    t_sq = None
    t_cons = [None, None]
    for i in range(NT):
        b = i % 2
        if t_cons[b] is not None:
            Y.wait(t_cons[b])
        t_x = Y.dma(x_sb[:, b, :], io["x"][128 * i:128 * (i + 1), :])
        V.wait(t_x)
        V.op(lambda e, b=b, i=i: e.reduce_sum(sx[:, i:i + 1], x_sb[:, b, :], axis=AX.X))
        S.wait(t_x)
        if t_sq:
            S.wait(t_sq)
        t_sq = S.op(lambda e, b=b, i=i: e.activation(sqs[:], x_sb[:, b, :], AF.Square,
                                                     accum_out=sq2[:, i:i + 1]))
        t_cons[b] = t_sq
        Y.wait((V.sem, V.n))
    P.barrier([V, S])
    c1 = 1.0 / D

    def vch(fn):
        t = V.op(fn)
        V.wait(t)
        return t

    vch(lambda e: e.tensor_scalar(mu[:], sx[:], c1, None, ALU.mult))
    vch(lambda e: e.tensor_scalar(tca[:], sq2[:], c1, None, ALU.mult))
    vch(lambda e: e.tensor_tensor(tcb[:], mu[:], mu[:], ALU.mult))
    t = vch(lambda e: e.tensor_tensor(var[:], tca[:], tcb[:], ALU.subtract))
    S.wait(t)
    t = S.op(lambda e: e.activation(tca[:], var[:], AF.Sqrt, bias=eps_c[:, 0:1]))
    V.wait(t)
    vch(lambda e: e.reciprocal(rstd[:], tca[:]))
    vch(lambda e: e.tensor_scalar(tca[:], var[:], LN_EPS, None, ALU.add))
    vch(lambda e: e.reciprocal(tcb[:], tca[:]))
    t = vch(lambda e: e.tensor_tensor(tcb[:], var[:], tcb[:], ALU.mult))
    S.wait(t)
    t = S.op(lambda e: e.activation(tca[:], tcb[:], AF.Sqrt, bias=eps_c[:, 0:1]))
    V.wait(t)
    vch(lambda e: e.reciprocal(tcb[:], tca[:]))
    vch(lambda e: e.tensor_tensor(rr2[:], rstd[:], tcb[:], ALU.mult))
    V.op(lambda e: e.tensor_copy(muvar[:, :, 0], mu[:]), inc=0)
    t = V.op(lambda e: e.tensor_copy(muvar[:, :, 1], var[:]))
    G.wait(t)
    G.dma(io["stats"].rearrange("(i p) s -> p i s", p=128), muvar[:])
    P.barrier()

    tr = {"i": 0, "ev": [None, None]}
    trf = {"i": 0, "ev": [None, None]}

    def transpose_to_f(dst_ap, src_ap, src_tok):
        b = trf["i"] % 2
        trf["i"] += 1
        pt = pkA if b == 0 else pkB
        T.wait(src_tok)
        if trf["ev"][b] is not None:
            T.wait(trf["ev"][b])
        tmm = T.op(lambda e, pt=pt, s=src_ap: e.transpose(pt[:], s, ident_f[:]))
        if b == 0:
            V.wait(tmm)
            ev = V.op(lambda e, pt=pt, d=dst_ap: e.tensor_copy(d, pt[:]))
        else:
            S.wait(tmm)
            ev = S.op(lambda e, pt=pt, d=dst_ap: e.copy(d, pt[:]))
        trf["ev"][b] = ev
        return ev

    def transpose_to(dst_ap, src_ap, src_tok):
        b = tr["i"] % 2
        tr["i"] += 1
        pt = ptA if b == 0 else ptB
        T.wait(src_tok)
        if tr["ev"][b] is not None:
            T.wait(tr["ev"][b])
        tmm = T.op(lambda e, pt=pt, s=src_ap: e.transpose(pt[:], s, ident[:]))
        if b == 0:
            V.wait(tmm)
            ev = V.op(lambda e, pt=pt, d=dst_ap: e.tensor_copy(d, pt[:]))
        else:
            S.wait(tmm)
            ev = S.op(lambda e, pt=pt, d=dst_ap: e.copy(d, pt[:]))
        tr["ev"][b] = ev
        return ev

    t_xcons = [None, None]
    for i in range(NT):
        b = i % 2
        if t_xcons[b] is not None:
            Y.wait(t_xcons[b])
        t_x = Y.dma(x_sb[:, b, :], io["x"][128 * i:128 * (i + 1), :])
        V.wait(t_x)
        t0 = V.op(lambda e, b=b, i=i: e.tensor_scalar(
            xn0_tm[:], x_sb[:, b, :], mu[:, i:i + 1], rstd[:, i:i + 1],
            ALU.subtract, ALU.mult))
        for r in range(ND):
            ev = transpose_to(xn0_fm[:, r, 128 * i:128 * (i + 1)],
                              xn0_tm[:, 128 * r:128 * (r + 1)], t0)
        V.wait(ev)
        t1 = V.op(lambda e, b=b, i=i: e.tensor_scalar(
            xn2_pre[:], x_sb[:, b, :], mu[:, i:i + 1], rr2[:, i:i + 1],
            ALU.subtract, ALU.mult))
        t_xcons[b] = t1
        V.wait(t1)
        t1 = V.op(lambda e: e.tensor_tensor(xn2_t2[:], xn2_pre[:], gb_bc[:], ALU.mult))
        V.wait(t1)
        t1b = V.op(lambda e: e.tensor_tensor(xn2_tm[:], xn2_t2[:], bb_bc[:], ALU.add))
        V.wait(t1b)
        t1 = V.op(lambda e: e.tensor_tensor(xn2f[:], xn2_t2[:], bb_bc[:], ALU.add))
        for r in range(ND):
            ev = transpose_to(fm_st[:, r, :], xn2_tm[:, 128 * r:128 * (r + 1)], t1b)
        G.wait(ev)
        t_fmdma = G.dma(io["xn2fm"][:, 128 * i:128 * (i + 1)]
                        .rearrange("(r p) t -> p r t", p=128), fm_st[:])
        evf = None
        for r in range(ND):
            evf = transpose_to_f(x2f_st[:, r, :], xn2f[:, 128 * r:128 * (r + 1)], t1)
        T.wait(evf)
        for r in range(ND):
            T.op(lambda e, r=r: e.matmul(
                ppA[:, 0:8], x2f_st[:, r, :], Wrf[:, r, :],
                start=(r == 0), stop=False), inc=0)
        t_mm8 = T.op(lambda e: e.matmul(ppA[:, 0:8], ones_f1[0:1, 0:128], brf[:],
                                        start=False, stop=True))
        V.wait(t_mm8)

        def vc0(fn):
            t = V.op(fn)
            V.wait(t)
            return t
        vc0(lambda e: e.reduce_max(m1[:], ppA[:, 0:8], axis=AX.X))
        vc0(lambda e: e.tensor_scalar(nm_[:], m1[:], -1.0, None, ALU.mult))
        tpx = vc0(lambda e: e.tensor_copy(Px[:], ppA[:, 0:8]))
        S.wait(tpx)
        tpx = S.op(lambda e: e.activation(Px[:], Px[:], AF.Exp, bias=nm_[:, 0:1]))
        V.wait(tpx)
        vc0(lambda e: e.reduce_sum(m2[:], Px[:], axis=AX.X))
        vc0(lambda e: e.reciprocal(rs_[:], m2[:]))
        vc0(lambda e: e.tensor_scalar(probs[:], Px[:], rs_[:, 0:1], None, ALU.mult))
        vc0(lambda e: e.reduce_max(m1[:], probs[:], axis=AX.X))
        vc0(lambda e: e.tensor_scalar(msk[:], probs[:], m1[:, 0:1], None, ALU.is_equal))
        vc0(lambda e: e.scalar_tensor_tensor(Px[:], msk[:], -1e30, probs[:],
                                             ALU.mult, ALU.add))
        vc0(lambda e: e.reduce_max(m2[:], Px[:], axis=AX.X))
        vc0(lambda e: e.tensor_scalar(msk[:], probs[:], m2[:, 0:1], None, ALU.is_ge))
        t_w = vc0(lambda e: e.tensor_tensor(w_t[:], probs[:], msk[:], ALU.mult))
        G.wait(t_w)
        t_wdma2 = G.dma(io["w"][128 * i:128 * (i + 1), :], w_t[:])
        V.wait(t_wdma2)
        T.wait(t_mm8)
        T.wait(t_fmdma)
    P.barrier()

    pp = [ppA, ppB]
    ppev = [None, None]
    ppi = [0]

    def mm_group(dst_bank_tok_consumer, emit_fn):
        b = ppi[0] % 2
        ppi[0] += 1
        if ppev[b] is not None:
            T.wait(ppev[b])
        tmm = emit_fn(pp[b])
        return b, pp[b], tmm

    def set_ppev(b, tok):
        ppev[b] = tok

    cpv = {}
    qdma_tok = [None, None]
    for g in range(4):
        qs = q_st[g % 2]
        for dq in range(ND):
            def mms(ppt, dq=dq, g=g):
                for r in range(ND):
                    ins = T.op(lambda e, r=r, dq=dq, g=g, ppt=ppt: e.matmul(
                        ppt[:], Wq_sb[:, r, 128 * dq:128 * (dq + 1)],
                        xn0_fm[:, r, 512 * g:512 * (g + 1)],
                        start=(r == 0), stop=(r == ND - 1)), inc=(1 if r == ND - 1 else 0))
                return ins
            b, ppt, tmm = mm_group(None, mms)
            S.wait(tmm)
            if dq == 0 and qdma_tok[g % 2] is not None:
                S.wait(qdma_tok[g % 2])
            ev = S.op(lambda e, ppt=ppt, dq=dq, qs=qs: e.activation(
                qs[:, dq, :], ppt[:], AF.Copy, scale=QSCALE))
            set_ppev(b, ev)
        G.wait(ev)
        if fused:
            s_, coff = g // 2, 512 * (g % 2)
            for dq in range(ND):
                for j in range(4):
                    rq = 1024 * (4 * s_ + j) + 128 * dq
                    G.dma(io["xsend"][rq:rq + 128, coff:coff + 512], qs[:, dq, :])
        else:
            for dq in range(ND):
                G.dma(io["qfm"][128 * dq:128 * (dq + 1), 512 * g:512 * (g + 1)], qs[:, dq, :])
        qdma_tok[g % 2] = G.dtok()
        for r in range(ND):
            tmm = T.op(lambda e, r=r, g=g: e.matmul(
                prow[:], Wb_sb[:, r:r + 1], xn0_fm[:, r, 512 * g:512 * (g + 1)],
                start=(r == 0), stop=(r == ND - 1)), inc=(1 if r == ND - 1 else 0))
        S.wait(tmm)
        t_brow = S.op(lambda e, g=g: e.activation(
            beta_r[:, 512 * g:512 * (g + 1)], prow[:], AF.Sigmoid))
        T.wait(t_brow)

        for ic in range(4):
            i = 4 * g + ic
            for r in range(ND):
                tmm = T.op(lambda e, r=r, i=i: e.matmul(
                    pcol[:], xn0_fm[:, r, 128 * i:128 * (i + 1)], Wb_sb[:, r:r + 1],
                    start=(r == 0), stop=(r == ND - 1)), inc=(1 if r == ND - 1 else 0))
            S.wait(tmm)
            t_bc = S.op(lambda e, i=i: e.activation(beta_c[:, i:i + 1], pcol[:], AF.Sigmoid))
            T.wait(t_bc)

            evs = []
            for h in range(2):
                def mms(ppt, i=i, h=h):
                    for r in range(ND):
                        ins = T.op(lambda e, r=r, i=i, h=h, ppt=ppt: e.matmul(
                            ppt[:], xn0_fm[:, r, 128 * i:128 * (i + 1)],
                            Wk_sb[:, r, 512 * h:512 * (h + 1)],
                            start=(r == 0), stop=(r == ND - 1)), inc=(1 if r == ND - 1 else 0))
                    return ins
                b, ppt, tmm = mm_group(None, mms)
                S.wait(tmm)
                ev = S.op(lambda e, ppt=ppt, h=h: e.activation(
                    ksc[:, 512 * h:512 * (h + 1)], ppt[:], AF.Copy))
                set_ppev(b, ev)
                evs.append(ev)
            S.wait(evs[-1])
            tsq = S.op(lambda e: e.activation(ksq[:], ksc[:], AF.Square))
            V.wait(tsq)
            tv = V.op(lambda e: e.reduce_sum(k2c[:], ksq[:], axis=AX.X))
            S.wait(tv)
            ts2 = S.op(lambda e: e.activation(k2h[:], k2c[:], AF.Sqrt))
            V.wait(ts2)
            tv = V.op(lambda e: e.tensor_scalar(k2h[:], k2h[:], 1e-12, None, ALU.max))
            V.wait(tv)
            tv = V.op(lambda e: e.reciprocal(rinv[:], k2h[:]))
            V.wait(tv)
            t_kn = V.op(lambda e: e.tensor_scalar(ktok_t[:], ksc[:], rinv[:, 0:1],
                                                  None, ALU.mult))
            G.wait(t_kn)
            s_ = i // 8
            tloc = 128 * (i - 8 * s_)
            if fused:
                for j in range(4):
                    r0 = 1024 * (4 * s_ + j)
                    t_ktdma = G.dma(io["xsend"][r0 + tloc:r0 + tloc + 128, 2048:3072],
                                    ktok_t[:])
            else:
                t_ktdma = G.dma(io["ktok"][128 * i:128 * (i + 1), :], ktok_t[:])
            for r in range(ND):
                ev = transpose_to(fm_st[:, r, :], ktok_t[:, 128 * r:128 * (r + 1)], t_kn)
            G.wait(ev)
            if fused:
                for j in range(4):
                    r0 = 1024 * (4 * s_ + j)
                    t_kbdma = G.dma(io["xsend"][r0:r0 + 1024,
                                               1024 + tloc:1024 + tloc + 128]
                                    .rearrange("(r p) t -> p r t", p=128), fm_st[:])
            else:
                t_kbdma = G.dma(io["kbar"][:, 128 * i:128 * (i + 1)]
                                .rearrange("(r p) t -> p r t", p=128), fm_st[:])

            evs = []
            for h in range(2):
                def mms(ppt, i=i, h=h):
                    for r in range(ND):
                        ins = T.op(lambda e, r=r, i=i, h=h, ppt=ppt: e.matmul(
                            ppt[:], xn0_fm[:, r, 128 * i:128 * (i + 1)],
                            Wv_sb[:, r, 512 * h:512 * (h + 1)],
                            start=(r == 0), stop=(r == ND - 1)), inc=(1 if r == ND - 1 else 0))
                    return ins
                b, ppt, tmm = mm_group(None, mms)
                S.wait(tmm)
                ev = S.op(lambda e, ppt=ppt, h=h: e.activation(
                    vsc[:, 512 * h:512 * (h + 1)], ppt[:], AF.Silu))
                set_ppev(b, ev)
                evs.append(ev)
            V.wait(evs[-1])
            V.wait(t_bc)
            t_bv = V.op(lambda e, i=i: e.tensor_scalar(
                bv_t[:], vsc[:], beta_c[:, i:i + 1], None, ALU.mult))
            G.wait(t_bv)
            if fused:
                for j in range(4):
                    r0 = 1024 * (4 * s_ + j)
                    G.dma(io["xsend"][r0 + tloc:r0 + tloc + 128, 3072:3328],
                          bv_t[:, 256 * j:256 * (j + 1)])
            else:
                G.dma(io["bv"][128 * i:128 * (i + 1), :], bv_t[:])

            if cpv.get("t_kqe") is not None:
                T.wait(cpv["t_kqe"])
            T.wait(t_brow)
            t_bb = T.op(lambda e, i=i: e.matmul(pkB[:], ones_f1[0:1, 0:128],
                                                beta_r[:, 128 * i:128 * (i + 1)],
                                                start=True, stop=True))
            S.wait(t_bb)
            if cpv.get("v4") is not None:
                S.wait(cpv["v4"])
            t_bbc = S.op(lambda e: e.copy(bbc[:], pkB[:]))
            T.wait(ev)
            if cpv.get("vP") is not None:
                T.wait(cpv["vP"])
            for r in range(ND):
                t_kk = T.op(lambda e, r=r: e.matmul(
                    pkA[:], fm_st[:, r, :], fm_st[:, r, :],
                    start=(r == 0), stop=(r == ND - 1)), inc=(1 if r == ND - 1 else 0))
            V.wait(t_kk)
            if cpv.get("tC") is not None:
                V.wait(cpv["tC"])
            V.op(lambda e: e.tensor_tensor(x0_sb[:], pkA[:], mSLn[:], ALU.mult), inc=0)
            V.wait(t_bc)
            V.op(lambda e, i=i: e.tensor_scalar(Xt[0][:], x0_sb[:],
                                                beta_c[:, i:i + 1], None, ALU.mult),
                 inc=0)
            V.op(lambda e: e.tensor_tensor(x0_sb[:], pkA[:], mSUn[:], ALU.mult), inc=0)
            V.wait(t_bbc)
            if cpv.get("s2") is not None:
                V.wait(cpv["s2"])
            v4 = V.op(lambda e: e.tensor_tensor(Yt[0][:], x0_sb[:], bbc[:], ALU.mult))
            if cpv.get("t_Tt") is not None:
                V.wait(cpv["t_Tt"])
            v5 = V.op(lambda e: e.tensor_tensor(Pt[0][:], Yt[0][:], ident[:], ALU.add))
            cur = 0
            vP = v5
            s1 = s2 = None
            tC = None
            for lvl in range(6):
                nxt = 1 - cur
                T.wait(v5 if lvl == 0 else s2)
                T.wait(vP)
                if lvl == 0:
                    T.wait(t_bbc)
                tA = T.op(lambda e, c=cur: e.matmul(pkA[:], Yt[c][:], Xt[c][:],
                                                    start=True, stop=True))
                tB = T.op(lambda e, c=cur: e.matmul(pkB[:], Xt[c][:], Yt[c][:],
                                                    start=True, stop=True))
                S.wait(tA)
                s1 = S.op(lambda e, nx=nxt: e.copy(Xt[nx][:], pkA[:]))
                S.wait(tB)
                s2 = S.op(lambda e, nx=nxt: e.copy(Yt[nx][:], pkB[:]))
                T.wait(s1)
                tC = T.op(lambda e, nx=nxt, c=cur: e.matmul(
                    pkA[:], Xt[nx][:], Pt[c][:], start=True, stop=True))
                V.wait(tC)
                vP = V.op(lambda e, nx=nxt, c=cur: e.tensor_tensor(
                    Pt[nx][:], pkA[:], Pt[c][:], ALU.add))
                cur = nxt
            G.wait(vP)
            if fused:
                for j in range(4):
                    r0 = 1024 * (4 * s_ + j)
                    t_Tt = G.dma(io["xsend"][r0 + tloc:r0 + tloc + 128, 3328:3456],
                                 Pt[cur][:])
            else:
                t_Tt = G.dma(io["Tt"][i, :, :], Pt[cur][:])
            T.wait(s2)
            for r in range(ND):
                t_kq = T.op(lambda e, r=r, ic=ic, qs=qs: e.matmul(
                    pkB[:], fm_st[:, r, :], qs[:, r, 128 * ic:128 * (ic + 1)],
                    start=(r == 0), stop=(r == ND - 1)), inc=(1 if r == ND - 1 else 0))
            V.wait(t_kq)
            if cpv.get("t_kqd") is not None:
                V.wait(cpv["t_kqd"])
            t_kqe = V.op(lambda e: e.tensor_tensor(kq_sb[:], pkB[:], mUI[:], ALU.mult))
            G.wait(t_kqe)
            if fused:
                for j in range(4):
                    r0 = 1024 * (4 * s_ + j)
                    t_kqd = G.dma(io["xsend"][r0 + tloc:r0 + tloc + 128, 3456:3584],
                                  kq_sb[:])
                G.wait(t_bc)
                for j in range(4):
                    r0 = 1024 * (4 * s_ + j)
                    G.dma(io["besend"][r0 + tloc:r0 + tloc + 128, :], beta_c[:, i:i + 1])
            else:
                t_kqd = G.dma(io["KQu"][i, :, :], kq_sb[:])
                G.wait(t_bc)
                G.dma(io["beta"][128 * i:128 * (i + 1), :], beta_c[:, i:i + 1])
            T.wait(t_kbdma)
            cpv = {"t_kqe": t_kqe, "vP": vP, "tC": tC, "s2": s2, "t_Tt": t_Tt,
                   "v4": v4, "t_kqd": t_kqd}


def shard_tokens(arr):
    return [np.concatenate([arr[0, 1024 * c:1024 * (c + 1)],
                            arr[1, 1024 * c:1024 * (c + 1)]], axis=0)
            for c in range(NC)]


def unshard_tokens(parts):
    s0 = np.concatenate([np.asarray(parts[c])[0:1024] for c in range(NC)], axis=0)
    s1 = np.concatenate([np.asarray(parts[b])[1024:2048] for b in range(8)], axis=0)
    return np.stack([s0, s1], axis=0)


def run_A(inputs):
    x = np.ascontiguousarray(inputs["L0_attn_x"])
    xs = shard_tokens(x)
    base = {"Wq": inputs["L0_attn_Wq"], "Wk": inputs["L0_attn_Wk"],
            "Wv": inputs["L0_attn_Wv"], "Wb": inputs["L0_attn_Wb"],
            "l0g": inputs["L0_ffn_gamma"], "l0b": inputs["L0_ffn_beta"]}
    base = {k: np.ascontiguousarray(v) for k, v in base.items()}
    in_maps = [dict(base, x=np.ascontiguousarray(xs[c])) for c in range(NC)]
    nc = build_A()
    res = run_bass_kernel_spmd(nc, in_maps, core_ids=list(range(NC)))
    return res.results


NCH = 64
DV = 256


def build_B():
    nc = bass.Bass("TRN2", target_bir_lowering=False, debug=False)
    io = {}
    io["qfm"] = nc.dram_tensor("qfm", [D, L], BF16, kind="ExternalInput").ap()
    io["kbar"] = nc.dram_tensor("kbar", [D, L], BF16, kind="ExternalInput").ap()
    io["ktok"] = nc.dram_tensor("ktok", [L, D], BF16, kind="ExternalInput").ap()
    io["bv"] = nc.dram_tensor("bv", [L, DV], BF16, kind="ExternalInput").ap()
    io["beta"] = nc.dram_tensor("beta", [L, 1], F32, kind="ExternalInput").ap()
    io["Tt"] = nc.dram_tensor("Tt", [NCH, 128, 128], BF16, kind="ExternalInput").ap()
    io["KQu"] = nc.dram_tensor("KQu", [NCH, 128, 128], BF16, kind="ExternalInput").ap()
    io["o"] = nc.dram_tensor("o", [L, DV], F32, kind="ExternalOutput").ap()
    with ExitStack() as stack:
        P = Prog(nc, stack)
        emit_B(P, io)
        P.finish()
    return nc


def emit_B(P, io):
    Y, G, T, V, S = P.Y, P.G, P.T, P.V, P.S
    q_c = P.sb("b_q", [128, 2, ND, 128], BF16)
    kb_c = P.sb("b_kb", [128, 2, ND, 128], BF16)
    kt_c = P.sb("b_kt", [128, 2, D], BF16)
    bv_c = P.sb("b_bv", [128, 2, DV], BF16)
    be_c = P.sb("b_be", [128, 2, 1], F32)
    Tt_c = P.sb("b_Tt", [128, 2, 128], BF16)
    KQ_c = P.sb("b_KQ", [128, 2, 128], BF16)
    S_f = P.sb("b_Sf", [128, ND, DV], F32)
    S_b = P.sb("b_Sb", [128, ND, DV], BF16)
    U_sb = P.sb("b_U", [128, DV], BF16)
    R_sb = P.sb("b_R", [128, DV], BF16)
    negb = P.sb("b_negb", [128, 2, 1], F32)
    o_t = P.sb("b_ot", [128, 2, DV], F32)

    pW = P.ps("b_pW", [128, DV])
    pU = P.ps("b_pU", [128, DV])
    pO = P.ps("b_pO", [128, DV])
    pS = P.ps("b_pS", [128, ND, DV])

    t_z1 = G.op(lambda e: e.memset(S_f[:], 0.0))
    t_z2 = G.op(lambda e: e.memset(S_b[:], 0.0))

    fused = "xr" in io

    def issue_loads(ci):
        b = ci % 2
        blk_ = ci // 8
        w_ = ci % 8
        toks = []
        if not fused:
            toks.append(Y.dma(q_c[:, b, :, :],
                              io["qfm"][:, 128 * ci:128 * (ci + 1)]
                              .rearrange("(r p) t -> p r t", p=128)))
            toks.append(Y.dma(kb_c[:, b, :, :],
                              io["kbar"][:, 128 * ci:128 * (ci + 1)]
                              .rearrange("(r p) t -> p r t", p=128)))
            toks.append(Y.dma(Tt_c[:, b, :], io["Tt"][ci, :, :]))
            toks.append(Y.dma(KQ_c[:, b, :], io["KQu"][ci, :, :]))
            toks.append(G.dma(kt_c[:, b, :], io["ktok"][128 * ci:128 * (ci + 1), :]))
            toks.append(G.dma(bv_c[:, b, :], io["bv"][128 * ci:128 * (ci + 1), :]))
            toks.append(G.dma(be_c[:, b, :], io["beta"][128 * ci:128 * (ci + 1), :]))
        else:
            xr = io["xr"]
            rtok = 1024 * blk_ + 128 * w_
            toks.append(Y.dma(q_c[:, b, :, :],
                              xr[1024 * blk_:1024 * (blk_ + 1), 128 * w_:128 * (w_ + 1)]
                              .rearrange("(r p) t -> p r t", p=128)))
            toks.append(Y.dma(kb_c[:, b, :, :],
                              xr[1024 * blk_:1024 * (blk_ + 1),
                                 1024 + 128 * w_:1024 + 128 * (w_ + 1)]
                              .rearrange("(r p) t -> p r t", p=128)))
            toks.append(Y.dma(Tt_c[:, b, :], xr[rtok:rtok + 128, 3328:3456]))
            toks.append(Y.dma(KQ_c[:, b, :], xr[rtok:rtok + 128, 3456:3584]))
            toks.append(G.dma(kt_c[:, b, :], xr[rtok:rtok + 128, 2048:3072]))
            toks.append(G.dma(bv_c[:, b, :], xr[rtok:rtok + 128, 3072:3328]))
            toks.append(G.dma(be_c[:, b, :], io["ber"][rtok:rtok + 128, :]))
        return toks

    cons_tok = [[], []]
    odma_tok = [None, None]
    t_loads = {0: issue_loads(0)}
    cp_tok = [t_z2] * 4
    t_oev = None
    t_Rlast = None
    for ci in range(NCH):
        b = ci % 2
        if ci + 1 < NCH:
            for tok in cons_tok[(ci + 1) % 2]:
                P.Y.wait(tok)
                P.G.wait(tok)
            t_loads[ci + 1] = issue_loads(ci + 1)
        tl = t_loads.pop(ci)
        T.wait(tl[0])
        T.wait(tl[1])
        if t_oev is not None:
            T.wait(t_oev)
        for r in range(ND):
            if r % 2 == 0:
                T.wait(cp_tok[r // 2])
            t_W = T.op(lambda e, r=r, b=b: e.matmul(
                pW[:], kb_c[:, b, r, :], S_b[:, r, :],
                start=(r == 0), stop=(r == ND - 1)), inc=(1 if r == ND - 1 else 0))
        for r in range(ND):
            t_O1 = T.op(lambda e, r=r, b=b: e.matmul(
                pO[:], q_c[:, b, r, :], S_b[:, r, :],
                start=(r == 0), stop=False), inc=(1 if r == ND - 1 else 0))
        V.wait(tl[6])
        tv = V.op(lambda e, b=b: e.tensor_scalar(negb[:, b, :], be_c[:, b, :],
                                                 -1.0, None, ALU.mult))
        V.wait(tv)
        V.wait(t_W)
        V.wait(tl[5])
        t_R = V.op(lambda e, b=b: e.scalar_tensor_tensor(
            R_sb[:], pW[:], negb[:, b, 0:1], bv_c[:, b, :], ALU.mult, ALU.add))
        t_Rlast = t_R
        T.wait(t_R)
        T.wait(tl[2])
        t_Umm = T.op(lambda e, b=b: e.matmul(pU[:], Tt_c[:, b, :], R_sb[:],
                                             start=True, stop=True))
        S.wait(t_Umm)
        t_Uev = S.op(lambda e: e.copy(U_sb[:], pU[:]))
        T.wait(t_Uev)
        T.wait(tl[3])
        t_O2 = T.op(lambda e, b=b: e.matmul(pO[:], KQ_c[:, b, :], U_sb[:],
                                            start=False, stop=True))
        T.wait(tl[4])
        t_dSp = []
        for r in range(ND):
            t_d = T.op(lambda e, r=r, b=b: e.matmul(
                pS[:, r, :], kt_c[:, b, 128 * r:128 * (r + 1)], U_sb[:],
                start=True, stop=True), inc=(1 if r % 2 == 1 else 0))
            if r % 2 == 1:
                t_dSp.append(t_d)
        for j in range(4):
            V.wait(t_dSp[j])
            if ci == 0:
                V.wait(t_z1)
            t_up = V.op(lambda e, j=j: e.scalar_tensor_tensor(
                S_f[:, 2 * j:2 * j + 2, :], pS[:, 2 * j:2 * j + 2, :], 1.0,
                S_f[:, 2 * j:2 * j + 2, :], ALU.mult, ALU.add))
            S.wait(t_up)
            S.wait(t_O1)
            cp_tok[j] = S.op(lambda e, j=j: e.copy(
                S_b[:, 2 * j:2 * j + 2, :], S_f[:, 2 * j:2 * j + 2, :]))
        S.wait(t_O2)
        if odma_tok[b] is not None:
            S.wait(odma_tok[b])
        t_oev = S.op(lambda e, b=b: e.copy(o_t[:, b, :], pO[:]))
        G.wait(t_oev)
        odma_tok[b] = G.dma(io["o"][128 * ci:128 * (ci + 1), :], o_t[:, b, :])
        cons_tok[b] = [(T.sem, T.n), (V.sem, V.n)]


NE = 8
NF0 = 32
TH = 1024


def build_C():
    nc = bass.Bass("TRN2", target_bir_lowering=False, debug=False)
    io = {}
    io["xn2fm"] = nc.dram_tensor("xn2fm", [D, TPC], BF16, kind="ExternalInput").ap()
    io["w"] = nc.dram_tensor("w", [TPC, NE], F32, kind="ExternalInput").ap()
    io["W1"] = nc.dram_tensor("W1", [NE, D, 4096], F32, kind="ExternalInput").ap()
    io["b1"] = nc.dram_tensor("b1", [NE, 4096], F32, kind="ExternalInput").ap()
    io["W2"] = nc.dram_tensor("W2", [NE, 4096, D], F32, kind="ExternalInput").ap()
    io["b2"] = nc.dram_tensor("b2", [NE, D], F32, kind="ExternalInput").ap()
    io["moe0"] = nc.dram_tensor("moe0", [TPC, D], F32, kind="ExternalOutput").ap()
    with ExitStack() as stack:
        P = Prog(nc, stack)
        emit_C(P, io)
        P.finish()
    return nc


def emit_C(P, io):
    Y, G, T, V, S = P.Y, P.G, P.T, P.V, P.S
    NFC = 32
    xn2 = P.sb("c_xn2", [128, ND, TPC], BF16)
    Hbuf = P.sb("c_H", [128, NFC, TH], BF16)
    W1b = [P.sb(f"c_W1{i}", [128, ND, 512], BF16) for i in range(2)]
    W2b = [P.sb(f"c_W2{i}", [128, 16, 512], BF16) for i in range(2)]
    Facc = P.sb("c_Facc", [128, ND, D], F32)
    ones8 = P.sb("c_o8", [8, 1], F32)
    onesr = P.sb("c_or", [1, 8], F32)
    ident_f = P.sb("c_idf", [128, 128], F32)
    w_all = P.sb("c_w", [128, NT, NE], F32)
    wTb = P.sb("c_wTb", [8, TPC], BF16)
    b1pad = P.sb("c_b1p", [128, 128], F32)
    b1col = P.sb("c_b1c", [128, NE, NFC], F32)
    rb1 = P.sb("c_rb1", [128, NE, NFC], BF16)
    b2rows = P.sb("c_b2r", [8, D], F32)
    CC8 = P.sb("c_CC8", [8, D], F32)
    CC8b = P.sb("c_CC8b", [8, D], F32)
    ctcrow = P.sb("c_ctc", [1, D], F32)
    ccstg = P.sb("c_ccstg", [1, D], F32)
    CCbf = P.sb("c_CCbf", [8, D], BF16)

    pH = [P.ps(f"c_pH{i}", [128, 512]) for i in range(4)]
    pF = [P.ps(f"c_pF{i}", [128, 512]) for i in range(4)]

    tz = G.op(lambda e: e.memset(ident_f[:], 0.0))
    G.wait(tz)
    G.op(lambda e: e.affine_select(out=ident_f[:], in_=ident_f[:],
                                   compare_op=ALU.not_equal, fill=1.0,
                                   base=0, pattern=[[-1, 128]], channel_multiplier=1))
    G.op(lambda e: e.memset(ones8[:], 1.0))
    G.op(lambda e: e.memset(onesr[:], 1.0))
    G.op(lambda e: e.memset(b1pad[:], 0.0))
    for r in range(ND):
        Y.dma(xn2[:, r, :], io["xn2fm"][128 * r:128 * (r + 1), :])
    Y.dma(w_all[:], io["w"].rearrange("(i p) e -> p i e", p=128))
    Y.dma(b2rows[:], io["b2"][:, :])
    P.barrier()

    t_pTe = None
    for i in range(NT):
        if t_pTe is not None:
            T.wait(t_pTe)
        t_tr = T.op(lambda e, i=i: e.transpose(pH[0][0:NE, 0:128], w_all[:, i, :],
                                               ident_f[:]))
        V.wait(t_tr)
        t_pTe = V.op(lambda e, i=i: e.tensor_copy(wTb[:, 128 * i:128 * (i + 1)],
                                                  pH[0][0:NE, 0:128]))
    for e8 in range(NE):
        t_l = Y.dma(b1pad[0:NFC, :], io["b1"][e8].rearrange("(c p) -> c p", p=128))
        T.wait(t_l)
        T.wait(t_pTe)
        t_tr = T.op(lambda e: e.transpose(pH[0][:, 0:128], b1pad[:], ident_f[:]))
        V.wait(t_tr)
        t1 = V.op(lambda e, e8=e8: e.tensor_copy(b1col[:, e8, :], pH[0][:, 0:NFC]))
        V.wait(t1)
        t_pTe = V.op(lambda e, e8=e8: e.tensor_scalar(rb1[:, e8, :], b1col[:, e8, :],
                                                      0.0, None, ALU.max))
        Y.wait(t_pTe)
    P.barrier()

    t_pHev = [None] * 4
    t_pFev = [None] * 4
    t_w1 = [None, None]
    t_w2 = [None, None]
    t_ccev = None
    t_ccG = None
    t_out = None
    for th in range(2):
        t_fz = None
        for e8 in range(NE):
            t_l2 = [None, None]

            def w2load(dh, fch, th=th, e8=e8):
                bw = fch
                if t_w2[bw] is not None:
                    G.wait(t_w2[bw])
                return G.dma(
                    W2b[bw][:],
                    io["W2"][e8][2048 * fch:2048 * (fch + 1),
                                 512 * dh:512 * (dh + 1)]
                    .rearrange("(fc p) d -> p fc d", p=128))
            if t_ccev is not None:
                T.wait(t_ccev)
            gidx = 0
            for quar in range(8):
                bq = quar % 2
                if t_w1[bq] is not None:
                    G.wait(t_w1[bq])
                t_l = G.dma(W1b[bq][:],
                            io["W1"][e8, :, 512 * quar:512 * (quar + 1)]
                            .rearrange("(r p) f -> p r f", p=128))
                if quar == 0:
                    t_l2[0] = w2load(0, 0)
                    t_l2[1] = w2load(0, 1)
                for fl in range(4):
                    f = 4 * quar + fl
                    for tt2 in range(2):
                        bh = gidx % 4
                        gidx += 1
                        if t_pHev[bh] is not None:
                            T.wait(t_pHev[bh])
                        T.wait(t_l)
                        for r in range(ND):
                            t_mm = T.op(lambda e, r=r, fl=fl, tt2=tt2, th=th, bq=bq,
                                        bh=bh: e.matmul(
                                pH[bh][:], W1b[bq][:, r, 128 * fl:128 * (fl + 1)],
                                xn2[:, r, 1024 * th + 512 * tt2:
                                    1024 * th + 512 * (tt2 + 1)],
                                start=(r == 0), stop=(r == ND - 1)),
                                inc=(1 if r == ND - 1 else 0))
                        S.wait(t_mm)
                        ev = S.op(lambda e, f=f, tt2=tt2, e8=e8, bh=bh: e.activation(
                            Hbuf[:, f, 512 * tt2:512 * (tt2 + 1)], pH[bh][:],
                            AF.Relu, bias=b1col[:, e8, f:f + 1]))
                        t_pHev[bh] = ev
                t_w1[bq] = (T.sem, T.n)
            if e8 == 0:
                if t_out is not None:
                    G.wait(t_out)
                t_fz = G.op(lambda e: e.memset(Facc[:], 0.0))
            S_H_done = (S.sem, S.n)
            fidx = 0
            for dh in range(2):
                for fch in range(2):
                    bw = fch
                    T.wait(t_l2[bw])
                    T.wait(S_H_done)
                    for tc in range(ND):
                        bb = fidx % 4
                        fidx += 1
                        if t_pFev[bb] is not None:
                            T.wait(t_pFev[bb])
                        for fc in range(16):
                            t_mm = T.op(lambda e, fc=fc, fch=fch, tc=tc, dh=dh, bw=bw,
                                        th=th, bb=bb: e.matmul(
                                pF[bb][:],
                                Hbuf[:, 16 * fch + fc, 128 * tc:128 * (tc + 1)],
                                W2b[bw][:, fc, :],
                                start=(fc == 0), stop=(fc == 15)),
                                inc=(1 if fc == 15 else 0))
                        V.wait(t_mm)
                        V.wait(t_fz)
                        tv = V.op(lambda e, tc=tc, dh=dh, th=th, e8=e8, bb=bb:
                                  e.scalar_tensor_tensor(
                            Facc[:, tc, 512 * dh:512 * (dh + 1)], pF[bb][:],
                            w_all[:, 8 * th + tc, e8:e8 + 1],
                            Facc[:, tc, 512 * dh:512 * (dh + 1)],
                            ALU.mult, ALU.add))
                        t_pFev[bb] = tv
                    if th == 0:
                        if fch == 0:
                            T.wait(t_pHev[0])
                            if t_ccev is not None:
                                T.wait(t_ccev)
                        for fc in range(16):
                            t_ccmm = T.op(lambda e, fc=fc, fch=fch, e8=e8, bw=bw:
                                          e.matmul(
                                pH[0][0:1, :], rb1[:, e8, 16 * fch + fc:
                                                   16 * fch + fc + 1],
                                W2b[bw][:, fc, :],
                                start=(fch == 0 and fc == 0),
                                stop=(fch == 1 and fc == 15)),
                                inc=(1 if (fch == 1 and fc == 15) else 0))
                    t_w2[bw] = (T.sem, T.n)
                    if dh == 0:
                        t_l2[fch] = w2load(1, fch)
                if th == 0:
                    S.wait(t_ccmm)
                    if t_ccG is not None:
                        S.wait(t_ccG)
                    t_ccev = S.op(lambda e, dh=dh: e.copy(
                        ccstg[0:1, 512 * dh:512 * (dh + 1)], pH[0][0:1, :]))
                    G.wait(t_ccev)
                    t_ccG = G.dma(CC8[e8:e8 + 1, 512 * dh:512 * (dh + 1)],
                                  ccstg[0:1, 512 * dh:512 * (dh + 1)])
        if th == 0:
            V.wait(t_ccG)
            T.wait(t_pHev[1])
            T.wait(t_pHev[2])
            tv = V.op(lambda e: e.tensor_tensor(CC8b[:], CC8[:], b2rows[:], ALU.add))
            for dhh in range(2):
                T.wait(tv)
                t_tr = T.op(lambda e, dhh=dhh: e.matmul(
                    pH[1][0:1, :], ones8[:], CC8b[:, 512 * dhh:512 * (dhh + 1)],
                    start=True, stop=True))
                S.wait(t_tr)
                t_ct = S.op(lambda e, dhh=dhh: e.copy(
                    ctcrow[:, 512 * dhh:512 * (dhh + 1)], pH[1][0:1, :]))
                T.wait(t_ct)
                t_bc = T.op(lambda e, dhh=dhh: e.matmul(
                    pH[2][0:NE, :], onesr[:], ctcrow[:, 512 * dhh:512 * (dhh + 1)],
                    start=True, stop=True))
                V.wait(t_bc)
                tv2 = V.op(lambda e, dhh=dhh: e.tensor_tensor(
                    CCbf[:, 512 * dhh:512 * (dhh + 1)], pH[2][0:NE, :],
                    CC8[:, 512 * dhh:512 * (dhh + 1)], ALU.subtract))
                T.wait(tv2)
        for tc in range(ND):
            for dh in range(2):
                bb = (tc * 2 + dh) % 4
                if t_pFev[bb] is not None:
                    T.wait(t_pFev[bb])
                t_mm = T.op(lambda e, tc=tc, dh=dh, th=th: e.matmul(
                    pF[(tc * 2 + dh) % 4][:],
                    wTb[:, 1024 * th + 128 * tc:1024 * th + 128 * (tc + 1)],
                    CCbf[:, 512 * dh:512 * (dh + 1)], start=True, stop=True))
                V.wait(t_mm)
                tv = V.op(lambda e, tc=tc, dh=dh, bb=bb: e.scalar_tensor_tensor(
                    Facc[:, tc, 512 * dh:512 * (dh + 1)], pF[bb][:], 1.0,
                    Facc[:, tc, 512 * dh:512 * (dh + 1)], ALU.mult, ALU.add))
                t_pFev[bb] = tv
            G.wait(tv)
            t_out = G.dma(io["moe0"][1024 * th + 128 * tc:1024 * th + 128 * (tc + 1), :],
                          Facc[:, tc, :])
            V.wait(t_out)
        if th == 1:
            P.barrier()


def build_D():
    nc = bass.Bass("TRN2", target_bir_lowering=False, debug=False)
    io = {}
    for nm, shape in [("x", [TPC, D]), ("stats", [TPC, 2]), ("o", [TPC, D]),
                      ("moe0", [TPC, D]), ("lng", [D]), ("lnb", [D]),
                      ("Wq1", [128, D]), ("bq1", [128]), ("Wk1", [128, D]),
                      ("bk1", [128]), ("Wv1", [128, D]), ("bv1", [128]),
                      ("Wg", [D, 8])]:
        io[nm] = nc.dram_tensor(nm, shape, F32, kind="ExternalInput").ap()
    for nm, shape, dt in [("x1", [TPC, D], F32), ("xn1fm", [D, TPC], BF16),
                          ("Qfm", [128, TPC], BF16), ("Kfm", [128, TPC], BF16),
                          ("Vtm", [TPC, 128], BF16), ("qn", [1, TPC], F32),
                          ("kn", [1, TPC], F32), ("w1", [TPC, 8], F32)]:
        io[nm] = nc.dram_tensor(nm, shape, dt, kind="ExternalOutput").ap()
    with ExitStack() as stack:
        P = Prog(nc, stack)
        emit_D(P, io)
        P.finish()
    return nc


def emit_D(P, io):
    Y, G, T, V, S = P.Y, P.G, P.T, P.V, P.S
    o_t = P.sb("d_o", [128, 2, D], F32)
    x_t = P.sb("d_x", [128, 2, D], F32)
    m_t = P.sb("d_m", [128, 2, D], F32)
    sqs = P.sb("d_sqs", [128, D], F32)
    stats = P.sb("d_stats", [128, NT, 2], F32)
    rstd0 = P.sb("d_rstd0", [128, NT], F32)
    mu0 = P.sb("d_mu0", [128, NT], F32)
    so = P.sb("d_so", [128, 1], F32)
    sq_c = P.sb("d_sqc", [128, 1], F32)
    muo = P.sb("d_muo", [128, 1], F32)
    varo = P.sb("d_varo", [128, 1], F32)
    rstdo = P.sb("d_rstdo", [128, 1], F32)
    tmpc = P.sb("d_tmpc", [128, 1], F32)
    s0f = P.sb("d_s0f", [128, D], F32)
    s0g = P.sb("d_s0g", [128, D], F32)
    xn0f = P.sb("d_xn0f", [128, D], F32)
    g0f = P.sb("d_g0f", [128, D], F32)
    x1_t = P.sb("d_x1", [128, D], F32)
    xn1_tm = P.sb("d_xn1tm", [128, D], BF16)
    xn1f = P.sb("d_xn1f", [128, D], F32)
    xn1_fm = P.sb("d_xn1fm", [128, ND, TPC], BF16)
    x2f_st = P.sb("d_x2fst", [128, ND, 128], F32)
    fm_st = P.sb("d_fmst", [128, ND, 128], BF16)
    gbc = P.sb("d_gbc", [128, D], F32)
    bbc = P.sb("d_bbc", [128, D], F32)
    grow = P.sb("d_grow", [1, D], F32)
    brow = P.sb("d_brow", [1, D], F32)
    Wpad = P.sb("d_Wpad", [128, D], BF16)
    WqT = P.sb("d_WqT", [128, ND, 128], BF16)
    WkT = P.sb("d_WkT", [128, ND, 128], BF16)
    WvT = P.sb("d_WvT", [128, ND, 128], BF16)
    bq_r = P.sb("d_bqr", [1, 128], F32)
    bk_r = P.sb("d_bkr", [1, 128], F32)
    bv_r = P.sb("d_bvr", [1, 128], F32)
    Wgf = P.sb("d_Wgf", [128, ND, 8], F32)
    ident = P.sb("d_id", [128, 128], BF16)
    ident_f = P.sb("d_idf", [128, 128], F32)
    ones_f1 = P.sb("d_o1", [1, 512], F32)
    ones_bf = P.sb("d_obf", [128, 1], BF16)
    eps_c = P.sb("d_eps", [128, 1], F32)
    q_st = P.sb("d_qst", [128, 512], BF16)
    qsq = P.sb("d_qsq", [128, 512], BF16)
    qn_sb = P.sb("d_qn", [1, TPC], F32)
    kn_sb = P.sb("d_kn", [1, TPC], F32)
    v_st = P.sb("d_vst", [128, 128], BF16)
    probs = P.sb("d_probs", [128, 8], F32)
    Px = P.sb("d_Px", [128, 8], F32)
    msk = P.sb("d_msk", [128, 8], F32)
    m1 = P.sb("d_m1", [128, 1], F32)
    m2 = P.sb("d_m2", [128, 1], F32)
    nm_ = P.sb("d_nm", [128, 1], F32)
    rs_ = P.sb("d_rs", [128, 1], F32)
    w_t = P.sb("d_wt", [128, 8], F32)

    ppA = P.ps("d_ppA", [128, 512])
    ppB = P.ps("d_ppB", [128, 512])
    ptA = P.ps("d_ptA", [128, 128], BF16)
    ptB = P.ps("d_ptB", [128, 128], BF16)
    pkA = P.ps("d_pkA", [128, 128])
    pkB = P.ps("d_pkB", [128, 128])
    prow = P.ps("d_prow", [1, 512])

    make_ident(P, ident[:])
    tz = G.op(lambda e: e.memset(ident_f[:], 0.0))
    G.wait(tz)
    G.op(lambda e: e.affine_select(out=ident_f[:], in_=ident_f[:],
                                   compare_op=ALU.not_equal, fill=1.0,
                                   base=0, pattern=[[-1, 128]], channel_multiplier=1))
    G.op(lambda e: e.memset(ones_f1[:], 1.0))
    G.op(lambda e: e.memset(ones_bf[:], 1.0))
    G.op(lambda e: e.memset(eps_c[:], LN_EPS))
    Y.dma(grow[:], io["lng"][None, :])
    Y.dma(brow[:], io["lnb"][None, :])
    Y.dma(stats[:], io["stats"].rearrange("(i p) s -> p i s", p=128))
    Y.dma(bq_r[:], io["bq1"][None, :])
    Y.dma(bk_r[:], io["bk1"][None, :])
    Y.dma(bv_r[:], io["bv1"][None, :])
    for r in range(ND):
        Y.dma(Wgf[:, r, :], io["Wg"][128 * r:128 * (r + 1), :])
    P.barrier()
    for h in range(2):
        T.op(lambda e, h=h: e.matmul(ppA[:], ones_f1[0:1, 0:128],
                                     grow[:, 512 * h:512 * (h + 1)], start=True, stop=True))
        T.op(lambda e, h=h: e.matmul(ppB[:], ones_f1[0:1, 0:128],
                                     brow[:, 512 * h:512 * (h + 1)], start=True, stop=True))
        P.barrier([T, S])
        S.op(lambda e, h=h: e.copy(gbc[:, 512 * h:512 * (h + 1)], ppA[:]))
        S.op(lambda e, h=h: e.copy(bbc[:, 512 * h:512 * (h + 1)], ppB[:]))
        P.barrier([T, S])
    tv = V.op(lambda e: e.tensor_copy(mu0[:], stats[:, :, 0]))
    V.wait(tv)
    tv = V.op(lambda e: e.tensor_scalar(rstd0[:], stats[:, :, 1], LN_EPS, None, ALU.add))
    V.wait(tv)
    S.wait(tv)
    ts2 = S.op(lambda e: e.activation(rstd0[:], rstd0[:], AF.Sqrt))
    V.wait(ts2)
    tv = V.op(lambda e: e.reciprocal(rstd0[:], rstd0[:]))
    P.barrier()
    for (wname, dst) in [("Wq1", WqT), ("Wk1", WkT), ("Wv1", WvT)]:
        t_l = G.dma(Wpad[:], io[wname])
        T.wait(t_l)
        for r in range(ND):
            b = r % 2
            pt = ptA if b == 0 else ptB
            tmm = T.op(lambda e, pt=pt, r=r: e.transpose(
                pt[:], Wpad[:, 128 * r:128 * (r + 1)], ident[:]))
            V.wait(tmm)
            ev = V.op(lambda e, pt=pt, dst=dst, r=r: e.tensor_copy(dst[:, r, :], pt[:]))
            T.wait(ev)
        G.wait(ev)
    P.barrier()

    trk = {"b": {"i": 0, "ev": [None, None]}, "f": {"i": 0, "ev": [None, None]}}

    def transpose_to(dst_ap, src_ap, src_tok, idb):
        tr = trk[idb]
        b = tr["i"] % 2
        tr["i"] += 1
        pt = (ptA if b == 0 else ptB) if idb == "b" else (pkA if b == 0 else pkB)
        T.wait(src_tok)
        if tr["ev"][b] is not None:
            T.wait(tr["ev"][b])
        idt = ident if idb == "b" else ident_f
        tmm = T.op(lambda e, pt=pt, s=src_ap, idt=idt: e.transpose(pt[:], s, idt[:]))
        if b == 0:
            V.wait(tmm)
            ev = V.op(lambda e, pt=pt, d=dst_ap: e.tensor_copy(d, pt[:]))
        else:
            S.wait(tmm)
            ev = S.op(lambda e, pt=pt, d=dst_ap: e.copy(d, pt[:]))
        tr["ev"][b] = ev
        return ev

    t_buf = [None, None]
    for i in range(NT):
        b = i % 2
        if t_buf[b] is not None:
            Y.wait(t_buf[b])
        if "orecv" in io:
            si_ = 0 if i < 8 else 1
            w_ = i % 8
            for r2 in range(4):
                t_lo = Y.dma(o_t[:, b, 256 * r2:256 * (r2 + 1)],
                             io["orecv"][1024 * (4 * si_ + r2) + 128 * w_:
                                         1024 * (4 * si_ + r2) + 128 * (w_ + 1), :])
        else:
            t_lo = Y.dma(o_t[:, b, :], io["o"][128 * i:128 * (i + 1), :])
        t_lx = Y.dma(x_t[:, b, :], io["x"][128 * i:128 * (i + 1), :])
        t_lm = Y.dma(m_t[:, b, :], io["moe0"][128 * i:128 * (i + 1), :])

        def vch(fn):
            t = V.op(fn)
            V.wait(t)
            return t
        V.wait(t_lo)
        vch(lambda e, b=b: e.reduce_sum(so[:], o_t[:, b, :], axis=AX.X))
        S.wait(t_lo)
        if i > 0:
            S.wait(t_sq)
        t_sq = S.op(lambda e, b=b: e.activation(sqs[:], o_t[:, b, :], AF.Square,
                                                accum_out=sq_c[:]))
        V.wait(t_sq)
        vch(lambda e: e.tensor_scalar(muo[:], so[:], 1.0 / D, None, ALU.mult))
        vch(lambda e: e.tensor_scalar(varo[:], sq_c[:], 1.0 / D, None, ALU.mult))
        vch(lambda e: e.tensor_tensor(tmpc[:], muo[:], muo[:], ALU.mult))
        t = vch(lambda e: e.tensor_tensor(varo[:], varo[:], tmpc[:], ALU.subtract))
        S.wait(t)
        t = S.op(lambda e: e.activation(tmpc[:], varo[:], AF.Sqrt, bias=eps_c[:, 0:1]))
        V.wait(t)
        vch(lambda e: e.reciprocal(rstdo[:], tmpc[:]))
        vch(lambda e, b=b: e.tensor_scalar(s0f[:], o_t[:, b, :], muo[:, 0:1],
                                           rstdo[:, 0:1], ALU.subtract, ALU.mult))
        vch(lambda e: e.tensor_tensor(s0g[:], s0f[:], gbc[:], ALU.mult))
        t_s0 = vch(lambda e: e.tensor_tensor(s0g[:], s0g[:], bbc[:], ALU.add))
        V.wait(t_lx)
        vch(lambda e, b=b, i=i: e.tensor_scalar(xn0f[:], x_t[:, b, :], mu0[:, i:i + 1],
                                                rstd0[:, i:i + 1], ALU.subtract, ALU.mult))
        V.wait(t_lm)
        t = vch(lambda e, b=b: e.tensor_tensor(xn0f[:], xn0f[:], m_t[:, b, :], ALU.add))
        S.wait(t)
        t_g0 = S.op(lambda e: e.activation(g0f[:], xn0f[:], AF.Silu))
        V.wait(t_g0)
        vch(lambda e: e.tensor_tensor(s0g[:], s0g[:], g0f[:], ALU.mult))
        t_x1 = vch(lambda e, b=b: e.tensor_tensor(x1_t[:], x_t[:, b, :], s0g[:], ALU.add))
        G.wait(t_x1)
        G.dma(io["x1"][128 * i:128 * (i + 1), :], x1_t[:])
        vch(lambda e: e.reduce_sum(so[:], x1_t[:], axis=AX.X))
        S.wait(t_x1)
        S.wait(t_g0)
        t_sq = S.op(lambda e: e.activation(sqs[:], x1_t[:], AF.Square, accum_out=sq_c[:]))
        V.wait(t_sq)
        vch(lambda e: e.tensor_scalar(muo[:], so[:], 1.0 / D, None, ALU.mult))
        vch(lambda e: e.tensor_scalar(varo[:], sq_c[:], 1.0 / D, None, ALU.mult))
        vch(lambda e: e.tensor_tensor(tmpc[:], muo[:], muo[:], ALU.mult))
        t = vch(lambda e: e.tensor_tensor(varo[:], varo[:], tmpc[:], ALU.subtract))
        S.wait(t)
        t = S.op(lambda e: e.activation(tmpc[:], varo[:], AF.Sqrt, bias=eps_c[:, 0:1]))
        V.wait(t)
        vch(lambda e: e.reciprocal(rstdo[:], tmpc[:]))
        t_n1 = vch(lambda e: e.tensor_scalar(xn1_tm[:], x1_t[:], muo[:, 0:1],
                                             rstdo[:, 0:1], ALU.subtract, ALU.mult))
        t_n1f = vch(lambda e: e.tensor_scalar(xn1f[:], x1_t[:], muo[:, 0:1],
                                              rstdo[:, 0:1], ALU.subtract, ALU.mult))
        t_buf[b] = t_n1f
        for r in range(ND):
            ev = transpose_to(xn1_fm[:, r, 128 * i:128 * (i + 1)],
                              xn1_tm[:, 128 * r:128 * (r + 1)], t_n1, "b")
        evf = None
        for r in range(ND):
            evf = transpose_to(x2f_st[:, r, :], xn1f[:, 128 * r:128 * (r + 1)], t_n1f, "f")
        T.wait(evf)
        for r in range(ND):
            T.op(lambda e, r=r: e.matmul(ppA[:, 0:8], x2f_st[:, r, :], Wgf[:, r, :],
                                         start=(r == 0), stop=(r == ND - 1)),
                 inc=(1 if r == ND - 1 else 0))
        t_mm8 = (T.sem, T.n)
        V.wait(t_mm8)
        vch(lambda e: e.reduce_max(m1[:], ppA[:, 0:8], axis=AX.X))
        vch(lambda e: e.tensor_scalar(nm_[:], m1[:], -1.0, None, ALU.mult))
        tpx = vch(lambda e: e.tensor_copy(Px[:], ppA[:, 0:8]))
        S.wait(tpx)
        tpx = S.op(lambda e: e.activation(Px[:], Px[:], AF.Exp, bias=nm_[:, 0:1]))
        V.wait(tpx)
        vch(lambda e: e.reduce_sum(m2[:], Px[:], axis=AX.X))
        vch(lambda e: e.reciprocal(rs_[:], m2[:]))
        vch(lambda e: e.tensor_scalar(probs[:], Px[:], rs_[:, 0:1], None, ALU.mult))
        vch(lambda e: e.reduce_max(m1[:], probs[:], axis=AX.X))
        vch(lambda e: e.tensor_scalar(msk[:], probs[:], m1[:, 0:1], None, ALU.is_equal))
        vch(lambda e: e.scalar_tensor_tensor(Px[:], msk[:], -1e30, probs[:],
                                             ALU.mult, ALU.add))
        vch(lambda e: e.reduce_max(m2[:], Px[:], axis=AX.X))
        vch(lambda e: e.tensor_scalar(msk[:], probs[:], m2[:, 0:1], None, ALU.is_ge))
        vch(lambda e: e.tensor_tensor(w_t[:], probs[:], msk[:], ALU.mult))
        vch(lambda e: e.tensor_tensor(tmpc[:], m1[:], m2[:], ALU.add))
        vch(lambda e: e.reciprocal(tmpc[:], tmpc[:]))
        t_w = vch(lambda e: e.tensor_scalar(w_t[:], w_t[:], tmpc[:, 0:1], None, ALU.mult))
        G.wait(t_w)
        t_wd = G.dma(io["w1"][128 * i:128 * (i + 1), :], w_t[:])
        V.wait(t_wd)
        T.wait((V.sem, V.n))
        T.wait(t_n1)
        for r in range(ND):
            T.op(lambda e, r=r, i=i: e.matmul(
                pkA[:], xn1_fm[:, r, 128 * i:128 * (i + 1)], WvT[:, r, :],
                start=(r == 0), stop=False), inc=0)
        t_vb = T.op(lambda e: e.matmul(pkA[:], ones_f1[0:1, 0:128], bv_r[:],
                                       start=False, stop=True))
        S.wait(t_vb)
        t_ve = S.op(lambda e: e.copy(v_st[:], pkA[:]))
        G.wait(t_ve)
        t_vd = G.dma(io["Vtm"][128 * i:128 * (i + 1), :], v_st[:])
        S.wait(t_vd)
        T.wait(t_ve)
    P.barrier()
    for r in range(ND):
        G.dma(io["xn1fm"][128 * r:128 * (r + 1), :], xn1_fm[:, r, :])
    t_qd = None
    for (WT, b_r, oname, nrow) in [(WqT, bq_r, "Qfm", qn_sb), (WkT, bk_r, "Kfm", kn_sb)]:
        for g in range(4):
            for r in range(ND):
                T.op(lambda e, r=r, g=g, WT=WT: e.matmul(
                    ppA[:], WT[:, r, :], xn1_fm[:, r, 512 * g:512 * (g + 1)],
                    start=(r == 0), stop=False), inc=0)
            t_qb = T.op(lambda e, b_r=b_r: e.matmul(
                ppA[:], b_r[:, :], ones_f1[0:1, :], start=False, stop=True))
            S.wait(t_qb)
            if t_qd is not None:
                S.wait(t_qd)
            t_qe = S.op(lambda e: e.copy(q_st[:], ppA[:]))
            G.wait(t_qe)
            t_qd = G.dma(io[oname][:, 512 * g:512 * (g + 1)], q_st[:])
            S.wait(t_qe)
            t_qs = S.op(lambda e: e.activation(qsq[:], q_st[:], AF.Square))
            T.wait(t_qs)
            t_nr = T.op(lambda e: e.matmul(prow[:], ones_bf[:], qsq[:], start=True, stop=True))
            S.wait(t_nr)
            t_sr = S.op(lambda e, nrow=nrow, g=g: e.activation(
                nrow[:, 512 * g:512 * (g + 1)], prow[:], AF.Sqrt))
            T.wait(t_sr)
    P.barrier([S, G])
    G.dma(io["qn"], qn_sb[:])
    G.dma(io["kn"], kn_sb[:])
    if "kmax" in io:
        kml = P.sb("d_kml", [1, 2], F32)
        P.barrier([V, G])
        for bi in range(2):
            tv = V.op(lambda e, bi=bi: e.reduce_max(
                kml[0:1, bi:bi + 1], kn_sb[0:1, 1024 * bi:1024 * (bi + 1)], axis=AX.X))
            V.wait(tv)
        G.wait(tv)
        G.dma(io["kmax"], kml[:])


NSLOT = 64
S128 = float(128) ** -0.5


def build_E():
    nc = bass.Bass("TRN2", target_bir_lowering=False, debug=False)
    io = {}
    io["Qfm"] = nc.dram_tensor("Qfm", [128, TPC], BF16, kind="ExternalInput").ap()
    io["Kpk"] = nc.dram_tensor("Kpk", [128, 2 * NSLOT * 128], BF16, kind="ExternalInput").ap()
    io["Vpk"] = nc.dram_tensor("Vpk", [2 * NSLOT * 128, 128], BF16, kind="ExternalInput").ap()
    io["kn01"] = nc.dram_tensor("kn01", [128, 2 * NSLOT], BF16, kind="ExternalInput").ap()
    io["knpk"] = nc.dram_tensor("knpk", [1, 2 * NSLOT * 128], F32, kind="ExternalInput").ap()
    io["qn"] = nc.dram_tensor("qn", [1, TPC], F32, kind="ExternalInput").ap()
    io["Wo"] = nc.dram_tensor("Wo", [D, 128], F32, kind="ExternalInput").ap()
    io["bo"] = nc.dram_tensor("bo", [D], F32, kind="ExternalInput").ap()
    io["s1"] = nc.dram_tensor("s1", [TPC, D], F32, kind="ExternalOutput").ap()
    with ExitStack() as stack:
        P = Prog(nc, stack)
        emit_E(P, io)
        P.finish()
    return nc


def emit_E(P, io):
    Y, G, T, V, S = P.Y, P.G, P.T, P.V, P.S
    Qf = P.sb("e_Q", [128, TPC], BF16)
    Kf = P.sb("e_K", [128, 2 * NSLOT * 128], BF16)
    Vt = P.sb("e_V", [128, 2 * NSLOT, 128], BF16)
    kn01 = P.sb("e_kn01", [128, 2 * NSLOT], BF16)
    knr = P.sb("e_knr", [1, 2 * NSLOT * 128], F32)
    qnr = P.sb("e_qnr", [1, TPC], F32)
    kmax = P.sb("e_kmax", [1, 1], F32)
    cbrow = P.sb("e_cbrow", [1, 512], F32)
    cb = P.sb("e_cb", [128, 512], F32)
    tsub = P.sb("e_tsub", [128, 512], F32)
    Pt = [P.sb(f"e_Pt{i}", [128, 512], BF16) for i in range(2)]
    srow = P.sb("e_srow", [1, 512], F32)
    rrow = P.sb("e_rrow", [1, 512], F32)
    rb = P.sb("e_rb", [128, 512], F32)
    Ofm = P.sb("e_Ofm", [128, 512], BF16)
    Wpad = P.sb("e_Wpad", [128, ND, 128], BF16)
    WoT = P.sb("e_WoT", [128, ND, 128], BF16)
    bo_r = P.sb("e_bor", [1, D], F32)
    s1_t = P.sb("e_s1t", [128, D], F32)
    ident = P.sb("e_id", [128, 128], BF16)
    ones_f1 = P.sb("e_o1", [1, 512], F32)
    masks = [P.sb(f"e_msk{d}", [128, 512], BF16) for d in range(4)]

    pStA = P.ps("e_pStA", [128, 512])
    pStB = P.ps("e_pStB", [128, 512])
    pSum = P.ps("e_pSum", [1, 512])
    pO = P.ps("e_pO", [128, 512])
    ptA = P.ps("e_ptA", [128, 128], BF16)

    make_ident(P, ident[:])
    G.op(lambda e: e.memset(ones_f1[:], 1.0))
    for d in range(4):
        tz = G.op(lambda e, d=d: e.memset(masks[d][:], 1.0))
        G.wait(tz)
        G.op(lambda e, d=d: e.affine_select(
            out=masks[d][:], in_=masks[d][:], compare_op=ALU.is_ge, fill=0.0,
            base=-128 * d, pattern=[[1, 512]], channel_multiplier=-1))
    Y.dma(Qf[:], io["Qfm"])
    Y.dma(Kf[:], io["Kpk"])
    Y.dma(Vt[:], io["Vpk"].rearrange("(j p) h -> p j h", p=128))
    Y.dma(kn01[:], io["kn01"])
    Y.dma(knr[:], io["knpk"])
    Y.dma(qnr[:], io["qn"])
    Y.dma(bo_r[:], io["bo"][None, :])
    t_wo = G.dma(Wpad[:], io["Wo"].rearrange("(r p) h -> p r h", p=128))
    P.barrier()
    for r in range(ND):
        tmm = T.op(lambda e, r=r: e.transpose(ptA[:], Wpad[:, r, :], ident[:]))
        V.wait(tmm)
        ev = V.op(lambda e, r=r: e.tensor_copy(WoT[:, r, :], ptA[:]))
        T.wait(ev)
    P.barrier()

    t_sub = [None, None]
    t_PtRd = [None, None]
    t_oev = None
    t_se = None
    for bi in range(2):
        tv = V.op(lambda e, bi=bi: e.reduce_max(
            kmax[:], knr[:, 8192 * bi:8192 * (bi + 1)], axis=AX.X))
        V.wait(tv)
        for qc in range(2):
            qlo = 1024 * bi + 512 * qc
            tv = V.op(lambda e, qlo=qlo: e.tensor_scalar(
                cbrow[:], qnr[:, qlo:qlo + 512], kmax[0:1, 0:1], None, ALU.mult))
            T.wait(tv)
            t_cb = T.op(lambda e: e.matmul(pStA[:], ones_f1[0:1, 0:128], cbrow[:],
                                           start=True, stop=True))
            S.wait(t_cb)
            t_cbe = S.op(lambda e: e.copy(cb[:], pStA[:]))
            V.wait(t_cbe)
            T.wait(t_cbe)
            slots = []
            for j in range(56):
                slots.append((j, None))
            if qc == 0:
                for d in range(4):
                    slots.append((56 + d, d))
            else:
                for d in range(4):
                    slots.append((56 + d, None))
                for d in range(4):
                    slots.append((60 + d, d))
            nsl = len(slots)
            for si, (j, mband) in enumerate(slots):
                jj = NSLOT * bi + j
                sb_ = si % 2
                pSt = pStA if sb_ == 0 else pStB
                if t_sub[sb_] is not None:
                    T.wait(t_sub[sb_])
                t_st = T.op(lambda e, jj=jj, qlo=qlo, pSt=pSt: e.matmul(
                    pSt[:], Kf[:, 128 * jj:128 * (jj + 1)], Qf[:, qlo:qlo + 512],
                    start=True, stop=True))
                V.wait(t_st)
                if t_se is not None:
                    V.wait(t_se)
                t_sub[sb_] = V.op(lambda e, pSt=pSt: e.tensor_tensor(
                    tsub[:], pSt[:], cb[:], ALU.subtract))
                pb = si % 2
                if t_PtRd[pb] is not None:
                    S.wait(t_PtRd[pb])
                S.wait(t_sub[sb_])
                t_se = S.op(lambda e, pb=pb: e.activation(
                    Pt[pb][:], tsub[:], AF.Exp, scale=S128))
                src_tok = t_se
                if mband is not None:
                    V.wait(t_se)
                    src_tok = V.op(lambda e, pb=pb, mband=mband: e.tensor_tensor(
                        Pt[pb][:], Pt[pb][:], masks[mband][:], ALU.mult))
                T.wait(src_tok)
                T.op(lambda e, jj=jj, pb=pb, si=si, nsl=nsl: e.matmul(
                    pSum[:], kn01[:, jj:jj + 1], Pt[pb][:],
                    start=(si == 0), stop=(si == nsl - 1)), inc=0)
                t_av = T.op(lambda e, jj=jj, pb=pb, si=si, nsl=nsl: e.matmul(
                    pO[:], Vt[:, jj, :], Pt[pb][:],
                    start=(si == 0), stop=(si == nsl - 1)))
                t_PtRd[pb] = t_av
            S.wait(t_av)
            if t_oev is not None:
                S.wait(t_oev)
            t_sre = S.op(lambda e: e.copy(srow[:], pSum[:]))
            V.wait(t_sre)
            tv = V.op(lambda e: e.reciprocal(rrow[:], srow[:]))
            T.wait(tv)
            T.wait(t_sub[0] if t_sub[0] else None)
            T.wait(t_sub[1] if t_sub[1] else None)
            t_rb = T.op(lambda e: e.matmul(pStA[:], ones_f1[0:1, 0:128], rrow[:],
                                           start=True, stop=True))
            S.wait(t_rb)
            t_rbe = S.op(lambda e: e.copy(rb[:], pStA[:]))
            V.wait(t_rbe)
            t_onrm = V.op(lambda e: e.tensor_tensor(Ofm[:], pO[:], rb[:], ALU.mult))
            t_sub[0] = t_onrm
            for tc in range(4):
                for dh in range(2):
                    pSt = pStA if dh == 0 else pStB
                    T.wait(t_onrm)
                    T.op(lambda e, tc=tc, dh=dh, pSt=pSt: e.matmul(
                        pSt[:], Ofm[:, 128 * tc:128 * (tc + 1)],
                        WoT[:, 4 * dh:4 * (dh + 1), :].rearrange("p a b -> p (a b)"),
                        start=True, stop=False), inc=0)
                    t_pr = T.op(lambda e, dh=dh, pSt=pSt: e.matmul(
                        pSt[:], ones_f1[0:1, 0:128], bo_r[:, 512 * dh:512 * (dh + 1)],
                        start=False, stop=True))
                    S.wait(t_pr)
                    t_pe = S.op(lambda e, dh=dh, pSt=pSt: e.copy(
                        s1_t[:, 512 * dh:512 * (dh + 1)], pSt[:]))
                G.wait(t_pe)
                t_oev = G.dma(io["s1"][qlo + 128 * tc:qlo + 128 * (tc + 1), :], s1_t[:])
                S.wait(t_oev)
                T.wait(t_pe)
            t_sub = [t_pe, t_pe]
            t_PtRd = [None, None]


def pack_attention(c, Kfm_seqs, Vtm_seqs, kn_seqs):
    Kpk = np.zeros((128, 2 * NSLOT * 128), dtype=BF)
    Vpk = np.zeros((2 * NSLOT * 128, 128), dtype=BF)
    kn01 = np.zeros((128, 2 * NSLOT), dtype=BF)
    knpk = np.zeros((1, 2 * NSLOT * 128), dtype=np.float32)
    for bi in range(2):
        s, qb = bi, c
        nreal_full = 8 * qb
        for j in range(nreal_full):
            Kpk[:, 128 * (NSLOT * bi + j):128 * (NSLOT * bi + j + 1)] = \
                Kfm_seqs[s][:, 128 * j:128 * (j + 1)]
            Vpk[128 * (NSLOT * bi + j):128 * (NSLOT * bi + j + 1)] = \
                Vtm_seqs[s][128 * j:128 * (j + 1)]
            kn01[:, NSLOT * bi + j] = 1
            knpk[0, 128 * (NSLOT * bi + j):128 * (NSLOT * bi + j + 1)] = \
                kn_seqs[s][128 * j:128 * (j + 1)]
        for d in range(8):
            j = 8 * qb + d
            sl = NSLOT * bi + 56 + d
            Kpk[:, 128 * sl:128 * (sl + 1)] = Kfm_seqs[s][:, 128 * j:128 * (j + 1)]
            Vpk[128 * sl:128 * (sl + 1)] = Vtm_seqs[s][128 * j:128 * (j + 1)]
            kn01[:, sl] = 1
            knpk[0, 128 * sl:128 * (sl + 1)] = kn_seqs[s][128 * j:128 * (j + 1)]
    return {"Kpk": Kpk, "Vpk": Vpk, "kn01": kn01, "knpk": knpk}


NF1 = 16


def build_F():
    nc = bass.Bass("TRN2", target_bir_lowering=False, debug=False)
    io = {}
    io["xn1fm"] = nc.dram_tensor("xn1fm", [D, TPC], BF16, kind="ExternalInput").ap()
    io["w1"] = nc.dram_tensor("w1", [TPC, NE], F32, kind="ExternalInput").ap()
    io["Wu"] = nc.dram_tensor("Wu", [NE, D, 2048], F32, kind="ExternalInput").ap()
    io["Wd"] = nc.dram_tensor("Wd", [NE, 2048, D], F32, kind="ExternalInput").ap()
    io["W1s"] = nc.dram_tensor("W1s", [D, 2048], F32, kind="ExternalInput").ap()
    io["W2s"] = nc.dram_tensor("W2s", [2048, D], F32, kind="ExternalInput").ap()
    io["x1"] = nc.dram_tensor("x1", [TPC, D], F32, kind="ExternalInput").ap()
    io["s1"] = nc.dram_tensor("s1", [TPC, D], F32, kind="ExternalInput").ap()
    io["out"] = nc.dram_tensor("out", [TPC, D], F32, kind="ExternalOutput").ap()
    with ExitStack() as stack:
        P = Prog(nc, stack)
        emit_F(P, io)
        P.finish()
    return nc


def emit_F(P, io):
    Y, G, T, V, S = P.Y, P.G, P.T, P.V, P.S
    NFC1 = 16
    xn1 = P.sb("f_xn1", [128, ND, TPC], BF16)
    Hbuf = P.sb("f_H", [128, NFC1, TH], BF16)
    W1b = [P.sb(f"f_W1{i}", [128, ND, 512], BF16) for i in range(2)]
    W2b = [P.sb(f"f_W2{i}", [128, 8, 512], BF16) for i in range(2)]
    Facc = P.sb("f_Facc", [128, ND, D], F32)
    w_all = P.sb("f_w", [128, NT, NE], F32)
    g1_t = P.sb("f_g1", [128, D], F32)
    x1_t = P.sb("f_x1", [128, 2, D], F32)
    s1_t = P.sb("f_s1", [128, 2, D], F32)
    out_t = P.sb("f_out", [128, 2, D], F32)

    pH = [P.ps(f"f_pH{i}", [128, 512]) for i in range(4)]
    pF = [P.ps(f"f_pF{i}", [128, 512]) for i in range(4)]

    for r in range(ND):
        Y.dma(xn1[:, r, :], io["xn1fm"][128 * r:128 * (r + 1), :])
    Y.dma(w_all[:], io["w1"].rearrange("(i p) e -> p i e", p=128))
    P.barrier()

    t_pHev = [None] * 4
    t_pFev = [None] * 4
    t_w1 = [None, None]
    t_w2 = [None, None]
    t_ob = [None, None]
    t_g1last = None
    for th in range(2):
        t_fz = None
        for e8 in range(NE + 1):
            shared = (e8 == NE)
            t_l2 = [None, None]

            def w2load(dh, fch, e8=e8, shared=shared):
                bw = fch
                if t_w2[bw] is not None:
                    G.wait(t_w2[bw])
                if shared:
                    src = io["W2s"][1024 * fch:1024 * (fch + 1),
                                    512 * dh:512 * (dh + 1)]
                else:
                    src = io["Wd"][e8][1024 * fch:1024 * (fch + 1),
                                       512 * dh:512 * (dh + 1)]
                return G.dma(W2b[bw][:], src.rearrange("(fc p) d -> p fc d", p=128))
            gidx = 0
            for quar in range(4):
                bq = quar % 2
                if t_w1[bq] is not None:
                    G.wait(t_w1[bq])
                if shared:
                    src = io["W1s"][:, 512 * quar:512 * (quar + 1)]
                else:
                    src = io["Wu"][e8, :, 512 * quar:512 * (quar + 1)]
                t_l = G.dma(W1b[bq][:], src.rearrange("(r p) f -> p r f", p=128))
                if quar == 0:
                    t_l2[0] = w2load(0, 0)
                    t_l2[1] = w2load(0, 1)
                for fl in range(4):
                    f = 4 * quar + fl
                    for tt2 in range(2):
                        bh = gidx % 4
                        gidx += 1
                        if t_pHev[bh] is not None:
                            T.wait(t_pHev[bh])
                        T.wait(t_l)
                        for r in range(ND):
                            t_mm = T.op(lambda e, r=r, fl=fl, tt2=tt2, th=th, bq=bq,
                                        bh=bh: e.matmul(
                                pH[bh][:], W1b[bq][:, r, 128 * fl:128 * (fl + 1)],
                                xn1[:, r, 1024 * th + 512 * tt2:
                                    1024 * th + 512 * (tt2 + 1)],
                                start=(r == 0), stop=(r == ND - 1)),
                                inc=(1 if r == ND - 1 else 0))
                        S.wait(t_mm)
                        ev = S.op(lambda e, f=f, tt2=tt2, bh=bh: e.activation(
                            Hbuf[:, f, 512 * tt2:512 * (tt2 + 1)], pH[bh][:],
                            AF.Gelu))
                        t_pHev[bh] = ev
                t_w1[bq] = (T.sem, T.n)
            if e8 == 0:
                if t_g1last is not None:
                    G.wait(t_g1last)
                t_fz = G.op(lambda e: e.memset(Facc[:], 0.0))
            S_H_done = (S.sem, S.n)
            fidx = 0
            for dh in range(2):
                for fch in range(2):
                    bw = fch
                    T.wait(t_l2[bw])
                    T.wait(S_H_done)
                    for tc in range(ND):
                        bb = fidx % 4
                        fidx += 1
                        if t_pFev[bb] is not None:
                            T.wait(t_pFev[bb])
                        for fc in range(8):
                            t_mm = T.op(lambda e, fc=fc, fch=fch, tc=tc, bw=bw,
                                        bb=bb: e.matmul(
                                pF[bb][:],
                                Hbuf[:, 8 * fch + fc, 128 * tc:128 * (tc + 1)],
                                W2b[bw][:, fc, :],
                                start=(fc == 0), stop=(fc == 7)),
                                inc=(1 if fc == 7 else 0))
                        V.wait(t_mm)
                        V.wait(t_fz)
                        if shared:
                            tv = V.op(lambda e, tc=tc, dh=dh, bb=bb:
                                      e.scalar_tensor_tensor(
                                Facc[:, tc, 512 * dh:512 * (dh + 1)], pF[bb][:],
                                1.0, Facc[:, tc, 512 * dh:512 * (dh + 1)],
                                ALU.mult, ALU.add))
                        else:
                            tv = V.op(lambda e, tc=tc, dh=dh, th=th, e8=e8, bb=bb:
                                      e.scalar_tensor_tensor(
                                Facc[:, tc, 512 * dh:512 * (dh + 1)], pF[bb][:],
                                w_all[:, 8 * th + tc, e8:e8 + 1],
                                Facc[:, tc, 512 * dh:512 * (dh + 1)],
                                ALU.mult, ALU.add))
                        t_pFev[bb] = tv
                    t_w2[bw] = (T.sem, T.n)
                    if dh == 0:
                        t_l2[fch] = w2load(1, fch)
        tok_comb = (V.sem, V.n)
        for tc in range(ND):
            b = tc % 2
            gtok = 1024 * th + 128 * tc
            if t_ob[b] is not None:
                Y.wait(t_ob[b])
            t_lx = Y.dma(x1_t[:, b, :], io["x1"][gtok:gtok + 128, :])
            t_ls = Y.dma(s1_t[:, b, :], io["s1"][gtok:gtok + 128, :])
            S.wait(tok_comb)
            t_g1 = S.op(lambda e, tc=tc: e.activation(g1_t[:], Facc[:, tc, :],
                                                      AF.Silu))
            V.wait(t_g1)
            V.wait(t_ls)
            tv = V.op(lambda e, b=b: e.tensor_tensor(g1_t[:], g1_t[:],
                                                     s1_t[:, b, :], ALU.mult))
            V.wait(tv)
            V.wait(t_lx)
            t_o = V.op(lambda e, b=b: e.tensor_tensor(out_t[:, b, :], x1_t[:, b, :],
                                                      g1_t[:], ALU.add))
            G.wait(t_o)
            t_od = G.dma(io["out"][gtok:gtok + 128, :], out_t[:, b, :])
            t_ob[b] = t_o
            t_g1last = t_g1
            S.wait(t_o)
            V.wait(t_od)
        if th == 1:
            P.barrier()


def kernel_unfused(**inputs):
    cc = np.ascontiguousarray
    x = cc(inputs["L0_attn_x"].astype(np.float32, copy=False))
    xs = shard_tokens(x)

    baseA = {"Wq": inputs["L0_attn_Wq"], "Wk": inputs["L0_attn_Wk"],
             "Wv": inputs["L0_attn_Wv"], "Wb": inputs["L0_attn_Wb"],
             "l0g": inputs["L0_ffn_gamma"], "l0b": inputs["L0_ffn_beta"],
             "Wr": inputs["L0_ffn_Wr"], "br": inputs["L0_ffn_br"]}
    baseA = {k: cc(v.astype(np.float32, copy=False)) for k, v in baseA.items()}
    ncA = build_A()
    resA = run_bass_kernel_spmd(
        ncA, [dict(baseA, x=cc(xs[c])) for c in range(NC)],
        core_ids=list(range(NC))).results

    def seq_fm(key):
        s0 = np.concatenate([resA[r][key][:, 0:1024] for r in range(NC)], axis=1)
        s1 = np.concatenate([resA[b][key][:, 1024:2048] for b in range(8)], axis=1)
        return [cc(s0), cc(s1)]

    def seq_tm(key):
        s0 = np.concatenate([resA[r][key][0:1024] for r in range(NC)], axis=0)
        s1 = np.concatenate([resA[b][key][1024:2048] for b in range(8)], axis=0)
        return [cc(s0), cc(s1)]

    def seq_ch(key):
        s0 = np.concatenate([resA[r][key][0:8] for r in range(NC)], axis=0)
        s1 = np.concatenate([resA[b][key][8:16] for b in range(8)], axis=0)
        return [cc(s0), cc(s1)]

    qfm_s = seq_fm("qfm"); kbar_s = seq_fm("kbar"); ktok_s = seq_tm("ktok")
    bv_s = seq_tm("bv"); beta_s = seq_tm("beta"); Tt_s = seq_ch("Tt"); KQu_s = seq_ch("KQu")
    ncB = build_B()
    inB = []
    for c in range(NC):
        s = 0 if c < 4 else 1
        j = c % 4
        inB.append({"qfm": qfm_s[s], "kbar": kbar_s[s], "ktok": ktok_s[s],
                    "bv": cc(bv_s[s][:, 256 * j:256 * (j + 1)]), "beta": beta_s[s],
                    "Tt": Tt_s[s], "KQu": KQu_s[s]})
    resB = run_bass_kernel_spmd(ncB, inB, core_ids=list(range(NC))).results

    baseC = {"W1": cc(inputs["L0_ffn_W1"].astype(np.float32, copy=False)),
             "b1": cc(inputs["L0_ffn_b1"].astype(np.float32, copy=False)),
             "W2": cc(inputs["L0_ffn_W2"].astype(np.float32, copy=False)),
             "b2": cc(inputs["L0_ffn_b2"].astype(np.float32, copy=False))}
    ncC = build_C()
    resC = run_bass_kernel_spmd(
        ncC, [dict(baseC, xn2fm=cc(resA[c]["xn2fm"]), w=cc(resA[c]["w"]))
              for c in range(NC)], core_ids=list(range(NC))).results

    o_seq = [np.concatenate([resB[4 * s + j]["o"] for j in range(4)], axis=1)
             for s in range(2)]
    o_sh = shard_tokens(np.stack(o_seq, axis=0))
    baseD = {"lng": inputs["L0_attn_ln_g"], "lnb": inputs["L0_attn_ln_b"],
             "Wq1": inputs["L1_attn_Wq"], "bq1": inputs["L1_attn_bq"],
             "Wk1": inputs["L1_attn_Wk"], "bk1": inputs["L1_attn_bk"],
             "Wv1": inputs["L1_attn_Wv"], "bv1": inputs["L1_attn_bv"],
             "Wg": inputs["L1_ffn_Wg"]}
    baseD = {k: cc(v.astype(np.float32, copy=False)) for k, v in baseD.items()}
    ncD = build_D()
    resD = run_bass_kernel_spmd(
        ncD, [dict(baseD, x=cc(xs[c]), stats=cc(resA[c]["stats"]),
                   o=cc(o_sh[c]), moe0=cc(resC[c]["moe0"])) for c in range(NC)],
        core_ids=list(range(NC))).results

    Kfm_seqs, Vtm_seqs, kn_seqs = [], [], []
    for s in range(2):
        if s == 0:
            Kfm = np.concatenate([resD[r]["Kfm"][:, 0:1024] for r in range(NC)], axis=1)
            Vtm = np.concatenate([resD[r]["Vtm"][0:1024] for r in range(NC)], axis=0)
            kn = np.concatenate([resD[r]["kn"][0, 0:1024] for r in range(NC)], axis=0)
        else:
            Kfm = np.concatenate([resD[b]["Kfm"][:, 1024:2048] for b in range(8)], axis=1)
            Vtm = np.concatenate([resD[b]["Vtm"][1024:2048] for b in range(8)], axis=0)
            kn = np.concatenate([resD[b]["kn"][0, 1024:2048] for b in range(8)], axis=0)
        Kfm_seqs.append(cc(Kfm)); Vtm_seqs.append(cc(Vtm)); kn_seqs.append(cc(kn))
    ncE = build_E()
    baseE = {"Wo": cc(inputs["L1_attn_Wo"].astype(np.float32, copy=False)),
             "bo": cc(inputs["L1_attn_bo"].astype(np.float32, copy=False))}
    inE = []
    for c in range(NC):
        pk = pack_attention(c, Kfm_seqs, Vtm_seqs, kn_seqs)
        inE.append(dict(baseE, Qfm=cc(resD[c]["Qfm"]), qn=cc(resD[c]["qn"]), **pk))
    resE = run_bass_kernel_spmd(ncE, inE, core_ids=list(range(NC))).results

    baseF = {"Wu": cc(inputs["L1_ffn_Wu"].astype(np.float32, copy=False)),
             "Wd": cc(inputs["L1_ffn_Wd"].astype(np.float32, copy=False)),
             "W1s": cc(inputs["L1_ffn_W1"].astype(np.float32, copy=False)),
             "W2s": cc(inputs["L1_ffn_W2"].astype(np.float32, copy=False))}
    ncF = build_F()
    resF = run_bass_kernel_spmd(
        ncF, [dict(baseF, xn1fm=cc(resD[c]["xn1fm"]), w1=cc(resD[c]["w1"]),
                   x1=cc(resD[c]["x1"]), s1=cc(resE[c]["s1"])) for c in range(NC)],
        core_ids=list(range(NC))).results

    out = unshard_tokens([resF[c]["out"] for c in range(NC)])
    return np.ascontiguousarray(out.astype(np.float32))


def emit_E2(P, io):
    Y, G, T, V, S = P.Y, P.G, P.T, P.V, P.S
    Qf = P.sb("e_Q", [128, TPC], BF16)
    Kg = P.sb("e_Kg", [128, ND, TPC], BF16)
    Vg = P.sb("e_Vg", [128, 128, 128], BF16)
    Kl = P.sb("e_Kl", [128, TPC], BF16)
    Vl = P.sb("e_Vl", [128, 16, 128], BF16)
    slotm = P.sb("e_slotm", [128, 128], F32)
    slotmb = P.sb("e_slotmb", [128, 128], BF16)
    kmr = P.sb("e_kmr", [1, 2], F32)
    qnr = P.sb("e_qnr", [1, TPC], F32)
    cbrow = P.sb("e_cbrow", [1, 512], F32)
    cb2 = P.sb("e_cb2", [128, 1024], F32)
    tsub2 = [P.sb(f"e_tsub{i}", [128, 1024], F32) for i in range(2)]
    Pt2 = [P.sb(f"e_Pt{i}", [128, 1024], BF16) for i in range(2)]
    srow = P.sb("e_srow", [1, 512], F32)
    rrow = P.sb("e_rrow", [1, 512], F32)
    rb = P.sb("e_rb", [128, 512], F32)
    Ofm = P.sb("e_Ofm", [128, 512], BF16)
    Wpad = P.sb("e_Wpad", [128, ND, 128], BF16)
    WoT = P.sb("e_WoT", [128, ND, 128], BF16)
    bo_r = P.sb("e_bor", [1, D], F32)
    s1_t = P.sb("e_s1t", [128, D], F32)
    ident = P.sb("e_id", [128, 128], BF16)
    ones_f1 = P.sb("e_o1", [1, 512], F32)
    ones_bf = P.sb("e_obf", [128, 1], BF16)
    masks = [P.sb(f"e_msk{d}", [128, 512], BF16) for d in range(4)]

    pSt2 = [P.ps(f"e_pSt2{i}", [128, 1024]) for i in range(2)]
    pSum = P.ps("e_pSum", [1, 512])
    pO = P.ps("e_pO", [128, 512])
    ptA = P.ps("e_ptA", [128, 128], BF16)

    make_ident(P, ident[:])
    G.op(lambda e: e.memset(ones_f1[:], 1.0))
    G.op(lambda e: e.memset(ones_bf[:], 1.0))
    for d in range(4):
        tz = G.op(lambda e, d=d: e.memset(masks[d][:], 1.0))
        G.wait(tz)
        G.op(lambda e, d=d: e.affine_select(
            out=masks[d][:], in_=masks[d][:], compare_op=ALU.is_ge, fill=0.0,
            base=-128 * d, pattern=[[1, 512]], channel_multiplier=-1))
    Y.dma(Qf[:], io["Qloc"])
    for r in range(ND):
        Y.dma(Kg[:, r, :], io["KG"][128 * r:128 * (r + 1), :])
    Y.dma(Vg[:], io["VG"].rearrange("(j p) h -> p j h", p=128))
    Y.dma(Kl[:], io["Kloc"])
    Y.dma(Vl[:], io["Vloc"].rearrange("(j p) h -> p j h", p=128))
    t_sm = Y.dma(slotm[:], io["slotmask"])
    V.wait(t_sm)
    V.op(lambda e: e.tensor_copy(slotmb[:], slotm[:]))
    Y.dma(kmr[:], io["kmR"])
    Y.dma(qnr[:], io["qnloc"])
    Y.dma(bo_r[:], io["bo"][None, :])
    G.dma(Wpad[:], io["Wo"].rearrange("(r p) h -> p r h", p=128))
    P.barrier()
    for r in range(ND):
        tmm = T.op(lambda e, r=r: e.transpose(ptA[:], Wpad[:, r, :], ident[:]))
        V.wait(tmm)
        ev = V.op(lambda e, r=r: e.tensor_copy(WoT[:, r, :], ptA[:]))
        T.wait(ev)
    for bi in range(2):
        for kc in range(64):
            jv = 16 * (kc // 8) + 8 * bi + (kc % 8)
            V.op(lambda e, jv=jv, bi=bi, kc=kc: e.tensor_scalar(
                Vg[:, jv, :], Vg[:, jv, :], slotm[:, 64 * bi + kc:64 * bi + kc + 1],
                None, ALU.mult))
    P.barrier()

    t_sub2 = [None, None]
    t_PtRd2 = [None, None]
    t_se2 = [None, None]
    t_oev = None
    for bi in range(2):
        for qc in range(2):
            qlo = 1024 * bi + 512 * qc
            tv = V.op(lambda e, qlo=qlo, bi=bi: e.tensor_scalar(
                cbrow[:], qnr[:, qlo:qlo + 512], kmr[0:1, bi:bi + 1], None, ALU.mult))
            T.wait(tv)
            if t_sub2[1] is not None:
                T.wait(t_sub2[1])
            t_cb = T.op(lambda e: e.matmul(pSt2[1][:, 0:512], ones_f1[0:1, 0:128],
                                           cbrow[:], start=True, stop=True))
            S.wait(t_cb)
            S.op(lambda e: e.copy(cb2[:, 0:512], pSt2[1][:, 0:512]))
            t_cbe = S.op(lambda e: e.copy(cb2[:, 512:1024], pSt2[1][:, 0:512]))
            V.wait(t_cbe)
            T.wait(t_cbe)
            slots = [("g", kc, None) for kc in range(64)]
            if qc == 0:
                slots += [("l", d, d) for d in range(4)]
            else:
                slots += [("l", d, None) for d in range(4)]
                slots += [("l", 4 + d, d) for d in range(4)]
            pairs = [(slots[2 * i], slots[2 * i + 1]) for i in range(len(slots) // 2)]
            npair = len(pairs)
            for pi, pr in enumerate(pairs):
                pb = pi % 2
                if t_sub2[pb] is not None:
                    T.wait(t_sub2[pb])
                for h, (kind, idx2, mband) in enumerate(pr):
                    if kind == "g":
                        r_, w_ = idx2 // 8, idx2 % 8
                        t_st = T.op(lambda e, r_=r_, w_=w_, bi=bi, qlo=qlo, pb=pb,
                                    h=h: e.matmul(
                            pSt2[pb][:, 512 * h:512 * (h + 1)],
                            Kg[:, r_, 1024 * bi + 128 * w_:1024 * bi + 128 * (w_ + 1)],
                            Qf[:, qlo:qlo + 512], start=True, stop=True))
                    else:
                        t_st = T.op(lambda e, idx2=idx2, bi=bi, qlo=qlo, pb=pb,
                                    h=h: e.matmul(
                            pSt2[pb][:, 512 * h:512 * (h + 1)],
                            Kl[:, 1024 * bi + 128 * idx2:1024 * bi + 128 * (idx2 + 1)],
                            Qf[:, qlo:qlo + 512], start=True, stop=True))
                V.wait(t_st)
                if t_se2[pb] is not None:
                    V.wait(t_se2[pb])
                t_sub2[pb] = V.op(lambda e, pb=pb: e.tensor_tensor(
                    tsub2[pb][:], pSt2[pb][:], cb2[:], ALU.subtract))
                if t_PtRd2[pb] is not None:
                    S.wait(t_PtRd2[pb])
                S.wait(t_sub2[pb])
                t_se2[pb] = S.op(lambda e, pb=pb: e.activation(
                    Pt2[pb][:], tsub2[pb][:], AF.Exp, scale=S128))
                src_tok = t_se2[pb]
                for h, (kind, idx2, mband) in enumerate(pr):
                    if mband is not None:
                        V.wait(t_se2[pb])
                        src_tok = V.op(lambda e, pb=pb, h=h, mband=mband:
                                       e.tensor_tensor(
                            Pt2[pb][:, 512 * h:512 * (h + 1)],
                            Pt2[pb][:, 512 * h:512 * (h + 1)],
                            masks[mband][:], ALU.mult))
                T.wait(src_tok)
                for h, (kind, idx2, mband) in enumerate(pr):
                    first = (pi == 0 and h == 0)
                    last = (pi == npair - 1 and h == 1)
                    if kind == "g":
                        lh = lambda e, bi=bi, idx2=idx2: slotmb[:, 64 * bi + idx2:
                                                                64 * bi + idx2 + 1]
                    else:
                        lh = lambda e: ones_bf[:]
                    T.op(lambda e, pb=pb, h=h, first=first, last=last, lh=lh:
                         e.matmul(pSum[:], lh(e),
                                  Pt2[pb][:, 512 * h:512 * (h + 1)],
                                  start=first, stop=last), inc=0)
                    if kind == "g":
                        r_, w_ = idx2 // 8, idx2 % 8
                        jv = 16 * r_ + 8 * bi + w_
                        t_av = T.op(lambda e, jv=jv, pb=pb, h=h, first=first,
                                    last=last: e.matmul(
                            pO[:], Vg[:, jv, :], Pt2[pb][:, 512 * h:512 * (h + 1)],
                            start=first, stop=last))
                    else:
                        jl = 8 * bi + idx2
                        t_av = T.op(lambda e, jl=jl, pb=pb, h=h, first=first,
                                    last=last: e.matmul(
                            pO[:], Vl[:, jl, :], Pt2[pb][:, 512 * h:512 * (h + 1)],
                            start=first, stop=last))
                t_PtRd2[pb] = t_av
            S.wait(t_av)
            if t_oev is not None:
                S.wait(t_oev)
            t_sre = S.op(lambda e: e.copy(srow[:], pSum[:]))
            V.wait(t_sre)
            tv = V.op(lambda e: e.reciprocal(rrow[:], srow[:]))
            T.wait(tv)
            if t_sub2[0] is not None:
                T.wait(t_sub2[0])
            if t_sub2[1] is not None:
                T.wait(t_sub2[1])
            t_rb = T.op(lambda e: e.matmul(pSt2[1][:, 0:512], ones_f1[0:1, 0:128],
                                           rrow[:], start=True, stop=True))
            S.wait(t_rb)
            t_rbe = S.op(lambda e: e.copy(rb[:], pSt2[1][:, 0:512]))
            V.wait(t_rbe)
            t_onrm = V.op(lambda e: e.tensor_tensor(Ofm[:], pO[:], rb[:], ALU.mult))
            for tc in range(4):
                for dh in range(2):
                    T.wait(t_onrm)
                    T.op(lambda e, tc=tc, dh=dh: e.matmul(
                        pSt2[0][:, 512 * dh:512 * (dh + 1)],
                        Ofm[:, 128 * tc:128 * (tc + 1)],
                        WoT[:, 4 * dh:4 * (dh + 1), :].rearrange("p a b -> p (a b)"),
                        start=True, stop=False), inc=0)
                    t_pr = T.op(lambda e, dh=dh: e.matmul(
                        pSt2[0][:, 512 * dh:512 * (dh + 1)], ones_f1[0:1, 0:128],
                        bo_r[:, 512 * dh:512 * (dh + 1)], start=False, stop=True))
                    S.wait(t_pr)
                    t_pe = S.op(lambda e, dh=dh: e.copy(
                        s1_t[:, 512 * dh:512 * (dh + 1)],
                        pSt2[0][:, 512 * dh:512 * (dh + 1)]))
                G.wait(t_pe)
                t_oev = G.dma(io["s1"][qlo + 128 * tc:qlo + 128 * (tc + 1), :], s1_t[:])
                S.wait(t_oev)
                T.wait(t_pe)
            t_sub2 = [t_pe, t_pe]
            t_PtRd2 = [None, None]



def build_full():
    from concourse.aot_env import allow_ici_all_to_all
    nc = bass.Bass("TRN2", target_bir_lowering=False, debug=False)
    io = {}
    ext_in = [("x", [TPC, D]), ("Wq", [D, D]), ("Wk", [D, D]), ("Wv", [D, D]),
              ("Wb", [D]), ("l0g", [D]), ("l0b", [D]), ("Wr", [D, 8]), ("br", [8]),
              ("W1", [NE, D, 4096]), ("b1", [NE, 4096]), ("W2", [NE, 4096, D]),
              ("b2", [NE, D]), ("lng", [D]), ("lnb", [D]),
              ("Wq1", [128, D]), ("bq1", [128]), ("Wk1", [128, D]), ("bk1", [128]),
              ("Wv1", [128, D]), ("bv1", [128]), ("Wo", [D, 128]), ("bo", [D]),
              ("Wg", [D, 8]), ("Wu", [NE, D, 2048]), ("Wd", [NE, 2048, D]),
              ("W1s", [D, 2048]), ("W2s", [2048, D])]
    for nm, shape in ext_in:
        io[nm] = nc.dram_tensor(nm, shape, F32, kind="ExternalInput").ap()
    io["slotmask"] = nc.dram_tensor("slotmask", [128, 128], F32, kind="ExternalInput").ap()
    io["out"] = nc.dram_tensor("out", [TPC, D], F32, kind="ExternalOutput").ap()
    ints = [("qfm", [D, TPC], BF16), ("kbar", [D, TPC], BF16), ("ktok", [TPC, D], BF16),
            ("bv", [TPC, D], BF16), ("beta", [TPC, 1], F32), ("Tt", [NT, 128, 128], BF16),
            ("KQu", [NT, 128, 128], BF16), ("xn2fm", [D, TPC], BF16),
            ("stats", [TPC, 2], F32), ("w", [TPC, 8], F32),
            ("xsend", [L, 3584], BF16), ("xr", [L, 3584], BF16),
            ("besend", [L, 1], F32), ("ber", [L, 1], F32),
            ("o", [L, DV], F32), ("orecv", [L, DV], F32),
            ("moe0", [TPC, D], F32),
            ("x1", [TPC, D], F32), ("xn1fm", [D, TPC], BF16),
            ("Qfm", [128, TPC], BF16), ("Kfm", [128, TPC], BF16),
            ("Vtm", [TPC, 128], BF16), ("qn", [1, TPC], F32), ("kn", [1, TPC], F32),
            ("w1", [TPC, 8], F32), ("kmax", [1, 2], F32),
            ("KG", [1024, TPC], BF16), ("VG", [8 * TPC, 128], BF16), ("kmR", [1, 2], F32),
            ("s1", [TPC, D], F32)]
    for nm, shape, dt in ints:
        io[nm] = nc.dram_tensor(nm, shape, dt, kind="Internal").ap()

    with ExitStack() as stack:
        P = Prog(nc, stack)
        Y, G, T, V, S = P.Y, P.G, P.T, P.V, P.S
        ccsems = [stack.enter_context(nc.semaphore(f"s_cc{i}")) for i in range(6)]
        ncc = [0]

        def coll(kind, op, ins_, outs_):
            sem = ccsems[ncc[0]]
            ncc[0] += 1

            def fn(e, k=kind, o=op, i=ins_, u=outs_):
                with allow_ici_all_to_all():
                    return e.collective_compute(
                        k, o, replica_groups=[list(range(8))], ins=[i], outs=[u])
            return G.op_sem(fn, sem, 1, inc=1)
        P.push_phase()
        emit_A(P, io)
        P.pop_phase()
        P.barrier()
        with allow_ici_all_to_all():
            t_cc1 = coll("AllToAll", ALU.bypass, io["xsend"], io["xr"])
            t_cc2 = coll("AllToAll", ALU.bypass, io["besend"], io["ber"])
        P.push_phase()
        emit_C(P, io)
        P.pop_phase()
        P.barrier()
        Y.wait(t_cc1)
        Y.wait(t_cc2)
        G.wait(t_cc1)
        G.wait(t_cc2)
        P.push_phase()
        emit_B(P, io)
        P.pop_phase()
        P.barrier()
        G.wait(G.dtok())
        with allow_ici_all_to_all():
            t_cc3 = coll("AllToAll", ALU.bypass, io["o"], io["orecv"])
        Y.wait(t_cc3)
        Y.wait(G.dtok())
        P.push_phase()
        emit_D(P, io)
        P.pop_phase()
        P.barrier()
        G.wait(G.dtok())
        t_cc4 = coll("AllGather", ALU.bypass, io["Kfm"], io["KG"])
        t_cc5 = coll("AllGather", ALU.bypass, io["Vtm"], io["VG"])
        t_cc6 = coll("AllReduce", ALU.max, io["kmax"], io["kmR"])
        for t in (t_cc4, t_cc5, t_cc6):
            Y.wait(t)
            G.wait(t)
        io2 = dict(io)
        io2["Qloc"] = io["Qfm"]
        io2["Kloc"] = io["Kfm"]
        io2["Vloc"] = io["Vtm"]
        io2["qnloc"] = io["qn"]
        P.push_phase()
        emit_E2(P, io2)
        P.pop_phase()
        P.barrier()
        Y.wait(G.dtok())
        P.push_phase()
        emit_F(P, io)
        P.pop_phase()
        P.finish()
    return nc


def host_slotmask(c):
    m = np.zeros((128, 128), dtype=np.float32)
    for bi in range(2):
        for kc in range(64):
            if kc < 8 * c:
                m[:, 64 * bi + kc] = 1
    return m


def kernel(**inputs):
    cc = np.ascontiguousarray
    x = cc(inputs["L0_attn_x"].astype(np.float32, copy=False))
    xs = shard_tokens(x)
    wmap = {"Wq": "L0_attn_Wq", "Wk": "L0_attn_Wk", "Wv": "L0_attn_Wv",
            "Wb": "L0_attn_Wb", "l0g": "L0_ffn_gamma", "l0b": "L0_ffn_beta",
            "Wr": "L0_ffn_Wr", "br": "L0_ffn_br", "W1": "L0_ffn_W1",
            "b1": "L0_ffn_b1", "W2": "L0_ffn_W2", "b2": "L0_ffn_b2",
            "lng": "L0_attn_ln_g", "lnb": "L0_attn_ln_b",
            "Wq1": "L1_attn_Wq", "bq1": "L1_attn_bq", "Wk1": "L1_attn_Wk",
            "bk1": "L1_attn_bk", "Wv1": "L1_attn_Wv", "bv1": "L1_attn_bv",
            "Wo": "L1_attn_Wo", "bo": "L1_attn_bo", "Wg": "L1_ffn_Wg",
            "Wu": "L1_ffn_Wu", "Wd": "L1_ffn_Wd", "W1s": "L1_ffn_W1",
            "W2s": "L1_ffn_W2"}
    base = {k: cc(inputs[v].astype(np.float32, copy=False)) for k, v in wmap.items()}
    nc = build_full()
    in_maps = [dict(base, x=cc(xs[c]), slotmask=host_slotmask(c)) for c in range(NC)]
    res = run_bass_kernel_spmd(nc, in_maps, core_ids=list(range(NC)))
    out = unshard_tokens([res.results[c]["out"] for c in range(NC)])
    return np.ascontiguousarray(out.astype(np.float32))


# version 17
# speedup vs baseline: 1.0139x; 1.0139x over previous
import numpy as np
import ml_dtypes
from contextlib import ExitStack
import concourse.bass as bass
import concourse.mybir as mybir
from concourse.bass_utils import run_bass_kernel_spmd

F32 = mybir.dt.float32
BF16 = mybir.dt.bfloat16
AF = mybir.ActivationFunctionType
ALU = mybir.AluOpType
AX = mybir.AxisListType

NC = 8
D = 1024
TPC = 2048
NT = TPC // 128
ND = D // 128
L = 8192
LN_EPS = 1e-5
QSCALE = float(D) ** -0.5
BF = ml_dtypes.bfloat16


class REng:

    def __init__(self, sem, dsem):
        self.sem = sem
        self.dsem = dsem
        self.n = 0
        self.nd = 0
        self.ops = []
        self.prog = None
        self.dbounds = set()

    def wait(self, tok):
        if tok is not None:
            self.ops.append(("w", tok))
            if self.prog is not None:
                own = self.prog.dsem_owner.get(id(tok[0]))
                if own is not None and own is not self:
                    own.dbounds.add(tok[1])

    def op(self, fn, inc=1):
        self.ops.append(("o", fn, inc, self.sem))
        if inc:
            self.n += inc
            return (self.sem, self.n)
        return None

    def op_sem(self, fn, sem, cum, inc=1):
        self.ops.append(("o", fn, inc, sem))
        return (sem, cum)

    def dma(self, out, in_):
        self.ops.append(("o", lambda e, out=out, in_=in_: e.dma_start(out=out, in_=in_),
                         16, self.dsem))
        self.nd += 16
        return (self.dsem, self.nd)

    def dma_sem(self, out, in_, sem):
        self.ops.append(("o", lambda e, out=out, in_=in_: e.dma_start(out=out, in_=in_),
                         16, sem))

    def apply_dma_fences(self):
        if not self.dbounds:
            return
        bounds = sorted(self.dbounds)
        out = []
        pre = 0
        fenced = 0
        for item in self.ops:
            if item[0] == "o" and item[3] is self.dsem and item[2]:
                need = 0
                for b in bounds:
                    if b <= pre:
                        need = b
                    else:
                        break
                if need > fenced:
                    out.append(("w", (self.dsem, need)))
                    fenced = need
                pre += item[2]
            out.append(item)
        self.ops = out

    def tok(self):
        return (self.sem, self.n)

    def dtok(self):
        return (self.dsem, self.nd)

    def emit(self, e):
        for item in self.ops:
            if item[0] == "w":
                e.wait_ge(item[1][0], item[1][1])
            else:
                ins = item[1](e)
                if item[2]:
                    ins.then_inc(item[3], item[2])


class Prog:

    def __init__(self, nc, stack):
        self.nc = nc
        self.stack = stack
        self.sems = {nm: stack.enter_context(nc.semaphore("s_" + nm))
                     for nm in ("Y", "G", "T", "V", "S", "Yd", "Gd", "Td", "Vd", "Sd")}
        self.Y = REng(self.sems["Y"], self.sems["Yd"])
        self.G = REng(self.sems["G"], self.sems["Gd"])
        self.T = REng(self.sems["T"], self.sems["Td"])
        self.V = REng(self.sems["V"], self.sems["Vd"])
        self.S = REng(self.sems["S"], self.sems["Sd"])
        self.all = [self.Y, self.G, self.T, self.V, self.S]
        self.dsem_owner = {}
        for r in self.all:
            r.prog = self
            self.dsem_owner[id(r.dsem)] = r
        self.phase_stacks = []

    def barrier(self, engs=None):
        engs = engs or self.all
        toks = []
        for r in engs:
            if r.n > 0:
                toks.append((r, (r.sem, r.n)))
            if r.nd > 0:
                toks.append((r, (r.dsem, r.nd)))
        for r in engs:
            for owner, t in toks:
                r.wait(t)

    def sb(self, name, shape, dt):
        st = self.phase_stacks[-1] if self.phase_stacks else self.stack
        return st.enter_context(self.nc.sbuf_tensor(name, shape, dt))

    def ps(self, name, shape, dt=F32):
        st = self.phase_stacks[-1] if self.phase_stacks else self.stack
        return st.enter_context(self.nc.psum_tensor(name, shape, dt))

    def push_phase(self):
        st = ExitStack()
        self.phase_stacks.append(st)
        return st

    def pop_phase(self):
        self.phase_stacks.pop().close()

    def finish(self):
        for r in self.all:
            r.apply_dma_fences()
        for r in self.all:
            if r.n > 0:
                r.wait((r.sem, r.n))
            if r.nd > 0:
                r.wait((r.dsem, r.nd))
        with self.nc.Block() as blk:
            blk.sync(lambda e: self.Y.emit(e))
            blk.gpsimd(lambda e: self.G.emit(e))
            blk.tensor(lambda e: self.T.emit(e))
            blk.vector(lambda e: self.V.emit(e))
            blk.scalar(lambda e: self.S.emit(e))


def make_ident(P, t):
    tz = P.G.op(lambda e, t=t: e.memset(t, 0.0))
    P.G.wait(tz)
    return P.G.op(lambda e, t=t: e.affine_select(
        out=t, in_=t, compare_op=ALU.not_equal, fill=1.0,
        base=0, pattern=[[-1, t.shape[-1]]], channel_multiplier=1))


def make_tri(P, t, kind, fill):
    tz = P.G.op(lambda e, t=t: e.memset(t, 0.0))
    P.G.wait(tz)
    if kind == "su":
        args = dict(compare_op=ALU.is_ge, pattern=[[-1, t.shape[-1]]], channel_multiplier=1)
    elif kind == "ui":
        args = dict(compare_op=ALU.is_gt, pattern=[[-1, t.shape[-1]]], channel_multiplier=1)
    elif kind == "sl":
        args = dict(compare_op=ALU.is_ge, pattern=[[1, t.shape[-1]]], channel_multiplier=-1)
    return P.G.op(lambda e, t=t, a=args: e.affine_select(out=t, in_=t, fill=fill, base=0, **a))


def build_A():
    nc = bass.Bass("TRN2", target_bir_lowering=False, debug=False)
    io = {}
    for nm, shape in [("x", [TPC, D]), ("Wq", [D, D]), ("Wk", [D, D]),
                      ("Wv", [D, D]), ("Wb", [D]), ("l0g", [D]), ("l0b", [D]),
                      ("Wr", [D, 8]), ("br", [8])]:
        io[nm] = nc.dram_tensor(nm, shape, F32, kind="ExternalInput").ap()
    for nm, shape, dt in [("qfm", [D, TPC], BF16), ("kbar", [D, TPC], BF16),
                          ("ktok", [TPC, D], BF16), ("bv", [TPC, D], BF16),
                          ("beta", [TPC, 1], F32), ("Tt", [NT, 128, 128], BF16),
                          ("KQu", [NT, 128, 128], BF16), ("xn2fm", [D, TPC], BF16),
                          ("stats", [TPC, 2], F32), ("w", [TPC, 8], F32)]:
        io[nm] = nc.dram_tensor(nm, shape, dt, kind="ExternalOutput").ap()
    with ExitStack() as stack:
        P = Prog(nc, stack)
        emit_A(P, io)
        P.finish()
    return nc


def emit_A(P, io):
    nc = P.nc
    Y, G, T, V, S = P.Y, P.G, P.T, P.V, P.S
    fused = "xsend" in io

    x_sb = P.sb("a_x", [128, 2, D], F32)
    sqs = P.sb("a_sqs", [128, D], F32)
    sx = P.sb("a_sx", [128, NT], F32)
    sq2 = P.sb("a_sq2", [128, NT], F32)
    mu = P.sb("a_mu", [128, NT], F32)
    var = P.sb("a_var", [128, NT], F32)
    rstd = P.sb("a_rstd", [128, NT], F32)
    rr2 = P.sb("a_rr2", [128, NT], F32)
    tca = P.sb("a_tca", [128, NT], F32)
    tcb = P.sb("a_tcb", [128, NT], F32)
    muvar = P.sb("a_muvar", [128, NT, 2], F32)
    xn0_tm = P.sb("a_xn0tm", [128, D], BF16)
    xn2_pre = P.sb("a_xn2pre", [128, D], F32)
    xn2_t2 = P.sb("a_xn2t2", [128, D], F32)
    xn2_tm = P.sb("a_xn2tm", [128, D], BF16)
    xn0_fm = P.sb("a_xn0fm", [128, ND, TPC], BF16)
    fm_st = P.sb("a_fmst", [128, ND, 128], BF16)
    q_st = [P.sb(f"a_qst{g}", [128, ND, 512], BF16) for g in range(2)]
    ksc = P.sb("a_ksc", [128, D], BF16)
    ksq = P.sb("a_ksq", [128, D], BF16)
    ktok_t = P.sb("a_ktokt", [128, D], BF16)
    vsc = P.sb("a_vsc", [128, D], BF16)
    bv_t = P.sb("a_bvt", [128, D], BF16)
    k2c = P.sb("a_k2c", [128, 1], F32)
    k2h = P.sb("a_k2h", [128, 1], F32)
    rinv = P.sb("a_rinv", [128, 1], F32)
    beta_c = P.sb("a_betac", [128, NT], F32)
    beta_r = P.sb("a_betar", [1, TPC], F32)
    gb_bc = P.sb("a_gbbc", [128, D], F32)
    bb_bc = P.sb("a_bbbc", [128, D], F32)
    grow = P.sb("a_grow", [1, D], F32)
    brow = P.sb("a_brow", [1, D], F32)
    Wq_sb = P.sb("a_Wq", [128, ND, D], BF16)
    Wk_sb = P.sb("a_Wk", [128, ND, D], BF16)
    Wv_sb = P.sb("a_Wv", [128, ND, D], BF16)
    Wb_pad = P.sb("a_Wbp", [128, 128], BF16)
    Wb_sb = P.sb("a_Wb", [128, ND], BF16)
    ident = P.sb("a_id", [128, 128], BF16)
    ones_f1 = P.sb("a_o1", [1, 512], F32)
    mSUn = P.sb("a_msun", [128, 128], F32)
    mSLn = P.sb("a_msln", [128, 128], F32)
    mUI = P.sb("a_mui", [128, 128], F32)
    bbc = P.sb("a_bbc", [128, 128], F32)
    Xt = [P.sb(f"a_X{i}", [128, 128], BF16) for i in range(2)]
    Yt = [P.sb(f"a_Y{i}", [128, 128], BF16) for i in range(2)]
    Pt = [P.sb(f"a_P{i}", [128, 128], BF16) for i in range(2)]
    kq_sb = P.sb("a_kq", [128, 128], BF16)
    x0_sb = P.sb("a_x0", [128, 128], F32)

    eps_c = P.sb("a_eps", [128, 1], F32)
    ident_f = P.sb("a_idf", [128, 128], F32)
    xn2f = P.sb("a_xn2f", [128, D], F32)
    x2f_st = P.sb("a_x2fst", [128, ND, 128], F32)
    Wrf = P.sb("a_Wrf", [128, ND, 8], F32)
    brf = P.sb("a_brf", [1, 8], F32)
    probs = P.sb("a_probs", [128, 8], F32)
    Px = P.sb("a_Px", [128, 8], F32)
    msk = P.sb("a_msk", [128, 8], F32)
    m1 = P.sb("a_m1", [128, 1], F32)
    m2 = P.sb("a_m2", [128, 1], F32)
    nm_ = P.sb("a_nm", [128, 1], F32)
    rs_ = P.sb("a_rs", [128, 1], F32)
    w_t = P.sb("a_wt", [128, 8], F32)
    ppA = P.ps("a_ppA", [128, 512])
    ppB = P.ps("a_ppB", [128, 512])
    ptA = P.ps("a_ptA", [128, 128], BF16)
    ptB = P.ps("a_ptB", [128, 128], BF16)
    pkA = P.ps("a_pkA", [128, 128])
    pkB = P.ps("a_pkB", [128, 128])
    prow = P.ps("a_prow", [1, 512])
    pcol = P.ps("a_pcol", [128, 1])

    make_ident(P, ident[:])
    make_tri(P, mSUn[:], "su", -1.0)
    make_tri(P, mSLn[:], "sl", -1.0)
    make_tri(P, mUI[:], "ui", 1.0)
    G.op(lambda e: e.memset(Wb_pad[:], 0.0))
    G.op(lambda e: e.memset(ones_f1[:], 1.0))
    G.op(lambda e: e.memset(eps_c[:], LN_EPS))
    tzf = G.op(lambda e: e.memset(ident_f[:], 0.0))
    G.wait(tzf)
    G.op(lambda e: e.affine_select(out=ident_f[:], in_=ident_f[:],
                                   compare_op=ALU.not_equal, fill=1.0,
                                   base=0, pattern=[[-1, 128]], channel_multiplier=1))
    for r in range(ND):
        G.dma(Wq_sb[:, r, :], io["Wq"][128 * r:128 * (r + 1), :])
        G.dma(Wk_sb[:, r, :], io["Wk"][128 * r:128 * (r + 1), :])
        G.dma(Wv_sb[:, r, :], io["Wv"][128 * r:128 * (r + 1), :])
        Y.dma(Wrf[:, r, :], io["Wr"][128 * r:128 * (r + 1), :])
    G.dma(Wb_pad[0:ND, :], io["Wb"].rearrange("(r p) -> r p", p=128))
    Y.dma(grow[:], io["l0g"][None, :])
    Y.dma(brow[:], io["l0b"][None, :])
    Y.dma(brf[:], io["br"][None, :])
    P.barrier()
    T.op(lambda e: e.transpose(ptA[:], Wb_pad[:], ident[:]))
    P.barrier([T, V])
    V.op(lambda e: e.tensor_copy(Wb_sb[:], ptA[:, 0:ND]))
    for h in range(2):
        T.op(lambda e, h=h: e.matmul(ppA[:], ones_f1[0:1, 0:128],
                                     grow[:, 512 * h:512 * (h + 1)], start=True, stop=True))
        T.op(lambda e, h=h: e.matmul(ppB[:], ones_f1[0:1, 0:128],
                                     brow[:, 512 * h:512 * (h + 1)], start=True, stop=True))
        P.barrier([T, S])
        S.op(lambda e, h=h: e.copy(gb_bc[:, 512 * h:512 * (h + 1)], ppA[:]))
        S.op(lambda e, h=h: e.copy(bb_bc[:, 512 * h:512 * (h + 1)], ppB[:]))
        P.barrier([T, S])

    t_sq = None
    t_cons = [None, None]
    for i in range(NT):
        b = i % 2
        if t_cons[b] is not None:
            Y.wait(t_cons[b])
        t_x = Y.dma(x_sb[:, b, :], io["x"][128 * i:128 * (i + 1), :])
        V.wait(t_x)
        V.op(lambda e, b=b, i=i: e.reduce_sum(sx[:, i:i + 1], x_sb[:, b, :], axis=AX.X))
        S.wait(t_x)
        if t_sq:
            S.wait(t_sq)
        t_sq = S.op(lambda e, b=b, i=i: e.activation(sqs[:], x_sb[:, b, :], AF.Square,
                                                     accum_out=sq2[:, i:i + 1]))
        t_cons[b] = t_sq
        Y.wait((V.sem, V.n))
    P.barrier([V, S])
    c1 = 1.0 / D

    def vch(fn):
        t = V.op(fn)
        V.wait(t)
        return t

    vch(lambda e: e.tensor_scalar(mu[:], sx[:], c1, None, ALU.mult))
    vch(lambda e: e.tensor_scalar(tca[:], sq2[:], c1, None, ALU.mult))
    vch(lambda e: e.tensor_tensor(tcb[:], mu[:], mu[:], ALU.mult))
    t = vch(lambda e: e.tensor_tensor(var[:], tca[:], tcb[:], ALU.subtract))
    S.wait(t)
    t = S.op(lambda e: e.activation(tca[:], var[:], AF.Sqrt, bias=eps_c[:, 0:1]))
    V.wait(t)
    vch(lambda e: e.reciprocal(rstd[:], tca[:]))
    vch(lambda e: e.tensor_scalar(tca[:], var[:], LN_EPS, None, ALU.add))
    vch(lambda e: e.reciprocal(tcb[:], tca[:]))
    t = vch(lambda e: e.tensor_tensor(tcb[:], var[:], tcb[:], ALU.mult))
    S.wait(t)
    t = S.op(lambda e: e.activation(tca[:], tcb[:], AF.Sqrt, bias=eps_c[:, 0:1]))
    V.wait(t)
    vch(lambda e: e.reciprocal(tcb[:], tca[:]))
    vch(lambda e: e.tensor_tensor(rr2[:], rstd[:], tcb[:], ALU.mult))
    V.op(lambda e: e.tensor_copy(muvar[:, :, 0], mu[:]), inc=0)
    t = V.op(lambda e: e.tensor_copy(muvar[:, :, 1], var[:]))
    G.wait(t)
    G.dma(io["stats"].rearrange("(i p) s -> p i s", p=128), muvar[:])
    P.barrier()

    tr = {"i": 0, "ev": [None, None]}
    trf = {"i": 0, "ev": [None, None]}

    def transpose_to_f(dst_ap, src_ap, src_tok):
        b = trf["i"] % 2
        trf["i"] += 1
        pt = pkA if b == 0 else pkB
        T.wait(src_tok)
        if trf["ev"][b] is not None:
            T.wait(trf["ev"][b])
        tmm = T.op(lambda e, pt=pt, s=src_ap: e.transpose(pt[:], s, ident_f[:]))
        if b == 0:
            V.wait(tmm)
            ev = V.op(lambda e, pt=pt, d=dst_ap: e.tensor_copy(d, pt[:]))
        else:
            S.wait(tmm)
            ev = S.op(lambda e, pt=pt, d=dst_ap: e.copy(d, pt[:]))
        trf["ev"][b] = ev
        return ev

    def transpose_to(dst_ap, src_ap, src_tok):
        b = tr["i"] % 2
        tr["i"] += 1
        pt = ptA if b == 0 else ptB
        T.wait(src_tok)
        if tr["ev"][b] is not None:
            T.wait(tr["ev"][b])
        tmm = T.op(lambda e, pt=pt, s=src_ap: e.transpose(pt[:], s, ident[:]))
        if b == 0:
            V.wait(tmm)
            ev = V.op(lambda e, pt=pt, d=dst_ap: e.tensor_copy(d, pt[:]))
        else:
            S.wait(tmm)
            ev = S.op(lambda e, pt=pt, d=dst_ap: e.copy(d, pt[:]))
        tr["ev"][b] = ev
        return ev

    t_xcons = [None, None]
    for i in range(NT):
        b = i % 2
        if t_xcons[b] is not None:
            Y.wait(t_xcons[b])
        t_x = Y.dma(x_sb[:, b, :], io["x"][128 * i:128 * (i + 1), :])
        V.wait(t_x)
        t0 = V.op(lambda e, b=b, i=i: e.tensor_scalar(
            xn0_tm[:], x_sb[:, b, :], mu[:, i:i + 1], rstd[:, i:i + 1],
            ALU.subtract, ALU.mult))
        for r in range(ND):
            ev = transpose_to(xn0_fm[:, r, 128 * i:128 * (i + 1)],
                              xn0_tm[:, 128 * r:128 * (r + 1)], t0)
        V.wait(ev)
        t1 = V.op(lambda e, b=b, i=i: e.tensor_scalar(
            xn2_pre[:], x_sb[:, b, :], mu[:, i:i + 1], rr2[:, i:i + 1],
            ALU.subtract, ALU.mult))
        t_xcons[b] = t1
        V.wait(t1)
        t1 = V.op(lambda e: e.tensor_tensor(xn2_t2[:], xn2_pre[:], gb_bc[:], ALU.mult))
        V.wait(t1)
        t1b = V.op(lambda e: e.tensor_tensor(xn2_tm[:], xn2_t2[:], bb_bc[:], ALU.add))
        V.wait(t1b)
        t1 = V.op(lambda e: e.tensor_tensor(xn2f[:], xn2_t2[:], bb_bc[:], ALU.add))
        for r in range(ND):
            ev = transpose_to(fm_st[:, r, :], xn2_tm[:, 128 * r:128 * (r + 1)], t1b)
        G.wait(tr["ev"][0])
        G.wait(tr["ev"][1])
        t_fmdma = G.dma(io["xn2fm"][:, 128 * i:128 * (i + 1)]
                        .rearrange("(r p) t -> p r t", p=128), fm_st[:])
        evf = None
        for r in range(ND):
            evf = transpose_to_f(x2f_st[:, r, :], xn2f[:, 128 * r:128 * (r + 1)], t1)
        T.wait(trf["ev"][0])
        T.wait(trf["ev"][1])
        for r in range(ND):
            T.op(lambda e, r=r: e.matmul(
                ppA[:, 0:8], x2f_st[:, r, :], Wrf[:, r, :],
                start=(r == 0), stop=False), inc=0)
        t_mm8 = T.op(lambda e: e.matmul(ppA[:, 0:8], ones_f1[0:1, 0:128], brf[:],
                                        start=False, stop=True))
        V.wait(t_mm8)

        def vc0(fn):
            t = V.op(fn)
            V.wait(t)
            return t
        vc0(lambda e: e.reduce_max(m1[:], ppA[:, 0:8], axis=AX.X))
        vc0(lambda e: e.tensor_scalar(nm_[:], m1[:], -1.0, None, ALU.mult))
        tpx = vc0(lambda e: e.tensor_copy(Px[:], ppA[:, 0:8]))
        S.wait(tpx)
        tpx = S.op(lambda e: e.activation(Px[:], Px[:], AF.Exp, bias=nm_[:, 0:1]))
        V.wait(tpx)
        vc0(lambda e: e.reduce_sum(m2[:], Px[:], axis=AX.X))
        vc0(lambda e: e.reciprocal(rs_[:], m2[:]))
        vc0(lambda e: e.tensor_scalar(probs[:], Px[:], rs_[:, 0:1], None, ALU.mult))
        vc0(lambda e: e.reduce_max(m1[:], probs[:], axis=AX.X))
        vc0(lambda e: e.tensor_scalar(msk[:], probs[:], m1[:, 0:1], None, ALU.is_equal))
        vc0(lambda e: e.scalar_tensor_tensor(Px[:], msk[:], -1e30, probs[:],
                                             ALU.mult, ALU.add))
        vc0(lambda e: e.reduce_max(m2[:], Px[:], axis=AX.X))
        vc0(lambda e: e.tensor_scalar(msk[:], probs[:], m2[:, 0:1], None, ALU.is_ge))
        t_w = vc0(lambda e: e.tensor_tensor(w_t[:], probs[:], msk[:], ALU.mult))
        G.wait(t_w)
        t_wdma2 = G.dma(io["w"][128 * i:128 * (i + 1), :], w_t[:])
        V.wait(t_wdma2)
        T.wait(t_mm8)
        T.wait(t_fmdma)
    P.barrier()

    pp = [ppA, ppB]
    ppev = [None, None]
    ppi = [0]

    def mm_group(dst_bank_tok_consumer, emit_fn):
        b = ppi[0] % 2
        ppi[0] += 1
        if ppev[b] is not None:
            T.wait(ppev[b])
        tmm = emit_fn(pp[b])
        return b, pp[b], tmm

    def set_ppev(b, tok):
        ppev[b] = tok

    cpv = {}
    qdma_tok = [None, None]
    for g in range(4):
        qs = q_st[g % 2]
        for dq in range(ND):
            def mms(ppt, dq=dq, g=g):
                for r in range(ND):
                    ins = T.op(lambda e, r=r, dq=dq, g=g, ppt=ppt: e.matmul(
                        ppt[:], Wq_sb[:, r, 128 * dq:128 * (dq + 1)],
                        xn0_fm[:, r, 512 * g:512 * (g + 1)],
                        start=(r == 0), stop=(r == ND - 1)), inc=(1 if r == ND - 1 else 0))
                return ins
            b, ppt, tmm = mm_group(None, mms)
            S.wait(tmm)
            if dq == 0 and qdma_tok[g % 2] is not None:
                S.wait(qdma_tok[g % 2])
            ev = S.op(lambda e, ppt=ppt, dq=dq, qs=qs: e.activation(
                qs[:, dq, :], ppt[:], AF.Copy, scale=QSCALE))
            set_ppev(b, ev)
        G.wait(ev)
        if fused:
            s_, coff = g // 2, 512 * (g % 2)
            for dq in range(ND):
                for j in range(4):
                    rq = 1024 * (4 * s_ + j) + 128 * dq
                    G.dma(io["xsend"][rq:rq + 128, coff:coff + 512], qs[:, dq, :])
        else:
            for dq in range(ND):
                G.dma(io["qfm"][128 * dq:128 * (dq + 1), 512 * g:512 * (g + 1)], qs[:, dq, :])
        qdma_tok[g % 2] = G.dtok()
        for r in range(ND):
            tmm = T.op(lambda e, r=r, g=g: e.matmul(
                prow[:], Wb_sb[:, r:r + 1], xn0_fm[:, r, 512 * g:512 * (g + 1)],
                start=(r == 0), stop=(r == ND - 1)), inc=(1 if r == ND - 1 else 0))
        S.wait(tmm)
        t_brow = S.op(lambda e, g=g: e.activation(
            beta_r[:, 512 * g:512 * (g + 1)], prow[:], AF.Sigmoid))
        T.wait(t_brow)

        for ic in range(4):
            i = 4 * g + ic
            for r in range(ND):
                tmm = T.op(lambda e, r=r, i=i: e.matmul(
                    pcol[:], xn0_fm[:, r, 128 * i:128 * (i + 1)], Wb_sb[:, r:r + 1],
                    start=(r == 0), stop=(r == ND - 1)), inc=(1 if r == ND - 1 else 0))
            S.wait(tmm)
            t_bc = S.op(lambda e, i=i: e.activation(beta_c[:, i:i + 1], pcol[:], AF.Sigmoid))
            T.wait(t_bc)

            evs = []
            for h in range(2):
                def mms(ppt, i=i, h=h):
                    for r in range(ND):
                        ins = T.op(lambda e, r=r, i=i, h=h, ppt=ppt: e.matmul(
                            ppt[:], xn0_fm[:, r, 128 * i:128 * (i + 1)],
                            Wk_sb[:, r, 512 * h:512 * (h + 1)],
                            start=(r == 0), stop=(r == ND - 1)), inc=(1 if r == ND - 1 else 0))
                    return ins
                b, ppt, tmm = mm_group(None, mms)
                S.wait(tmm)
                ev = S.op(lambda e, ppt=ppt, h=h: e.activation(
                    ksc[:, 512 * h:512 * (h + 1)], ppt[:], AF.Copy))
                set_ppev(b, ev)
                evs.append(ev)
            S.wait(evs[-1])
            tsq = S.op(lambda e: e.activation(ksq[:], ksc[:], AF.Square))
            V.wait(tsq)
            tv = V.op(lambda e: e.reduce_sum(k2c[:], ksq[:], axis=AX.X))
            S.wait(tv)
            ts2 = S.op(lambda e: e.activation(k2h[:], k2c[:], AF.Sqrt))
            V.wait(ts2)
            tv = V.op(lambda e: e.tensor_scalar(k2h[:], k2h[:], 1e-12, None, ALU.max))
            V.wait(tv)
            tv = V.op(lambda e: e.reciprocal(rinv[:], k2h[:]))
            V.wait(tv)
            t_kn = V.op(lambda e: e.tensor_scalar(ktok_t[:], ksc[:], rinv[:, 0:1],
                                                  None, ALU.mult))
            G.wait(t_kn)
            s_ = i // 8
            tloc = 128 * (i - 8 * s_)
            if fused:
                for j in range(4):
                    r0 = 1024 * (4 * s_ + j)
                    t_ktdma = G.dma(io["xsend"][r0 + tloc:r0 + tloc + 128, 2048:3072],
                                    ktok_t[:])
            else:
                t_ktdma = G.dma(io["ktok"][128 * i:128 * (i + 1), :], ktok_t[:])
            for r in range(ND):
                ev = transpose_to(fm_st[:, r, :], ktok_t[:, 128 * r:128 * (r + 1)], t_kn)
            G.wait(tr["ev"][0])
            G.wait(tr["ev"][1])
            if fused:
                for j in range(4):
                    r0 = 1024 * (4 * s_ + j)
                    t_kbdma = G.dma(io["xsend"][r0:r0 + 1024,
                                               1024 + tloc:1024 + tloc + 128]
                                    .rearrange("(r p) t -> p r t", p=128), fm_st[:])
            else:
                t_kbdma = G.dma(io["kbar"][:, 128 * i:128 * (i + 1)]
                                .rearrange("(r p) t -> p r t", p=128), fm_st[:])

            evs = []
            for h in range(2):
                def mms(ppt, i=i, h=h):
                    for r in range(ND):
                        ins = T.op(lambda e, r=r, i=i, h=h, ppt=ppt: e.matmul(
                            ppt[:], xn0_fm[:, r, 128 * i:128 * (i + 1)],
                            Wv_sb[:, r, 512 * h:512 * (h + 1)],
                            start=(r == 0), stop=(r == ND - 1)), inc=(1 if r == ND - 1 else 0))
                    return ins
                b, ppt, tmm = mm_group(None, mms)
                S.wait(tmm)
                ev = S.op(lambda e, ppt=ppt, h=h: e.activation(
                    vsc[:, 512 * h:512 * (h + 1)], ppt[:], AF.Silu))
                set_ppev(b, ev)
                evs.append(ev)
            V.wait(evs[-1])
            V.wait(t_bc)
            t_bv = V.op(lambda e, i=i: e.tensor_scalar(
                bv_t[:], vsc[:], beta_c[:, i:i + 1], None, ALU.mult))
            G.wait(t_bv)
            if fused:
                for j in range(4):
                    r0 = 1024 * (4 * s_ + j)
                    G.dma(io["xsend"][r0 + tloc:r0 + tloc + 128, 3072:3328],
                          bv_t[:, 256 * j:256 * (j + 1)])
            else:
                G.dma(io["bv"][128 * i:128 * (i + 1), :], bv_t[:])

            if cpv.get("t_kqe") is not None:
                T.wait(cpv["t_kqe"])
            T.wait(t_brow)
            t_bb = T.op(lambda e, i=i: e.matmul(pkB[:], ones_f1[0:1, 0:128],
                                                beta_r[:, 128 * i:128 * (i + 1)],
                                                start=True, stop=True))
            S.wait(t_bb)
            if cpv.get("v4") is not None:
                S.wait(cpv["v4"])
            t_bbc = S.op(lambda e: e.copy(bbc[:], pkB[:]))
            T.wait(tr["ev"][0])
            T.wait(tr["ev"][1])
            if cpv.get("vP") is not None:
                T.wait(cpv["vP"])
            for r in range(ND):
                t_kk = T.op(lambda e, r=r: e.matmul(
                    pkA[:], fm_st[:, r, :], fm_st[:, r, :],
                    start=(r == 0), stop=(r == ND - 1)), inc=(1 if r == ND - 1 else 0))
            V.wait(t_kk)
            if cpv.get("tC") is not None:
                V.wait(cpv["tC"])
            v1 = V.op(lambda e: e.tensor_tensor(x0_sb[:], pkA[:], mSLn[:], ALU.mult))
            V.wait(v1)
            V.wait(t_bc)
            V.op(lambda e, i=i: e.tensor_scalar(Xt[0][:], x0_sb[:],
                                                beta_c[:, i:i + 1], None, ALU.mult),
                 inc=0)
            v3 = V.op(lambda e: e.tensor_tensor(x0_sb[:], pkA[:], mSUn[:], ALU.mult))
            V.wait(v3)
            V.wait(t_bbc)
            if cpv.get("s2") is not None:
                V.wait(cpv["s2"])
            v4 = V.op(lambda e: e.tensor_tensor(Yt[0][:], x0_sb[:], bbc[:], ALU.mult))
            V.wait(v4)
            if cpv.get("t_Tt") is not None:
                V.wait(cpv["t_Tt"])
            v5 = V.op(lambda e: e.tensor_tensor(Pt[0][:], Yt[0][:], ident[:], ALU.add))
            cur = 0
            vP = v5
            s1 = s2 = None
            tC = None
            for lvl in range(6):
                nxt = 1 - cur
                T.wait(v5 if lvl == 0 else s2)
                T.wait(vP)
                if lvl == 0:
                    T.wait(t_bbc)
                tA = T.op(lambda e, c=cur: e.matmul(pkA[:], Yt[c][:], Xt[c][:],
                                                    start=True, stop=True))
                tB = T.op(lambda e, c=cur: e.matmul(pkB[:], Xt[c][:], Yt[c][:],
                                                    start=True, stop=True))
                S.wait(tA)
                s1 = S.op(lambda e, nx=nxt: e.copy(Xt[nx][:], pkA[:]))
                S.wait(tB)
                s2 = S.op(lambda e, nx=nxt: e.copy(Yt[nx][:], pkB[:]))
                T.wait(s1)
                tC = T.op(lambda e, nx=nxt, c=cur: e.matmul(
                    pkA[:], Xt[nx][:], Pt[c][:], start=True, stop=True))
                V.wait(tC)
                V.wait(vP)
                vP = V.op(lambda e, nx=nxt, c=cur: e.tensor_tensor(
                    Pt[nx][:], pkA[:], Pt[c][:], ALU.add))
                cur = nxt
            G.wait(vP)
            if fused:
                for j in range(4):
                    r0 = 1024 * (4 * s_ + j)
                    t_Tt = G.dma(io["xsend"][r0 + tloc:r0 + tloc + 128, 3328:3456],
                                 Pt[cur][:])
            else:
                t_Tt = G.dma(io["Tt"][i, :, :], Pt[cur][:])
            T.wait(s2)
            for r in range(ND):
                t_kq = T.op(lambda e, r=r, ic=ic, qs=qs: e.matmul(
                    pkB[:], fm_st[:, r, :], qs[:, r, 128 * ic:128 * (ic + 1)],
                    start=(r == 0), stop=(r == ND - 1)), inc=(1 if r == ND - 1 else 0))
            V.wait(t_kq)
            if cpv.get("t_kqd") is not None:
                V.wait(cpv["t_kqd"])
            t_kqe = V.op(lambda e: e.tensor_tensor(kq_sb[:], pkB[:], mUI[:], ALU.mult))
            G.wait(t_kqe)
            if fused:
                for j in range(4):
                    r0 = 1024 * (4 * s_ + j)
                    t_kqd = G.dma(io["xsend"][r0 + tloc:r0 + tloc + 128, 3456:3584],
                                  kq_sb[:])
                G.wait(t_bc)
                for j in range(4):
                    r0 = 1024 * (4 * s_ + j)
                    G.dma(io["besend"][r0 + tloc:r0 + tloc + 128, :], beta_c[:, i:i + 1])
            else:
                t_kqd = G.dma(io["KQu"][i, :, :], kq_sb[:])
                G.wait(t_bc)
                G.dma(io["beta"][128 * i:128 * (i + 1), :], beta_c[:, i:i + 1])
            T.wait(t_kbdma)
            cpv = {"t_kqe": t_kqe, "vP": vP, "tC": tC, "s2": s2, "t_Tt": t_Tt,
                   "v4": v4, "t_kqd": t_kqd}


def shard_tokens(arr):
    return [np.concatenate([arr[0, 1024 * c:1024 * (c + 1)],
                            arr[1, 1024 * c:1024 * (c + 1)]], axis=0)
            for c in range(NC)]


def unshard_tokens(parts):
    s0 = np.concatenate([np.asarray(parts[c])[0:1024] for c in range(NC)], axis=0)
    s1 = np.concatenate([np.asarray(parts[b])[1024:2048] for b in range(8)], axis=0)
    return np.stack([s0, s1], axis=0)


def run_A(inputs):
    x = np.ascontiguousarray(inputs["L0_attn_x"])
    xs = shard_tokens(x)
    base = {"Wq": inputs["L0_attn_Wq"], "Wk": inputs["L0_attn_Wk"],
            "Wv": inputs["L0_attn_Wv"], "Wb": inputs["L0_attn_Wb"],
            "l0g": inputs["L0_ffn_gamma"], "l0b": inputs["L0_ffn_beta"]}
    base = {k: np.ascontiguousarray(v) for k, v in base.items()}
    in_maps = [dict(base, x=np.ascontiguousarray(xs[c])) for c in range(NC)]
    nc = build_A()
    res = run_bass_kernel_spmd(nc, in_maps, core_ids=list(range(NC)))
    return res.results


NCH = 64
DV = 256


def build_B():
    nc = bass.Bass("TRN2", target_bir_lowering=False, debug=False)
    io = {}
    io["qfm"] = nc.dram_tensor("qfm", [D, L], BF16, kind="ExternalInput").ap()
    io["kbar"] = nc.dram_tensor("kbar", [D, L], BF16, kind="ExternalInput").ap()
    io["ktok"] = nc.dram_tensor("ktok", [L, D], BF16, kind="ExternalInput").ap()
    io["bv"] = nc.dram_tensor("bv", [L, DV], BF16, kind="ExternalInput").ap()
    io["beta"] = nc.dram_tensor("beta", [L, 1], F32, kind="ExternalInput").ap()
    io["Tt"] = nc.dram_tensor("Tt", [NCH, 128, 128], BF16, kind="ExternalInput").ap()
    io["KQu"] = nc.dram_tensor("KQu", [NCH, 128, 128], BF16, kind="ExternalInput").ap()
    io["o"] = nc.dram_tensor("o", [L, DV], F32, kind="ExternalOutput").ap()
    with ExitStack() as stack:
        P = Prog(nc, stack)
        emit_B(P, io)
        P.finish()
    return nc


def emit_B(P, io):
    Y, G, T, V, S = P.Y, P.G, P.T, P.V, P.S
    q_c = P.sb("b_q", [128, 2, ND, 128], BF16)
    kb_c = P.sb("b_kb", [128, 2, ND, 128], BF16)
    kt_c = P.sb("b_kt", [128, 2, D], BF16)
    bv_c = P.sb("b_bv", [128, 2, DV], BF16)
    be_c = P.sb("b_be", [128, 2, 1], F32)
    Tt_c = P.sb("b_Tt", [128, 2, 128], BF16)
    KQ_c = P.sb("b_KQ", [128, 2, 128], BF16)
    S_f = P.sb("b_Sf", [128, ND, DV], F32)
    S_b = P.sb("b_Sb", [128, ND, DV], BF16)
    U_sb = P.sb("b_U", [128, DV], BF16)
    R_sb = P.sb("b_R", [128, DV], BF16)
    negb = P.sb("b_negb", [128, 2, 1], F32)
    o_t = P.sb("b_ot", [128, 2, DV], F32)

    pW = P.ps("b_pW", [128, DV])
    pU = P.ps("b_pU", [128, DV])
    pO = P.ps("b_pO", [128, DV])
    pS = P.ps("b_pS", [128, ND, DV])

    t_z1 = G.op(lambda e: e.memset(S_f[:], 0.0))
    t_z2 = G.op(lambda e: e.memset(S_b[:], 0.0))

    fused = "xr" in io

    def issue_loads(ci):
        b = ci % 2
        blk_ = ci // 8
        w_ = ci % 8
        toks = []
        if not fused:
            toks.append(Y.dma(q_c[:, b, :, :],
                              io["qfm"][:, 128 * ci:128 * (ci + 1)]
                              .rearrange("(r p) t -> p r t", p=128)))
            toks.append(Y.dma(kb_c[:, b, :, :],
                              io["kbar"][:, 128 * ci:128 * (ci + 1)]
                              .rearrange("(r p) t -> p r t", p=128)))
            toks.append(Y.dma(Tt_c[:, b, :], io["Tt"][ci, :, :]))
            toks.append(Y.dma(KQ_c[:, b, :], io["KQu"][ci, :, :]))
            toks.append(G.dma(kt_c[:, b, :], io["ktok"][128 * ci:128 * (ci + 1), :]))
            toks.append(G.dma(bv_c[:, b, :], io["bv"][128 * ci:128 * (ci + 1), :]))
            toks.append(G.dma(be_c[:, b, :], io["beta"][128 * ci:128 * (ci + 1), :]))
        else:
            xr = io["xr"]
            rtok = 1024 * blk_ + 128 * w_
            toks.append(Y.dma(q_c[:, b, :, :],
                              xr[1024 * blk_:1024 * (blk_ + 1), 128 * w_:128 * (w_ + 1)]
                              .rearrange("(r p) t -> p r t", p=128)))
            toks.append(Y.dma(kb_c[:, b, :, :],
                              xr[1024 * blk_:1024 * (blk_ + 1),
                                 1024 + 128 * w_:1024 + 128 * (w_ + 1)]
                              .rearrange("(r p) t -> p r t", p=128)))
            toks.append(Y.dma(Tt_c[:, b, :], xr[rtok:rtok + 128, 3328:3456]))
            toks.append(Y.dma(KQ_c[:, b, :], xr[rtok:rtok + 128, 3456:3584]))
            toks.append(G.dma(kt_c[:, b, :], xr[rtok:rtok + 128, 2048:3072]))
            toks.append(G.dma(bv_c[:, b, :], xr[rtok:rtok + 128, 3072:3328]))
            toks.append(G.dma(be_c[:, b, :], io["ber"][rtok:rtok + 128, :]))
        return toks

    cons_tok = [[], []]
    odma_tok = [None, None]
    t_loads = {0: issue_loads(0)}
    cp_tok = [t_z2] * 4
    t_oev = None
    t_Rlast = None
    for ci in range(NCH):
        b = ci % 2
        if ci + 1 < NCH:
            for tok in cons_tok[(ci + 1) % 2]:
                P.Y.wait(tok)
                P.G.wait(tok)
            t_loads[ci + 1] = issue_loads(ci + 1)
        tl = t_loads.pop(ci)
        T.wait(tl[0])
        T.wait(tl[1])
        if t_oev is not None:
            T.wait(t_oev)
        for r in range(ND):
            if r % 2 == 0:
                T.wait(cp_tok[r // 2])
            t_W = T.op(lambda e, r=r, b=b: e.matmul(
                pW[:], kb_c[:, b, r, :], S_b[:, r, :],
                start=(r == 0), stop=(r == ND - 1)), inc=(1 if r == ND - 1 else 0))
        for r in range(ND):
            t_O1 = T.op(lambda e, r=r, b=b: e.matmul(
                pO[:], q_c[:, b, r, :], S_b[:, r, :],
                start=(r == 0), stop=False), inc=(1 if r == ND - 1 else 0))
        V.wait(tl[6])
        tv = V.op(lambda e, b=b: e.tensor_scalar(negb[:, b, :], be_c[:, b, :],
                                                 -1.0, None, ALU.mult))
        V.wait(tv)
        V.wait(t_W)
        V.wait(tl[5])
        t_R = V.op(lambda e, b=b: e.scalar_tensor_tensor(
            R_sb[:], pW[:], negb[:, b, 0:1], bv_c[:, b, :], ALU.mult, ALU.add))
        t_Rlast = t_R
        T.wait(t_R)
        T.wait(tl[2])
        t_Umm = T.op(lambda e, b=b: e.matmul(pU[:], Tt_c[:, b, :], R_sb[:],
                                             start=True, stop=True))
        S.wait(t_Umm)
        t_Uev = S.op(lambda e: e.copy(U_sb[:], pU[:]))
        T.wait(t_Uev)
        T.wait(tl[3])
        t_O2 = T.op(lambda e, b=b: e.matmul(pO[:], KQ_c[:, b, :], U_sb[:],
                                            start=False, stop=True))
        T.wait(tl[4])
        t_dSp = []
        for r in range(ND):
            t_d = T.op(lambda e, r=r, b=b: e.matmul(
                pS[:, r, :], kt_c[:, b, 128 * r:128 * (r + 1)], U_sb[:],
                start=True, stop=True), inc=(1 if r % 2 == 1 else 0))
            if r % 2 == 1:
                t_dSp.append(t_d)
        for j in range(4):
            V.wait(t_dSp[j])
            if ci == 0:
                V.wait(t_z1)
            t_up = V.op(lambda e, j=j: e.scalar_tensor_tensor(
                S_f[:, 2 * j:2 * j + 2, :], pS[:, 2 * j:2 * j + 2, :], 1.0,
                S_f[:, 2 * j:2 * j + 2, :], ALU.mult, ALU.add))
            S.wait(t_up)
            S.wait(t_O1)
            cp_tok[j] = S.op(lambda e, j=j: e.copy(
                S_b[:, 2 * j:2 * j + 2, :], S_f[:, 2 * j:2 * j + 2, :]))
        S.wait(t_O2)
        if odma_tok[b] is not None:
            S.wait(odma_tok[b])
        t_oev = S.op(lambda e, b=b: e.copy(o_t[:, b, :], pO[:]))
        G.wait(t_oev)
        odma_tok[b] = G.dma(io["o"][128 * ci:128 * (ci + 1), :], o_t[:, b, :])
        cons_tok[b] = [(T.sem, T.n), (V.sem, V.n)]


NE = 8
NF0 = 32
TH = 1024


def build_C():
    nc = bass.Bass("TRN2", target_bir_lowering=False, debug=False)
    io = {}
    io["xn2fm"] = nc.dram_tensor("xn2fm", [D, TPC], BF16, kind="ExternalInput").ap()
    io["w"] = nc.dram_tensor("w", [TPC, NE], F32, kind="ExternalInput").ap()
    io["W1"] = nc.dram_tensor("W1", [NE, D, 4096], F32, kind="ExternalInput").ap()
    io["b1"] = nc.dram_tensor("b1", [NE, 4096], F32, kind="ExternalInput").ap()
    io["W2"] = nc.dram_tensor("W2", [NE, 4096, D], F32, kind="ExternalInput").ap()
    io["b2"] = nc.dram_tensor("b2", [NE, D], F32, kind="ExternalInput").ap()
    io["moe0"] = nc.dram_tensor("moe0", [TPC, D], F32, kind="ExternalOutput").ap()
    with ExitStack() as stack:
        P = Prog(nc, stack)
        emit_C(P, io)
        P.finish()
    return nc


def emit_C(P, io):
    Y, G, T, V, S = P.Y, P.G, P.T, P.V, P.S
    NFC = 32
    xn2 = P.sb("c_xn2", [128, ND, TPC], BF16)
    Hbuf = P.sb("c_H", [128, NFC, TH], BF16)
    W1b = [P.sb(f"c_W1{i}", [128, ND, 512], BF16) for i in range(2)]
    W2b = [P.sb(f"c_W2{i}", [128, 16, 512], BF16) for i in range(2)]
    Facc = P.sb("c_Facc", [128, ND, D], F32)
    ones8 = P.sb("c_o8", [8, 1], F32)
    onesr = P.sb("c_or", [1, 8], F32)
    ident_f = P.sb("c_idf", [128, 128], F32)
    w_all = P.sb("c_w", [128, NT, NE], F32)
    wTb = P.sb("c_wTb", [8, TPC], BF16)
    b1pad = P.sb("c_b1p", [128, 128], F32)
    b1col = P.sb("c_b1c", [128, NE, NFC], F32)
    rb1 = P.sb("c_rb1", [128, NE, NFC], BF16)
    b2rows = P.sb("c_b2r", [8, D], F32)
    CC8 = P.sb("c_CC8", [8, D], F32)
    CC8b = P.sb("c_CC8b", [8, D], F32)
    ctcrow = P.sb("c_ctc", [1, D], F32)
    ccstg = P.sb("c_ccstg", [1, D], F32)
    CCbf = P.sb("c_CCbf", [8, D], BF16)

    pH = [P.ps(f"c_pH{i}", [128, 512]) for i in range(4)]
    pF = [P.ps(f"c_pF{i}", [128, 512]) for i in range(4)]

    tz = G.op(lambda e: e.memset(ident_f[:], 0.0))
    G.wait(tz)
    G.op(lambda e: e.affine_select(out=ident_f[:], in_=ident_f[:],
                                   compare_op=ALU.not_equal, fill=1.0,
                                   base=0, pattern=[[-1, 128]], channel_multiplier=1))
    G.op(lambda e: e.memset(ones8[:], 1.0))
    G.op(lambda e: e.memset(onesr[:], 1.0))
    G.op(lambda e: e.memset(b1pad[:], 0.0))
    for r in range(ND):
        Y.dma(xn2[:, r, :], io["xn2fm"][128 * r:128 * (r + 1), :])
    Y.dma(w_all[:], io["w"].rearrange("(i p) e -> p i e", p=128))
    Y.dma(b2rows[:], io["b2"][:, :])
    P.barrier()

    t_pTe = None
    for i in range(NT):
        if t_pTe is not None:
            T.wait(t_pTe)
        t_tr = T.op(lambda e, i=i: e.transpose(pH[0][0:NE, 0:128], w_all[:, i, :],
                                               ident_f[:]))
        V.wait(t_tr)
        t_pTe = V.op(lambda e, i=i: e.tensor_copy(wTb[:, 128 * i:128 * (i + 1)],
                                                  pH[0][0:NE, 0:128]))
    for e8 in range(NE):
        t_l = Y.dma(b1pad[0:NFC, :], io["b1"][e8].rearrange("(c p) -> c p", p=128))
        T.wait(t_l)
        T.wait(t_pTe)
        t_tr = T.op(lambda e: e.transpose(pH[0][:, 0:128], b1pad[:], ident_f[:]))
        V.wait(t_tr)
        t1 = V.op(lambda e, e8=e8: e.tensor_copy(b1col[:, e8, :], pH[0][:, 0:NFC]))
        V.wait(t1)
        t_pTe = V.op(lambda e, e8=e8: e.tensor_scalar(rb1[:, e8, :], b1col[:, e8, :],
                                                      0.0, None, ALU.max))
        Y.wait(t_pTe)
    P.barrier()

    t_pHev = [None] * 4
    t_pFev = [None] * 4
    t_w1 = [None, None]
    t_w2 = [None, None]
    t_ccev = None
    t_ccG = None
    t_out = None
    for th in range(2):
        t_fz = None
        for e8 in range(NE):
            t_l2 = [None, None]

            def w2load(dh, fch, th=th, e8=e8):
                bw = fch
                if t_w2[bw] is not None:
                    G.wait(t_w2[bw])
                return G.dma(
                    W2b[bw][:],
                    io["W2"][e8][2048 * fch:2048 * (fch + 1),
                                 512 * dh:512 * (dh + 1)]
                    .rearrange("(fc p) d -> p fc d", p=128))
            if t_ccev is not None:
                T.wait(t_ccev)
            gidx = 0
            for quar in range(8):
                bq = quar % 2
                if t_w1[bq] is not None:
                    G.wait(t_w1[bq])
                t_l = G.dma(W1b[bq][:],
                            io["W1"][e8, :, 512 * quar:512 * (quar + 1)]
                            .rearrange("(r p) f -> p r f", p=128))
                if quar == 0:
                    t_l2[0] = w2load(0, 0)
                    t_l2[1] = w2load(0, 1)
                for fl in range(4):
                    f = 4 * quar + fl
                    for tt2 in range(2):
                        bh = gidx % 4
                        gidx += 1
                        if t_pHev[bh] is not None:
                            T.wait(t_pHev[bh])
                        T.wait(t_l)
                        for r in range(ND):
                            t_mm = T.op(lambda e, r=r, fl=fl, tt2=tt2, th=th, bq=bq,
                                        bh=bh: e.matmul(
                                pH[bh][:], W1b[bq][:, r, 128 * fl:128 * (fl + 1)],
                                xn2[:, r, 1024 * th + 512 * tt2:
                                    1024 * th + 512 * (tt2 + 1)],
                                start=(r == 0), stop=(r == ND - 1)),
                                inc=(1 if r == ND - 1 else 0))
                        S.wait(t_mm)
                        ev = S.op(lambda e, f=f, tt2=tt2, e8=e8, bh=bh: e.activation(
                            Hbuf[:, f, 512 * tt2:512 * (tt2 + 1)], pH[bh][:],
                            AF.Relu, bias=b1col[:, e8, f:f + 1]))
                        t_pHev[bh] = ev
                t_w1[bq] = (T.sem, T.n)
            if e8 == 0:
                if t_out is not None:
                    G.wait(t_out)
                t_fz = G.op(lambda e: e.memset(Facc[:], 0.0))
            S_H_done = (S.sem, S.n)
            fidx = 0
            for dh in range(2):
                for fch in range(2):
                    bw = fch
                    T.wait(t_l2[bw])
                    T.wait(S_H_done)
                    for tc in range(ND):
                        bb = fidx % 4
                        fidx += 1
                        if t_pFev[bb] is not None:
                            T.wait(t_pFev[bb])
                        for fc in range(16):
                            t_mm = T.op(lambda e, fc=fc, fch=fch, tc=tc, dh=dh, bw=bw,
                                        th=th, bb=bb: e.matmul(
                                pF[bb][:],
                                Hbuf[:, 16 * fch + fc, 128 * tc:128 * (tc + 1)],
                                W2b[bw][:, fc, :],
                                start=(fc == 0), stop=(fc == 15)),
                                inc=(1 if fc == 15 else 0))
                        V.wait(t_mm)
                        V.wait(t_fz)
                        tv = V.op(lambda e, tc=tc, dh=dh, th=th, e8=e8, bb=bb:
                                  e.scalar_tensor_tensor(
                            Facc[:, tc, 512 * dh:512 * (dh + 1)], pF[bb][:],
                            w_all[:, 8 * th + tc, e8:e8 + 1],
                            Facc[:, tc, 512 * dh:512 * (dh + 1)],
                            ALU.mult, ALU.add))
                        t_pFev[bb] = tv
                    if th == 0:
                        if fch == 0:
                            T.wait(t_pHev[0])
                            if t_ccev is not None:
                                T.wait(t_ccev)
                        for fc in range(16):
                            t_ccmm = T.op(lambda e, fc=fc, fch=fch, e8=e8, bw=bw:
                                          e.matmul(
                                pH[0][0:1, :], rb1[:, e8, 16 * fch + fc:
                                                   16 * fch + fc + 1],
                                W2b[bw][:, fc, :],
                                start=(fch == 0 and fc == 0),
                                stop=(fch == 1 and fc == 15)),
                                inc=(1 if (fch == 1 and fc == 15) else 0))
                    t_w2[bw] = (T.sem, T.n)
                    if dh == 0:
                        t_l2[fch] = w2load(1, fch)
                if th == 0:
                    S.wait(t_ccmm)
                    if t_ccG is not None:
                        S.wait(t_ccG)
                    t_ccev = S.op(lambda e, dh=dh: e.copy(
                        ccstg[0:1, 512 * dh:512 * (dh + 1)], pH[0][0:1, :]))
                    G.wait(t_ccev)
                    t_ccG = G.dma(CC8[e8:e8 + 1, 512 * dh:512 * (dh + 1)],
                                  ccstg[0:1, 512 * dh:512 * (dh + 1)])
        if th == 0:
            V.wait(t_ccG)
            T.wait(t_pHev[1])
            T.wait(t_pHev[2])
            tv = V.op(lambda e: e.tensor_tensor(CC8b[:], CC8[:], b2rows[:], ALU.add))
            for dhh in range(2):
                T.wait(tv)
                t_tr = T.op(lambda e, dhh=dhh: e.matmul(
                    pH[1][0:1, :], ones8[:], CC8b[:, 512 * dhh:512 * (dhh + 1)],
                    start=True, stop=True))
                S.wait(t_tr)
                t_ct = S.op(lambda e, dhh=dhh: e.copy(
                    ctcrow[:, 512 * dhh:512 * (dhh + 1)], pH[1][0:1, :]))
                T.wait(t_ct)
                t_bc = T.op(lambda e, dhh=dhh: e.matmul(
                    pH[2][0:NE, :], onesr[:], ctcrow[:, 512 * dhh:512 * (dhh + 1)],
                    start=True, stop=True))
                V.wait(t_bc)
                tv2 = V.op(lambda e, dhh=dhh: e.tensor_tensor(
                    CCbf[:, 512 * dhh:512 * (dhh + 1)], pH[2][0:NE, :],
                    CC8[:, 512 * dhh:512 * (dhh + 1)], ALU.subtract))
                T.wait(tv2)
        for tc in range(ND):
            for dh in range(2):
                bb = (tc * 2 + dh) % 4
                if t_pFev[bb] is not None:
                    T.wait(t_pFev[bb])
                t_mm = T.op(lambda e, tc=tc, dh=dh, th=th: e.matmul(
                    pF[(tc * 2 + dh) % 4][:],
                    wTb[:, 1024 * th + 128 * tc:1024 * th + 128 * (tc + 1)],
                    CCbf[:, 512 * dh:512 * (dh + 1)], start=True, stop=True))
                V.wait(t_mm)
                tv = V.op(lambda e, tc=tc, dh=dh, bb=bb: e.scalar_tensor_tensor(
                    Facc[:, tc, 512 * dh:512 * (dh + 1)], pF[bb][:], 1.0,
                    Facc[:, tc, 512 * dh:512 * (dh + 1)], ALU.mult, ALU.add))
                t_pFev[bb] = tv
            G.wait(tv)
            t_out = G.dma(io["moe0"][1024 * th + 128 * tc:1024 * th + 128 * (tc + 1), :],
                          Facc[:, tc, :])
            V.wait(t_out)
        if th == 1:
            P.barrier()


def build_D():
    nc = bass.Bass("TRN2", target_bir_lowering=False, debug=False)
    io = {}
    for nm, shape in [("x", [TPC, D]), ("stats", [TPC, 2]), ("o", [TPC, D]),
                      ("moe0", [TPC, D]), ("lng", [D]), ("lnb", [D]),
                      ("Wq1", [128, D]), ("bq1", [128]), ("Wk1", [128, D]),
                      ("bk1", [128]), ("Wv1", [128, D]), ("bv1", [128]),
                      ("Wg", [D, 8])]:
        io[nm] = nc.dram_tensor(nm, shape, F32, kind="ExternalInput").ap()
    for nm, shape, dt in [("x1", [TPC, D], F32), ("xn1fm", [D, TPC], BF16),
                          ("Qfm", [128, TPC], BF16), ("Kfm", [128, TPC], BF16),
                          ("Vtm", [TPC, 128], BF16), ("qn", [1, TPC], F32),
                          ("kn", [1, TPC], F32), ("w1", [TPC, 8], F32)]:
        io[nm] = nc.dram_tensor(nm, shape, dt, kind="ExternalOutput").ap()
    with ExitStack() as stack:
        P = Prog(nc, stack)
        emit_D(P, io)
        P.finish()
    return nc


def emit_D(P, io):
    Y, G, T, V, S = P.Y, P.G, P.T, P.V, P.S
    NTc = NT
    o_all = P.sb("d_oall", [128, NTc, D], F32)
    x1a = P.sb("d_x1a", [128, 2, D], F32)
    x_t = P.sb("d_x", [128, 2, D], F32)
    m_t = P.sb("d_m", [128, 2, D], F32)
    sqs = P.sb("d_sqs", [128, D], F32)
    stats = P.sb("d_stats", [128, NT, 2], F32)
    rstd0 = P.sb("d_rstd0", [128, NT], F32)
    mu0 = P.sb("d_mu0", [128, NT], F32)
    so_a = P.sb("d_soa", [128, NTc], F32)
    sq_a = P.sb("d_sqa", [128, NTc], F32)
    muo_a = P.sb("d_muoa", [128, NTc], F32)
    rstdo_a = P.sb("d_rstdoa", [128, NTc], F32)
    sx1_a = P.sb("d_sx1a", [128, NTc], F32)
    sq1_a = P.sb("d_sq1a", [128, NTc], F32)
    mu1_a = P.sb("d_mu1a", [128, NTc], F32)
    rstd1_a = P.sb("d_rstd1a", [128, NTc], F32)
    ta_a = P.sb("d_taa", [128, NTc], F32)
    tb_a = P.sb("d_tba", [128, NTc], F32)
    s0f = P.sb("d_s0f", [128, D], F32)
    s0g = P.sb("d_s0g", [128, D], F32)
    xn0f = P.sb("d_xn0f", [128, D], F32)
    g0f = P.sb("d_g0f", [128, D], F32)
    xn1_tm = P.sb("d_xn1tm", [128, D], BF16)
    xn1f = P.sb("d_xn1f", [128, D], F32)
    xn1_fm = P.sb("d_xn1fm", [128, ND, TPC], BF16)
    x2f_st = P.sb("d_x2fst", [128, ND, 128], F32)
    gbc = P.sb("d_gbc", [128, D], F32)
    bbc = P.sb("d_bbc", [128, D], F32)
    grow = P.sb("d_grow", [1, D], F32)
    brow = P.sb("d_brow", [1, D], F32)
    Wpad = P.sb("d_Wpad", [128, D], BF16)
    WqT = P.sb("d_WqT", [128, ND, 128], BF16)
    WkT = P.sb("d_WkT", [128, ND, 128], BF16)
    WvT = P.sb("d_WvT", [128, ND, 128], BF16)
    bq_r = P.sb("d_bqr", [1, 128], F32)
    bk_r = P.sb("d_bkr", [1, 128], F32)
    bv_r = P.sb("d_bvr", [1, 128], F32)
    Wgf = P.sb("d_Wgf", [128, ND, 8], F32)
    ident = P.sb("d_id", [128, 128], BF16)
    ident_f = P.sb("d_idf", [128, 128], F32)
    ones_f1 = P.sb("d_o1", [1, 512], F32)
    ones_bf = P.sb("d_obf", [128, 1], BF16)
    eps_c = P.sb("d_eps", [128, 1], F32)
    q_st = P.sb("d_qst", [128, 512], BF16)
    qsq = P.sb("d_qsq", [128, 512], BF16)
    qn_sb = P.sb("d_qn", [1, TPC], F32)
    kn_sb = P.sb("d_kn", [1, TPC], F32)
    v_st = P.sb("d_vst", [128, 128], BF16)
    probs = P.sb("d_probs", [128, 8], F32)
    Px = P.sb("d_Px", [128, 8], F32)
    msk = P.sb("d_msk", [128, 8], F32)
    m1 = P.sb("d_m1", [128, 1], F32)
    m2 = P.sb("d_m2", [128, 1], F32)
    nm_ = P.sb("d_nm", [128, 1], F32)
    rs_ = P.sb("d_rs", [128, 1], F32)
    tmpc = P.sb("d_tmpc", [128, 1], F32)
    w_t = P.sb("d_wt", [128, 8], F32)

    ppA = P.ps("d_ppA", [128, 512])
    ppB = P.ps("d_ppB", [128, 512])
    ptA = P.ps("d_ptA", [128, 128], BF16)
    ptB = P.ps("d_ptB", [128, 128], BF16)
    pkA = P.ps("d_pkA", [128, 128])
    pkB = P.ps("d_pkB", [128, 128])
    prow = P.ps("d_prow", [1, 512])

    make_ident(P, ident[:])
    tz = G.op(lambda e: e.memset(ident_f[:], 0.0))
    G.wait(tz)
    G.op(lambda e: e.affine_select(out=ident_f[:], in_=ident_f[:],
                                   compare_op=ALU.not_equal, fill=1.0,
                                   base=0, pattern=[[-1, 128]], channel_multiplier=1))
    G.op(lambda e: e.memset(ones_f1[:], 1.0))
    G.op(lambda e: e.memset(ones_bf[:], 1.0))
    G.op(lambda e: e.memset(eps_c[:], LN_EPS))
    Y.dma(grow[:], io["lng"][None, :])
    Y.dma(brow[:], io["lnb"][None, :])
    Y.dma(stats[:], io["stats"].rearrange("(i p) s -> p i s", p=128))
    Y.dma(bq_r[:], io["bq1"][None, :])
    Y.dma(bk_r[:], io["bk1"][None, :])
    Y.dma(bv_r[:], io["bv1"][None, :])
    for r in range(ND):
        Y.dma(Wgf[:, r, :], io["Wg"][128 * r:128 * (r + 1), :])
    P.barrier()
    for h in range(2):
        T.op(lambda e, h=h: e.matmul(ppA[:], ones_f1[0:1, 0:128],
                                     grow[:, 512 * h:512 * (h + 1)], start=True, stop=True))
        T.op(lambda e, h=h: e.matmul(ppB[:], ones_f1[0:1, 0:128],
                                     brow[:, 512 * h:512 * (h + 1)], start=True, stop=True))
        P.barrier([T, S])
        S.op(lambda e, h=h: e.copy(gbc[:, 512 * h:512 * (h + 1)], ppA[:]))
        S.op(lambda e, h=h: e.copy(bbc[:, 512 * h:512 * (h + 1)], ppB[:]))
        P.barrier([T, S])
    tv = V.op(lambda e: e.tensor_copy(mu0[:], stats[:, :, 0]))
    V.wait(tv)
    tv = V.op(lambda e: e.tensor_scalar(rstd0[:], stats[:, :, 1], LN_EPS, None, ALU.add))
    V.wait(tv)
    S.wait(tv)
    ts2 = S.op(lambda e: e.activation(rstd0[:], rstd0[:], AF.Sqrt))
    V.wait(ts2)
    tv = V.op(lambda e: e.reciprocal(rstd0[:], rstd0[:]))
    P.barrier()
    for (wname, dst) in [("Wq1", WqT), ("Wk1", WkT), ("Wv1", WvT)]:
        t_l = G.dma(Wpad[:], io[wname])
        T.wait(t_l)
        for r in range(ND):
            b = r % 2
            pt = ptA if b == 0 else ptB
            tmm = T.op(lambda e, pt=pt, r=r: e.transpose(
                pt[:], Wpad[:, 128 * r:128 * (r + 1)], ident[:]))
            V.wait(tmm)
            ev = V.op(lambda e, pt=pt, dst=dst, r=r: e.tensor_copy(dst[:, r, :], pt[:]))
            T.wait(ev)
        G.wait(ev)
    P.barrier()

    trk = {"b": {"i": 0, "ev": [None, None]}, "f": {"i": 0, "ev": [None, None]}}

    def transpose_to(dst_ap, src_ap, src_tok, idb):
        tr = trk[idb]
        b = tr["i"] % 2
        tr["i"] += 1
        pt = (ptA if b == 0 else ptB) if idb == "b" else (pkA if b == 0 else pkB)
        T.wait(src_tok)
        if tr["ev"][b] is not None:
            T.wait(tr["ev"][b])
        idt = ident if idb == "b" else ident_f
        tmm = T.op(lambda e, pt=pt, s=src_ap, idt=idt: e.transpose(pt[:], s, idt[:]))
        if b == 0:
            V.wait(tmm)
            ev = V.op(lambda e, pt=pt, d=dst_ap: e.tensor_copy(d, pt[:]))
        else:
            S.wait(tmm)
            ev = S.op(lambda e, pt=pt, d=dst_ap: e.copy(d, pt[:]))
        tr["ev"][b] = ev
        return ev

    def batched_stats(sx, sq, mu_o, rstd_o):
        c1 = 1.0 / D
        tv1 = V.op(lambda e: e.tensor_scalar(mu_o[:], sx[:], c1, None, ALU.mult))
        V.wait(tv1)
        V.op(lambda e: e.tensor_scalar(ta_a[:], sq[:], c1, None, ALU.mult), inc=0)
        t3 = V.op(lambda e: e.tensor_tensor(tb_a[:], mu_o[:], mu_o[:], ALU.mult))
        V.wait(t3)
        tv2 = V.op(lambda e: e.tensor_tensor(ta_a[:], ta_a[:], tb_a[:], ALU.subtract))
        S.wait(tv2)
        ts = S.op(lambda e: e.activation(tb_a[:], ta_a[:], AF.Sqrt, bias=eps_c[:, 0:1]))
        V.wait(ts)
        return V.op(lambda e: e.reciprocal(rstd_o[:], tb_a[:]))

    t_sqp = None
    for i in range(NT):
        if "orecv" in io:
            si_ = 0 if i < 8 else 1
            w_ = i % 8
            for r2 in range(4):
                t_lo = Y.dma(o_all[:, i, 256 * r2:256 * (r2 + 1)],
                             io["orecv"][1024 * (4 * si_ + r2) + 128 * w_:
                                         1024 * (4 * si_ + r2) + 128 * (w_ + 1), :])
        else:
            t_lo = Y.dma(o_all[:, i, :], io["o"][128 * i:128 * (i + 1), :])
        V.wait(t_lo)
        t_red = V.op(lambda e, i=i: e.reduce_sum(so_a[:, i:i + 1], o_all[:, i, :],
                                                 axis=AX.X))
        S.wait(t_lo)
        if t_sqp is not None:
            S.wait(t_sqp)
        t_sqp = S.op(lambda e, i=i: e.activation(sqs[:], o_all[:, i, :], AF.Square,
                                                 accum_out=sq_a[:, i:i + 1]))
    S_red = (S.sem, S.n)
    V.wait(S_red)
    V.wait(t_red)
    t_rstdo = batched_stats(so_a, sq_a, muo_a, rstdo_a)
    V.wait(t_rstdo)

    t_buf = [None, None]
    t_x1p = None
    for i in range(NT):
        b = i % 2
        if t_buf[b] is not None:
            for tk in t_buf[b]:
                Y.wait(tk)
            V.wait(t_buf[b][2])
            V.wait(t_buf[b][1])
        t_lx = Y.dma(x_t[:, b, :], io["x"][128 * i:128 * (i + 1), :])
        t_lm = Y.dma(m_t[:, b, :], io["moe0"][128 * i:128 * (i + 1), :])
        va = V.op(lambda e, i=i: e.tensor_scalar(
            s0f[:], o_all[:, i, :], muo_a[:, i:i + 1], rstdo_a[:, i:i + 1],
            ALU.subtract, ALU.mult))
        V.wait(va)
        vb = V.op(lambda e: e.tensor_tensor(s0g[:], s0f[:], gbc[:], ALU.mult))
        V.wait(vb)
        vc = V.op(lambda e: e.tensor_tensor(s0g[:], s0g[:], bbc[:], ALU.add))
        V.wait(t_lx)
        vd = V.op(lambda e, b=b, i=i: e.tensor_scalar(
            xn0f[:], x_t[:, b, :], mu0[:, i:i + 1], rstd0[:, i:i + 1],
            ALU.subtract, ALU.mult))
        V.wait(vd)
        V.wait(t_lm)
        t = V.op(lambda e, b=b: e.tensor_tensor(xn0f[:], xn0f[:], m_t[:, b, :],
                                                ALU.add))
        S.wait(t)
        if t_x1p is not None:
            S.wait(t_x1p)
        t_g0 = S.op(lambda e: e.activation(g0f[:], xn0f[:], AF.Silu))
        V.wait(t_g0)
        V.wait(vc)
        ve = V.op(lambda e: e.tensor_tensor(s0g[:], s0g[:], g0f[:], ALU.mult))
        V.wait(ve)
        t_x1 = V.op(lambda e, b=b, i=i: e.tensor_tensor(x1a[:, b, :], x_t[:, b, :],
                                                        s0g[:], ALU.add))
        t_x1p = t_x1
        G.wait(t_x1)
        t_x1d = G.dma(io["x1"][128 * i:128 * (i + 1), :], x1a[:, b, :])
        V.wait(t_x1)
        t_red2 = V.op(lambda e, i=i, b=b: e.reduce_sum(sx1_a[:, i:i + 1], x1a[:, b, :],
                                                       axis=AX.X))
        S.wait(t_x1)
        if t_sqp is not None:
            S.wait(t_sqp)
        t_sq1 = S.op(lambda e, i=i, b=b: e.activation(
            sqs[:], x1a[:, b, :], AF.Square, accum_out=sq1_a[:, i:i + 1]))
        t_sqp = t_sq1
        t_buf[b] = [t_x1, t_x1d, t_sq1]
    S_red2 = (S.sem, S.n)
    V.wait(S_red2)
    V.wait(t_red2)
    t_rstd1 = batched_stats(sx1_a, sq1_a, mu1_a, rstd1_a)
    V.wait(t_rstd1)

    Y.wait(G.dtok())
    Y.wait((V.sem, V.n))
    t_b3 = [None, None]
    for i in range(NT):
        b = i % 2
        if t_b3[b] is not None:
            Y.wait(t_b3[b])
        t_l1 = Y.dma(x_t[:, b, :], io["x1"][128 * i:128 * (i + 1), :])
        V.wait(t_l1)
        t_n1 = V.op(lambda e, i=i, b=b: e.tensor_scalar(
            xn1_tm[:], x_t[:, b, :], mu1_a[:, i:i + 1], rstd1_a[:, i:i + 1],
            ALU.subtract, ALU.mult))
        V.wait(t_n1)
        t_n1f = V.op(lambda e, i=i, b=b: e.tensor_scalar(
            xn1f[:], x_t[:, b, :], mu1_a[:, i:i + 1], rstd1_a[:, i:i + 1],
            ALU.subtract, ALU.mult))
        t_b3[b] = t_n1f
        for r in range(ND):
            ev = transpose_to(xn1_fm[:, r, 128 * i:128 * (i + 1)],
                              xn1_tm[:, 128 * r:128 * (r + 1)], t_n1, "b")
        evf = None
        for r in range(ND):
            evf = transpose_to(x2f_st[:, r, :], xn1f[:, 128 * r:128 * (r + 1)],
                               t_n1f, "f")
        T.wait(trk["f"]["ev"][0])
        T.wait(trk["f"]["ev"][1])
        for r in range(ND):
            T.op(lambda e, r=r: e.matmul(ppA[:, 0:8], x2f_st[:, r, :], Wgf[:, r, :],
                                         start=(r == 0), stop=(r == ND - 1)),
                 inc=(1 if r == ND - 1 else 0))
        t_mm8 = (T.sem, T.n)
        V.wait(t_mm8)

        def vch(fn):
            t = V.op(fn)
            V.wait(t)
            return t
        vch(lambda e: e.reduce_max(m1[:], ppA[:, 0:8], axis=AX.X))
        vch(lambda e: e.tensor_scalar(nm_[:], m1[:], -1.0, None, ALU.mult))
        tpx = vch(lambda e: e.tensor_copy(Px[:], ppA[:, 0:8]))
        S.wait(tpx)
        tpx = S.op(lambda e: e.activation(Px[:], Px[:], AF.Exp, bias=nm_[:, 0:1]))
        V.wait(tpx)
        vch(lambda e: e.reduce_sum(m2[:], Px[:], axis=AX.X))
        vch(lambda e: e.reciprocal(rs_[:], m2[:]))
        vch(lambda e: e.tensor_scalar(probs[:], Px[:], rs_[:, 0:1], None, ALU.mult))
        vch(lambda e: e.reduce_max(m1[:], probs[:], axis=AX.X))
        vch(lambda e: e.tensor_scalar(msk[:], probs[:], m1[:, 0:1], None, ALU.is_equal))
        vch(lambda e: e.scalar_tensor_tensor(Px[:], msk[:], -1e30, probs[:],
                                             ALU.mult, ALU.add))
        vch(lambda e: e.reduce_max(m2[:], Px[:], axis=AX.X))
        vch(lambda e: e.tensor_scalar(msk[:], probs[:], m2[:, 0:1], None, ALU.is_ge))
        vch(lambda e: e.tensor_tensor(w_t[:], probs[:], msk[:], ALU.mult))
        vch(lambda e: e.tensor_tensor(tmpc[:], m1[:], m2[:], ALU.add))
        vch(lambda e: e.reciprocal(tmpc[:], tmpc[:]))
        t_w = vch(lambda e: e.tensor_scalar(w_t[:], w_t[:], tmpc[:, 0:1], None, ALU.mult))
        G.wait(t_w)
        t_wd = G.dma(io["w1"][128 * i:128 * (i + 1), :], w_t[:])
        V.wait(t_wd)
        T.wait((V.sem, V.n))
        T.wait(t_n1)
        T.wait(trk["b"]["ev"][0])
        T.wait(trk["b"]["ev"][1])
        for r in range(ND):
            T.op(lambda e, r=r, i=i: e.matmul(
                pkA[:], xn1_fm[:, r, 128 * i:128 * (i + 1)], WvT[:, r, :],
                start=(r == 0), stop=False), inc=0)
        t_vb = T.op(lambda e: e.matmul(pkA[:], ones_f1[0:1, 0:128], bv_r[:],
                                       start=False, stop=True))
        S.wait(t_vb)
        t_ve = S.op(lambda e: e.copy(v_st[:], pkA[:]))
        G.wait(t_ve)
        t_vd = G.dma(io["Vtm"][128 * i:128 * (i + 1), :], v_st[:])
        S.wait(t_vd)
        T.wait(t_ve)
    P.barrier()
    for r in range(ND):
        G.dma(io["xn1fm"][128 * r:128 * (r + 1), :], xn1_fm[:, r, :])
    t_qd = None
    for (WT, b_r, oname, nrow) in [(WqT, bq_r, "Qfm", qn_sb), (WkT, bk_r, "Kfm", kn_sb)]:
        for g in range(4):
            for r in range(ND):
                T.op(lambda e, r=r, g=g, WT=WT: e.matmul(
                    ppA[:], WT[:, r, :], xn1_fm[:, r, 512 * g:512 * (g + 1)],
                    start=(r == 0), stop=False), inc=0)
            t_qb = T.op(lambda e, b_r=b_r: e.matmul(
                ppA[:], b_r[:, :], ones_f1[0:1, :], start=False, stop=True))
            S.wait(t_qb)
            if t_qd is not None:
                S.wait(t_qd)
            t_qe = S.op(lambda e: e.copy(q_st[:], ppA[:]))
            G.wait(t_qe)
            t_qd = G.dma(io[oname][:, 512 * g:512 * (g + 1)], q_st[:])
            S.wait(t_qe)
            t_qs = S.op(lambda e: e.activation(qsq[:], q_st[:], AF.Square))
            T.wait(t_qs)
            t_nr = T.op(lambda e: e.matmul(prow[:], ones_bf[:], qsq[:], start=True, stop=True))
            S.wait(t_nr)
            t_sr = S.op(lambda e, nrow=nrow, g=g: e.activation(
                nrow[:, 512 * g:512 * (g + 1)], prow[:], AF.Sqrt))
            T.wait(t_sr)
    P.barrier([S, G])
    G.dma(io["qn"], qn_sb[:])
    G.dma(io["kn"], kn_sb[:])
    if "kmax" in io:
        kml = P.sb("d_kml", [1, 2], F32)
        P.barrier([V, G])
        for bi in range(2):
            tv = V.op(lambda e, bi=bi: e.reduce_max(
                kml[0:1, bi:bi + 1], kn_sb[0:1, 1024 * bi:1024 * (bi + 1)], axis=AX.X))
            V.wait(tv)
        G.wait(tv)
        G.dma(io["kmax"], kml[:])


NSLOT = 64
S128 = float(128) ** -0.5


def build_E():
    nc = bass.Bass("TRN2", target_bir_lowering=False, debug=False)
    io = {}
    io["Qfm"] = nc.dram_tensor("Qfm", [128, TPC], BF16, kind="ExternalInput").ap()
    io["Kpk"] = nc.dram_tensor("Kpk", [128, 2 * NSLOT * 128], BF16, kind="ExternalInput").ap()
    io["Vpk"] = nc.dram_tensor("Vpk", [2 * NSLOT * 128, 128], BF16, kind="ExternalInput").ap()
    io["kn01"] = nc.dram_tensor("kn01", [128, 2 * NSLOT], BF16, kind="ExternalInput").ap()
    io["knpk"] = nc.dram_tensor("knpk", [1, 2 * NSLOT * 128], F32, kind="ExternalInput").ap()
    io["qn"] = nc.dram_tensor("qn", [1, TPC], F32, kind="ExternalInput").ap()
    io["Wo"] = nc.dram_tensor("Wo", [D, 128], F32, kind="ExternalInput").ap()
    io["bo"] = nc.dram_tensor("bo", [D], F32, kind="ExternalInput").ap()
    io["s1"] = nc.dram_tensor("s1", [TPC, D], F32, kind="ExternalOutput").ap()
    with ExitStack() as stack:
        P = Prog(nc, stack)
        emit_E(P, io)
        P.finish()
    return nc


def emit_E(P, io):
    Y, G, T, V, S = P.Y, P.G, P.T, P.V, P.S
    Qf = P.sb("e_Q", [128, TPC], BF16)
    Kf = P.sb("e_K", [128, 2 * NSLOT * 128], BF16)
    Vt = P.sb("e_V", [128, 2 * NSLOT, 128], BF16)
    kn01 = P.sb("e_kn01", [128, 2 * NSLOT], BF16)
    knr = P.sb("e_knr", [1, 2 * NSLOT * 128], F32)
    qnr = P.sb("e_qnr", [1, TPC], F32)
    kmax = P.sb("e_kmax", [1, 1], F32)
    cbrow = P.sb("e_cbrow", [1, 512], F32)
    cb = P.sb("e_cb", [128, 512], F32)
    tsub = P.sb("e_tsub", [128, 512], F32)
    Pt = [P.sb(f"e_Pt{i}", [128, 512], BF16) for i in range(2)]
    srow = P.sb("e_srow", [1, 512], F32)
    rrow = P.sb("e_rrow", [1, 512], F32)
    rb = P.sb("e_rb", [128, 512], F32)
    Ofm = P.sb("e_Ofm", [128, 512], BF16)
    Wpad = P.sb("e_Wpad", [128, ND, 128], BF16)
    WoT = P.sb("e_WoT", [128, ND, 128], BF16)
    bo_r = P.sb("e_bor", [1, D], F32)
    s1_t = P.sb("e_s1t", [128, D], F32)
    ident = P.sb("e_id", [128, 128], BF16)
    ones_f1 = P.sb("e_o1", [1, 512], F32)
    masks = [P.sb(f"e_msk{d}", [128, 512], BF16) for d in range(4)]

    pStA = P.ps("e_pStA", [128, 512])
    pStB = P.ps("e_pStB", [128, 512])
    pSum = P.ps("e_pSum", [1, 512])
    pO = P.ps("e_pO", [128, 512])
    ptA = P.ps("e_ptA", [128, 128], BF16)

    make_ident(P, ident[:])
    G.op(lambda e: e.memset(ones_f1[:], 1.0))
    for d in range(4):
        tz = G.op(lambda e, d=d: e.memset(masks[d][:], 1.0))
        G.wait(tz)
        G.op(lambda e, d=d: e.affine_select(
            out=masks[d][:], in_=masks[d][:], compare_op=ALU.is_ge, fill=0.0,
            base=-128 * d, pattern=[[1, 512]], channel_multiplier=-1))
    Y.dma(Qf[:], io["Qfm"])
    Y.dma(Kf[:], io["Kpk"])
    Y.dma(Vt[:], io["Vpk"].rearrange("(j p) h -> p j h", p=128))
    Y.dma(kn01[:], io["kn01"])
    Y.dma(knr[:], io["knpk"])
    Y.dma(qnr[:], io["qn"])
    Y.dma(bo_r[:], io["bo"][None, :])
    t_wo = G.dma(Wpad[:], io["Wo"].rearrange("(r p) h -> p r h", p=128))
    P.barrier()
    for r in range(ND):
        tmm = T.op(lambda e, r=r: e.transpose(ptA[:], Wpad[:, r, :], ident[:]))
        V.wait(tmm)
        ev = V.op(lambda e, r=r: e.tensor_copy(WoT[:, r, :], ptA[:]))
        T.wait(ev)
    P.barrier()

    t_sub = [None, None]
    t_PtRd = [None, None]
    t_oev = None
    t_se = None
    for bi in range(2):
        tv = V.op(lambda e, bi=bi: e.reduce_max(
            kmax[:], knr[:, 8192 * bi:8192 * (bi + 1)], axis=AX.X))
        V.wait(tv)
        for qc in range(2):
            qlo = 1024 * bi + 512 * qc
            tv = V.op(lambda e, qlo=qlo: e.tensor_scalar(
                cbrow[:], qnr[:, qlo:qlo + 512], kmax[0:1, 0:1], None, ALU.mult))
            T.wait(tv)
            t_cb = T.op(lambda e: e.matmul(pStA[:], ones_f1[0:1, 0:128], cbrow[:],
                                           start=True, stop=True))
            S.wait(t_cb)
            t_cbe = S.op(lambda e: e.copy(cb[:], pStA[:]))
            V.wait(t_cbe)
            T.wait(t_cbe)
            slots = []
            for j in range(56):
                slots.append((j, None))
            if qc == 0:
                for d in range(4):
                    slots.append((56 + d, d))
            else:
                for d in range(4):
                    slots.append((56 + d, None))
                for d in range(4):
                    slots.append((60 + d, d))
            nsl = len(slots)
            for si, (j, mband) in enumerate(slots):
                jj = NSLOT * bi + j
                sb_ = si % 2
                pSt = pStA if sb_ == 0 else pStB
                if t_sub[sb_] is not None:
                    T.wait(t_sub[sb_])
                t_st = T.op(lambda e, jj=jj, qlo=qlo, pSt=pSt: e.matmul(
                    pSt[:], Kf[:, 128 * jj:128 * (jj + 1)], Qf[:, qlo:qlo + 512],
                    start=True, stop=True))
                V.wait(t_st)
                if t_se is not None:
                    V.wait(t_se)
                t_sub[sb_] = V.op(lambda e, pSt=pSt: e.tensor_tensor(
                    tsub[:], pSt[:], cb[:], ALU.subtract))
                pb = si % 2
                if t_PtRd[pb] is not None:
                    S.wait(t_PtRd[pb])
                S.wait(t_sub[sb_])
                t_se = S.op(lambda e, pb=pb: e.activation(
                    Pt[pb][:], tsub[:], AF.Exp, scale=S128))
                src_tok = t_se
                if mband is not None:
                    V.wait(t_se)
                    src_tok = V.op(lambda e, pb=pb, mband=mband: e.tensor_tensor(
                        Pt[pb][:], Pt[pb][:], masks[mband][:], ALU.mult))
                T.wait(src_tok)
                T.op(lambda e, jj=jj, pb=pb, si=si, nsl=nsl: e.matmul(
                    pSum[:], kn01[:, jj:jj + 1], Pt[pb][:],
                    start=(si == 0), stop=(si == nsl - 1)), inc=0)
                t_av = T.op(lambda e, jj=jj, pb=pb, si=si, nsl=nsl: e.matmul(
                    pO[:], Vt[:, jj, :], Pt[pb][:],
                    start=(si == 0), stop=(si == nsl - 1)))
                t_PtRd[pb] = t_av
            S.wait(t_av)
            if t_oev is not None:
                S.wait(t_oev)
            t_sre = S.op(lambda e: e.copy(srow[:], pSum[:]))
            V.wait(t_sre)
            tv = V.op(lambda e: e.reciprocal(rrow[:], srow[:]))
            T.wait(tv)
            T.wait(t_sub[0] if t_sub[0] else None)
            T.wait(t_sub[1] if t_sub[1] else None)
            t_rb = T.op(lambda e: e.matmul(pStA[:], ones_f1[0:1, 0:128], rrow[:],
                                           start=True, stop=True))
            S.wait(t_rb)
            t_rbe = S.op(lambda e: e.copy(rb[:], pStA[:]))
            V.wait(t_rbe)
            t_onrm = V.op(lambda e: e.tensor_tensor(Ofm[:], pO[:], rb[:], ALU.mult))
            t_sub[0] = t_onrm
            for tc in range(4):
                for dh in range(2):
                    pSt = pStA if dh == 0 else pStB
                    T.wait(t_onrm)
                    T.op(lambda e, tc=tc, dh=dh, pSt=pSt: e.matmul(
                        pSt[:], Ofm[:, 128 * tc:128 * (tc + 1)],
                        WoT[:, 4 * dh:4 * (dh + 1), :].rearrange("p a b -> p (a b)"),
                        start=True, stop=False), inc=0)
                    t_pr = T.op(lambda e, dh=dh, pSt=pSt: e.matmul(
                        pSt[:], ones_f1[0:1, 0:128], bo_r[:, 512 * dh:512 * (dh + 1)],
                        start=False, stop=True))
                    S.wait(t_pr)
                    t_pe = S.op(lambda e, dh=dh, pSt=pSt: e.copy(
                        s1_t[:, 512 * dh:512 * (dh + 1)], pSt[:]))
                G.wait(t_pe)
                t_oev = G.dma(io["s1"][qlo + 128 * tc:qlo + 128 * (tc + 1), :], s1_t[:])
                S.wait(t_oev)
                T.wait(t_pe)
            t_sub = [t_pe, t_pe]
            t_PtRd = [None, None]


def pack_attention(c, Kfm_seqs, Vtm_seqs, kn_seqs):
    Kpk = np.zeros((128, 2 * NSLOT * 128), dtype=BF)
    Vpk = np.zeros((2 * NSLOT * 128, 128), dtype=BF)
    kn01 = np.zeros((128, 2 * NSLOT), dtype=BF)
    knpk = np.zeros((1, 2 * NSLOT * 128), dtype=np.float32)
    for bi in range(2):
        s, qb = bi, c
        nreal_full = 8 * qb
        for j in range(nreal_full):
            Kpk[:, 128 * (NSLOT * bi + j):128 * (NSLOT * bi + j + 1)] = \
                Kfm_seqs[s][:, 128 * j:128 * (j + 1)]
            Vpk[128 * (NSLOT * bi + j):128 * (NSLOT * bi + j + 1)] = \
                Vtm_seqs[s][128 * j:128 * (j + 1)]
            kn01[:, NSLOT * bi + j] = 1
            knpk[0, 128 * (NSLOT * bi + j):128 * (NSLOT * bi + j + 1)] = \
                kn_seqs[s][128 * j:128 * (j + 1)]
        for d in range(8):
            j = 8 * qb + d
            sl = NSLOT * bi + 56 + d
            Kpk[:, 128 * sl:128 * (sl + 1)] = Kfm_seqs[s][:, 128 * j:128 * (j + 1)]
            Vpk[128 * sl:128 * (sl + 1)] = Vtm_seqs[s][128 * j:128 * (j + 1)]
            kn01[:, sl] = 1
            knpk[0, 128 * sl:128 * (sl + 1)] = kn_seqs[s][128 * j:128 * (j + 1)]
    return {"Kpk": Kpk, "Vpk": Vpk, "kn01": kn01, "knpk": knpk}


NF1 = 16


def build_F():
    nc = bass.Bass("TRN2", target_bir_lowering=False, debug=False)
    io = {}
    io["xn1fm"] = nc.dram_tensor("xn1fm", [D, TPC], BF16, kind="ExternalInput").ap()
    io["w1"] = nc.dram_tensor("w1", [TPC, NE], F32, kind="ExternalInput").ap()
    io["Wu"] = nc.dram_tensor("Wu", [NE, D, 2048], F32, kind="ExternalInput").ap()
    io["Wd"] = nc.dram_tensor("Wd", [NE, 2048, D], F32, kind="ExternalInput").ap()
    io["W1s"] = nc.dram_tensor("W1s", [D, 2048], F32, kind="ExternalInput").ap()
    io["W2s"] = nc.dram_tensor("W2s", [2048, D], F32, kind="ExternalInput").ap()
    io["x1"] = nc.dram_tensor("x1", [TPC, D], F32, kind="ExternalInput").ap()
    io["s1"] = nc.dram_tensor("s1", [TPC, D], F32, kind="ExternalInput").ap()
    io["out"] = nc.dram_tensor("out", [TPC, D], F32, kind="ExternalOutput").ap()
    with ExitStack() as stack:
        P = Prog(nc, stack)
        emit_F(P, io)
        P.finish()
    return nc


def emit_F(P, io):
    Y, G, T, V, S = P.Y, P.G, P.T, P.V, P.S
    NFC1 = 16
    xn1 = P.sb("f_xn1", [128, ND, TPC], BF16)
    Hbuf = P.sb("f_H", [128, NFC1, TH], BF16)
    W1b = [P.sb(f"f_W1{i}", [128, ND, 512], BF16) for i in range(2)]
    W2b = [P.sb(f"f_W2{i}", [128, 8, 512], BF16) for i in range(2)]
    Facc = P.sb("f_Facc", [128, ND, D], F32)
    w_all = P.sb("f_w", [128, NT, NE], F32)
    g1_t = P.sb("f_g1", [128, D], F32)
    x1_t = P.sb("f_x1", [128, 2, D], F32)
    s1_t = P.sb("f_s1", [128, 2, D], F32)
    out_t = P.sb("f_out", [128, 2, D], F32)

    pH = [P.ps(f"f_pH{i}", [128, 512]) for i in range(4)]
    pF = [P.ps(f"f_pF{i}", [128, 512]) for i in range(4)]

    for r in range(ND):
        Y.dma(xn1[:, r, :], io["xn1fm"][128 * r:128 * (r + 1), :])
    Y.dma(w_all[:], io["w1"].rearrange("(i p) e -> p i e", p=128))
    P.barrier()

    t_pHev = [None] * 4
    t_pFev = [None] * 4
    t_w1 = [None, None]
    t_w2 = [None, None]
    t_ob = [None, None]
    t_g1last = None
    for th in range(2):
        t_fz = None
        for e8 in range(NE + 1):
            shared = (e8 == NE)
            t_l2 = [None, None]

            def w2load(dh, fch, e8=e8, shared=shared):
                bw = fch
                if t_w2[bw] is not None:
                    G.wait(t_w2[bw])
                if shared:
                    src = io["W2s"][1024 * fch:1024 * (fch + 1),
                                    512 * dh:512 * (dh + 1)]
                else:
                    src = io["Wd"][e8][1024 * fch:1024 * (fch + 1),
                                       512 * dh:512 * (dh + 1)]
                return G.dma(W2b[bw][:], src.rearrange("(fc p) d -> p fc d", p=128))
            gidx = 0
            for quar in range(4):
                bq = quar % 2
                if t_w1[bq] is not None:
                    G.wait(t_w1[bq])
                if shared:
                    src = io["W1s"][:, 512 * quar:512 * (quar + 1)]
                else:
                    src = io["Wu"][e8, :, 512 * quar:512 * (quar + 1)]
                t_l = G.dma(W1b[bq][:], src.rearrange("(r p) f -> p r f", p=128))
                if quar == 0:
                    t_l2[0] = w2load(0, 0)
                    t_l2[1] = w2load(0, 1)
                for fl in range(4):
                    f = 4 * quar + fl
                    for tt2 in range(2):
                        bh = gidx % 4
                        gidx += 1
                        if t_pHev[bh] is not None:
                            T.wait(t_pHev[bh])
                        T.wait(t_l)
                        for r in range(ND):
                            t_mm = T.op(lambda e, r=r, fl=fl, tt2=tt2, th=th, bq=bq,
                                        bh=bh: e.matmul(
                                pH[bh][:], W1b[bq][:, r, 128 * fl:128 * (fl + 1)],
                                xn1[:, r, 1024 * th + 512 * tt2:
                                    1024 * th + 512 * (tt2 + 1)],
                                start=(r == 0), stop=(r == ND - 1)),
                                inc=(1 if r == ND - 1 else 0))
                        S.wait(t_mm)
                        ev = S.op(lambda e, f=f, tt2=tt2, bh=bh: e.activation(
                            Hbuf[:, f, 512 * tt2:512 * (tt2 + 1)], pH[bh][:],
                            AF.Gelu))
                        t_pHev[bh] = ev
                t_w1[bq] = (T.sem, T.n)
            if e8 == 0:
                if t_g1last is not None:
                    G.wait(t_g1last)
                t_fz = G.op(lambda e: e.memset(Facc[:], 0.0))
            S_H_done = (S.sem, S.n)
            fidx = 0
            for dh in range(2):
                for fch in range(2):
                    bw = fch
                    T.wait(t_l2[bw])
                    T.wait(S_H_done)
                    for tc in range(ND):
                        bb = fidx % 4
                        fidx += 1
                        if t_pFev[bb] is not None:
                            T.wait(t_pFev[bb])
                        for fc in range(8):
                            t_mm = T.op(lambda e, fc=fc, fch=fch, tc=tc, bw=bw,
                                        bb=bb: e.matmul(
                                pF[bb][:],
                                Hbuf[:, 8 * fch + fc, 128 * tc:128 * (tc + 1)],
                                W2b[bw][:, fc, :],
                                start=(fc == 0), stop=(fc == 7)),
                                inc=(1 if fc == 7 else 0))
                        V.wait(t_mm)
                        V.wait(t_fz)
                        if shared:
                            tv = V.op(lambda e, tc=tc, dh=dh, bb=bb:
                                      e.scalar_tensor_tensor(
                                Facc[:, tc, 512 * dh:512 * (dh + 1)], pF[bb][:],
                                1.0, Facc[:, tc, 512 * dh:512 * (dh + 1)],
                                ALU.mult, ALU.add))
                        else:
                            tv = V.op(lambda e, tc=tc, dh=dh, th=th, e8=e8, bb=bb:
                                      e.scalar_tensor_tensor(
                                Facc[:, tc, 512 * dh:512 * (dh + 1)], pF[bb][:],
                                w_all[:, 8 * th + tc, e8:e8 + 1],
                                Facc[:, tc, 512 * dh:512 * (dh + 1)],
                                ALU.mult, ALU.add))
                        t_pFev[bb] = tv
                    t_w2[bw] = (T.sem, T.n)
                    if dh == 0:
                        t_l2[fch] = w2load(1, fch)
        tok_comb = (V.sem, V.n)
        for tc in range(ND):
            b = tc % 2
            gtok = 1024 * th + 128 * tc
            if t_ob[b] is not None:
                Y.wait(t_ob[b])
            t_lx = Y.dma(x1_t[:, b, :], io["x1"][gtok:gtok + 128, :])
            t_ls = Y.dma(s1_t[:, b, :], io["s1"][gtok:gtok + 128, :])
            S.wait(tok_comb)
            t_g1 = S.op(lambda e, tc=tc: e.activation(g1_t[:], Facc[:, tc, :],
                                                      AF.Silu))
            V.wait(t_g1)
            V.wait(t_ls)
            tv = V.op(lambda e, b=b: e.tensor_tensor(g1_t[:], g1_t[:],
                                                     s1_t[:, b, :], ALU.mult))
            V.wait(tv)
            V.wait(t_lx)
            t_o = V.op(lambda e, b=b: e.tensor_tensor(out_t[:, b, :], x1_t[:, b, :],
                                                      g1_t[:], ALU.add))
            G.wait(t_o)
            t_od = G.dma(io["out"][gtok:gtok + 128, :], out_t[:, b, :])
            t_ob[b] = t_o
            t_g1last = t_g1
            S.wait(t_o)
            V.wait(t_od)
        if th == 1:
            P.barrier()


def kernel_unfused(**inputs):
    cc = np.ascontiguousarray
    x = cc(inputs["L0_attn_x"].astype(np.float32, copy=False))
    xs = shard_tokens(x)

    baseA = {"Wq": inputs["L0_attn_Wq"], "Wk": inputs["L0_attn_Wk"],
             "Wv": inputs["L0_attn_Wv"], "Wb": inputs["L0_attn_Wb"],
             "l0g": inputs["L0_ffn_gamma"], "l0b": inputs["L0_ffn_beta"],
             "Wr": inputs["L0_ffn_Wr"], "br": inputs["L0_ffn_br"]}
    baseA = {k: cc(v.astype(np.float32, copy=False)) for k, v in baseA.items()}
    ncA = build_A()
    resA = run_bass_kernel_spmd(
        ncA, [dict(baseA, x=cc(xs[c])) for c in range(NC)],
        core_ids=list(range(NC))).results

    def seq_fm(key):
        s0 = np.concatenate([resA[r][key][:, 0:1024] for r in range(NC)], axis=1)
        s1 = np.concatenate([resA[b][key][:, 1024:2048] for b in range(8)], axis=1)
        return [cc(s0), cc(s1)]

    def seq_tm(key):
        s0 = np.concatenate([resA[r][key][0:1024] for r in range(NC)], axis=0)
        s1 = np.concatenate([resA[b][key][1024:2048] for b in range(8)], axis=0)
        return [cc(s0), cc(s1)]

    def seq_ch(key):
        s0 = np.concatenate([resA[r][key][0:8] for r in range(NC)], axis=0)
        s1 = np.concatenate([resA[b][key][8:16] for b in range(8)], axis=0)
        return [cc(s0), cc(s1)]

    qfm_s = seq_fm("qfm"); kbar_s = seq_fm("kbar"); ktok_s = seq_tm("ktok")
    bv_s = seq_tm("bv"); beta_s = seq_tm("beta"); Tt_s = seq_ch("Tt"); KQu_s = seq_ch("KQu")
    ncB = build_B()
    inB = []
    for c in range(NC):
        s = 0 if c < 4 else 1
        j = c % 4
        inB.append({"qfm": qfm_s[s], "kbar": kbar_s[s], "ktok": ktok_s[s],
                    "bv": cc(bv_s[s][:, 256 * j:256 * (j + 1)]), "beta": beta_s[s],
                    "Tt": Tt_s[s], "KQu": KQu_s[s]})
    resB = run_bass_kernel_spmd(ncB, inB, core_ids=list(range(NC))).results

    baseC = {"W1": cc(inputs["L0_ffn_W1"].astype(np.float32, copy=False)),
             "b1": cc(inputs["L0_ffn_b1"].astype(np.float32, copy=False)),
             "W2": cc(inputs["L0_ffn_W2"].astype(np.float32, copy=False)),
             "b2": cc(inputs["L0_ffn_b2"].astype(np.float32, copy=False))}
    ncC = build_C()
    resC = run_bass_kernel_spmd(
        ncC, [dict(baseC, xn2fm=cc(resA[c]["xn2fm"]), w=cc(resA[c]["w"]))
              for c in range(NC)], core_ids=list(range(NC))).results

    o_seq = [np.concatenate([resB[4 * s + j]["o"] for j in range(4)], axis=1)
             for s in range(2)]
    o_sh = shard_tokens(np.stack(o_seq, axis=0))
    baseD = {"lng": inputs["L0_attn_ln_g"], "lnb": inputs["L0_attn_ln_b"],
             "Wq1": inputs["L1_attn_Wq"], "bq1": inputs["L1_attn_bq"],
             "Wk1": inputs["L1_attn_Wk"], "bk1": inputs["L1_attn_bk"],
             "Wv1": inputs["L1_attn_Wv"], "bv1": inputs["L1_attn_bv"],
             "Wg": inputs["L1_ffn_Wg"]}
    baseD = {k: cc(v.astype(np.float32, copy=False)) for k, v in baseD.items()}
    ncD = build_D()
    resD = run_bass_kernel_spmd(
        ncD, [dict(baseD, x=cc(xs[c]), stats=cc(resA[c]["stats"]),
                   o=cc(o_sh[c]), moe0=cc(resC[c]["moe0"])) for c in range(NC)],
        core_ids=list(range(NC))).results

    Kfm_seqs, Vtm_seqs, kn_seqs = [], [], []
    for s in range(2):
        if s == 0:
            Kfm = np.concatenate([resD[r]["Kfm"][:, 0:1024] for r in range(NC)], axis=1)
            Vtm = np.concatenate([resD[r]["Vtm"][0:1024] for r in range(NC)], axis=0)
            kn = np.concatenate([resD[r]["kn"][0, 0:1024] for r in range(NC)], axis=0)
        else:
            Kfm = np.concatenate([resD[b]["Kfm"][:, 1024:2048] for b in range(8)], axis=1)
            Vtm = np.concatenate([resD[b]["Vtm"][1024:2048] for b in range(8)], axis=0)
            kn = np.concatenate([resD[b]["kn"][0, 1024:2048] for b in range(8)], axis=0)
        Kfm_seqs.append(cc(Kfm)); Vtm_seqs.append(cc(Vtm)); kn_seqs.append(cc(kn))
    ncE = build_E()
    baseE = {"Wo": cc(inputs["L1_attn_Wo"].astype(np.float32, copy=False)),
             "bo": cc(inputs["L1_attn_bo"].astype(np.float32, copy=False))}
    inE = []
    for c in range(NC):
        pk = pack_attention(c, Kfm_seqs, Vtm_seqs, kn_seqs)
        inE.append(dict(baseE, Qfm=cc(resD[c]["Qfm"]), qn=cc(resD[c]["qn"]), **pk))
    resE = run_bass_kernel_spmd(ncE, inE, core_ids=list(range(NC))).results

    baseF = {"Wu": cc(inputs["L1_ffn_Wu"].astype(np.float32, copy=False)),
             "Wd": cc(inputs["L1_ffn_Wd"].astype(np.float32, copy=False)),
             "W1s": cc(inputs["L1_ffn_W1"].astype(np.float32, copy=False)),
             "W2s": cc(inputs["L1_ffn_W2"].astype(np.float32, copy=False))}
    ncF = build_F()
    resF = run_bass_kernel_spmd(
        ncF, [dict(baseF, xn1fm=cc(resD[c]["xn1fm"]), w1=cc(resD[c]["w1"]),
                   x1=cc(resD[c]["x1"]), s1=cc(resE[c]["s1"])) for c in range(NC)],
        core_ids=list(range(NC))).results

    out = unshard_tokens([resF[c]["out"] for c in range(NC)])
    return np.ascontiguousarray(out.astype(np.float32))


def emit_E2(P, io):
    Y, G, T, V, S = P.Y, P.G, P.T, P.V, P.S
    Qf = P.sb("e_Q", [128, TPC], BF16)
    Kg = P.sb("e_Kg", [128, ND, TPC], BF16)
    Vg = P.sb("e_Vg", [128, 128, 128], BF16)
    Kl = P.sb("e_Kl", [128, TPC], BF16)
    Vl = P.sb("e_Vl", [128, 16, 128], BF16)
    slotm = P.sb("e_slotm", [128, 128], F32)
    slotmb = P.sb("e_slotmb", [128, 128], BF16)
    kmr = P.sb("e_kmr", [1, 2], F32)
    qnr = P.sb("e_qnr", [1, TPC], F32)
    cbrow = P.sb("e_cbrow", [1, 512], F32)
    cb2 = P.sb("e_cb2", [128, 1024], F32)
    tsub2 = [P.sb(f"e_tsub{i}", [128, 1024], F32) for i in range(2)]
    Pt2 = [P.sb(f"e_Pt{i}", [128, 1024], BF16) for i in range(2)]
    srow = P.sb("e_srow", [1, 512], F32)
    rrow = P.sb("e_rrow", [1, 512], F32)
    rb = P.sb("e_rb", [128, 512], F32)
    Ofm = P.sb("e_Ofm", [128, 512], BF16)
    Wpad = P.sb("e_Wpad", [128, ND, 128], BF16)
    WoT = P.sb("e_WoT", [128, ND, 128], BF16)
    bo_r = P.sb("e_bor", [1, D], F32)
    s1_t = P.sb("e_s1t", [128, D], F32)
    ident = P.sb("e_id", [128, 128], BF16)
    ones_f1 = P.sb("e_o1", [1, 512], F32)
    ones_bf = P.sb("e_obf", [128, 1], BF16)
    masks = [P.sb(f"e_msk{d}", [128, 512], BF16) for d in range(4)]

    pSt2 = [P.ps(f"e_pSt2{i}", [128, 1024]) for i in range(2)]
    pSum = P.ps("e_pSum", [1, 512])
    pO = P.ps("e_pO", [128, 512])
    ptA = P.ps("e_ptA", [128, 128], BF16)

    make_ident(P, ident[:])
    G.op(lambda e: e.memset(ones_f1[:], 1.0))
    G.op(lambda e: e.memset(ones_bf[:], 1.0))
    for d in range(4):
        tz = G.op(lambda e, d=d: e.memset(masks[d][:], 1.0))
        G.wait(tz)
        G.op(lambda e, d=d: e.affine_select(
            out=masks[d][:], in_=masks[d][:], compare_op=ALU.is_ge, fill=0.0,
            base=-128 * d, pattern=[[1, 512]], channel_multiplier=-1))
    Y.dma(Qf[:], io["Qloc"])
    for r in range(ND):
        Y.dma(Kg[:, r, :], io["KG"][128 * r:128 * (r + 1), :])
    Y.dma(Vg[:], io["VG"].rearrange("(j p) h -> p j h", p=128))
    Y.dma(Kl[:], io["Kloc"])
    Y.dma(Vl[:], io["Vloc"].rearrange("(j p) h -> p j h", p=128))
    t_sm = Y.dma(slotm[:], io["slotmask"])
    V.wait(t_sm)
    V.op(lambda e: e.tensor_copy(slotmb[:], slotm[:]))
    Y.dma(kmr[:], io["kmR"])
    Y.dma(qnr[:], io["qnloc"])
    Y.dma(bo_r[:], io["bo"][None, :])
    G.dma(Wpad[:], io["Wo"].rearrange("(r p) h -> p r h", p=128))
    P.barrier()
    for r in range(ND):
        tmm = T.op(lambda e, r=r: e.transpose(ptA[:], Wpad[:, r, :], ident[:]))
        V.wait(tmm)
        ev = V.op(lambda e, r=r: e.tensor_copy(WoT[:, r, :], ptA[:]))
        T.wait(ev)
    for bi in range(2):
        for kc in range(64):
            jv = 16 * (kc // 8) + 8 * bi + (kc % 8)
            V.op(lambda e, jv=jv, bi=bi, kc=kc: e.tensor_scalar(
                Vg[:, jv, :], Vg[:, jv, :], slotm[:, 64 * bi + kc:64 * bi + kc + 1],
                None, ALU.mult))
    P.barrier()

    t_sub2 = [None, None]
    t_PtRd2 = [None, None]
    t_se2 = [None, None]
    t_oev = None
    for bi in range(2):
        for qc in range(2):
            qlo = 1024 * bi + 512 * qc
            tv = V.op(lambda e, qlo=qlo, bi=bi: e.tensor_scalar(
                cbrow[:], qnr[:, qlo:qlo + 512], kmr[0:1, bi:bi + 1], None, ALU.mult))
            T.wait(tv)
            if t_sub2[1] is not None:
                T.wait(t_sub2[1])
            t_cb = T.op(lambda e: e.matmul(pSt2[1][:, 0:512], ones_f1[0:1, 0:128],
                                           cbrow[:], start=True, stop=True))
            S.wait(t_cb)
            S.op(lambda e: e.copy(cb2[:, 0:512], pSt2[1][:, 0:512]))
            t_cbe = S.op(lambda e: e.copy(cb2[:, 512:1024], pSt2[1][:, 0:512]))
            V.wait(t_cbe)
            T.wait(t_cbe)
            slots = [("g", kc, None) for kc in range(64)]
            if qc == 0:
                slots += [("l", d, d) for d in range(4)]
            else:
                slots += [("l", d, None) for d in range(4)]
                slots += [("l", 4 + d, d) for d in range(4)]
            pairs = [(slots[2 * i], slots[2 * i + 1]) for i in range(len(slots) // 2)]
            npair = len(pairs)
            for pi, pr in enumerate(pairs):
                pb = pi % 2
                if t_sub2[pb] is not None:
                    T.wait(t_sub2[pb])
                for h, (kind, idx2, mband) in enumerate(pr):
                    if kind == "g":
                        r_, w_ = idx2 // 8, idx2 % 8
                        t_st = T.op(lambda e, r_=r_, w_=w_, bi=bi, qlo=qlo, pb=pb,
                                    h=h: e.matmul(
                            pSt2[pb][:, 512 * h:512 * (h + 1)],
                            Kg[:, r_, 1024 * bi + 128 * w_:1024 * bi + 128 * (w_ + 1)],
                            Qf[:, qlo:qlo + 512], start=True, stop=True))
                    else:
                        t_st = T.op(lambda e, idx2=idx2, bi=bi, qlo=qlo, pb=pb,
                                    h=h: e.matmul(
                            pSt2[pb][:, 512 * h:512 * (h + 1)],
                            Kl[:, 1024 * bi + 128 * idx2:1024 * bi + 128 * (idx2 + 1)],
                            Qf[:, qlo:qlo + 512], start=True, stop=True))
                V.wait(t_st)
                if t_se2[pb] is not None:
                    V.wait(t_se2[pb])
                t_sub2[pb] = V.op(lambda e, pb=pb: e.tensor_tensor(
                    tsub2[pb][:], pSt2[pb][:], cb2[:], ALU.subtract))
                if t_PtRd2[pb] is not None:
                    S.wait(t_PtRd2[pb])
                S.wait(t_sub2[pb])
                t_se2[pb] = S.op(lambda e, pb=pb: e.activation(
                    Pt2[pb][:], tsub2[pb][:], AF.Exp, scale=S128))
                src_tok = t_se2[pb]
                for h, (kind, idx2, mband) in enumerate(pr):
                    if mband is not None:
                        V.wait(t_se2[pb])
                        src_tok = V.op(lambda e, pb=pb, h=h, mband=mband:
                                       e.tensor_tensor(
                            Pt2[pb][:, 512 * h:512 * (h + 1)],
                            Pt2[pb][:, 512 * h:512 * (h + 1)],
                            masks[mband][:], ALU.mult))
                T.wait(src_tok)
                for h, (kind, idx2, mband) in enumerate(pr):
                    first = (pi == 0 and h == 0)
                    last = (pi == npair - 1 and h == 1)
                    if kind == "g":
                        lh = lambda e, bi=bi, idx2=idx2: slotmb[:, 64 * bi + idx2:
                                                                64 * bi + idx2 + 1]
                    else:
                        lh = lambda e: ones_bf[:]
                    T.op(lambda e, pb=pb, h=h, first=first, last=last, lh=lh:
                         e.matmul(pSum[:], lh(e),
                                  Pt2[pb][:, 512 * h:512 * (h + 1)],
                                  start=first, stop=last), inc=0)
                    if kind == "g":
                        r_, w_ = idx2 // 8, idx2 % 8
                        jv = 16 * r_ + 8 * bi + w_
                        t_av = T.op(lambda e, jv=jv, pb=pb, h=h, first=first,
                                    last=last: e.matmul(
                            pO[:], Vg[:, jv, :], Pt2[pb][:, 512 * h:512 * (h + 1)],
                            start=first, stop=last))
                    else:
                        jl = 8 * bi + idx2
                        t_av = T.op(lambda e, jl=jl, pb=pb, h=h, first=first,
                                    last=last: e.matmul(
                            pO[:], Vl[:, jl, :], Pt2[pb][:, 512 * h:512 * (h + 1)],
                            start=first, stop=last))
                t_PtRd2[pb] = t_av
            S.wait(t_av)
            if t_oev is not None:
                S.wait(t_oev)
            t_sre = S.op(lambda e: e.copy(srow[:], pSum[:]))
            V.wait(t_sre)
            tv = V.op(lambda e: e.reciprocal(rrow[:], srow[:]))
            T.wait(tv)
            if t_sub2[0] is not None:
                T.wait(t_sub2[0])
            if t_sub2[1] is not None:
                T.wait(t_sub2[1])
            t_rb = T.op(lambda e: e.matmul(pSt2[1][:, 0:512], ones_f1[0:1, 0:128],
                                           rrow[:], start=True, stop=True))
            S.wait(t_rb)
            t_rbe = S.op(lambda e: e.copy(rb[:], pSt2[1][:, 0:512]))
            V.wait(t_rbe)
            t_onrm = V.op(lambda e: e.tensor_tensor(Ofm[:], pO[:], rb[:], ALU.mult))
            for tc in range(4):
                for dh in range(2):
                    T.wait(t_onrm)
                    T.op(lambda e, tc=tc, dh=dh: e.matmul(
                        pSt2[0][:, 512 * dh:512 * (dh + 1)],
                        Ofm[:, 128 * tc:128 * (tc + 1)],
                        WoT[:, 4 * dh:4 * (dh + 1), :].rearrange("p a b -> p (a b)"),
                        start=True, stop=False), inc=0)
                    t_pr = T.op(lambda e, dh=dh: e.matmul(
                        pSt2[0][:, 512 * dh:512 * (dh + 1)], ones_f1[0:1, 0:128],
                        bo_r[:, 512 * dh:512 * (dh + 1)], start=False, stop=True))
                    S.wait(t_pr)
                    t_pe = S.op(lambda e, dh=dh: e.copy(
                        s1_t[:, 512 * dh:512 * (dh + 1)],
                        pSt2[0][:, 512 * dh:512 * (dh + 1)]))
                G.wait(t_pe)
                t_oev = G.dma(io["s1"][qlo + 128 * tc:qlo + 128 * (tc + 1), :], s1_t[:])
                S.wait(t_oev)
                T.wait(t_pe)
            t_sub2 = [t_pe, t_pe]
            t_PtRd2 = [None, None]



def build_full():
    from concourse.aot_env import allow_ici_all_to_all
    nc = bass.Bass("TRN2", target_bir_lowering=False, debug=False)
    io = {}
    ext_in = [("x", [TPC, D]), ("Wq", [D, D]), ("Wk", [D, D]), ("Wv", [D, D]),
              ("Wb", [D]), ("l0g", [D]), ("l0b", [D]), ("Wr", [D, 8]), ("br", [8]),
              ("W1", [NE, D, 4096]), ("b1", [NE, 4096]), ("W2", [NE, 4096, D]),
              ("b2", [NE, D]), ("lng", [D]), ("lnb", [D]),
              ("Wq1", [128, D]), ("bq1", [128]), ("Wk1", [128, D]), ("bk1", [128]),
              ("Wv1", [128, D]), ("bv1", [128]), ("Wo", [D, 128]), ("bo", [D]),
              ("Wg", [D, 8]), ("Wu", [NE, D, 2048]), ("Wd", [NE, 2048, D]),
              ("W1s", [D, 2048]), ("W2s", [2048, D])]
    for nm, shape in ext_in:
        io[nm] = nc.dram_tensor(nm, shape, F32, kind="ExternalInput").ap()
    io["slotmask"] = nc.dram_tensor("slotmask", [128, 128], F32, kind="ExternalInput").ap()
    io["out"] = nc.dram_tensor("out", [TPC, D], F32, kind="ExternalOutput").ap()
    ints = [("qfm", [D, TPC], BF16), ("kbar", [D, TPC], BF16), ("ktok", [TPC, D], BF16),
            ("bv", [TPC, D], BF16), ("beta", [TPC, 1], F32), ("Tt", [NT, 128, 128], BF16),
            ("KQu", [NT, 128, 128], BF16), ("xn2fm", [D, TPC], BF16),
            ("stats", [TPC, 2], F32), ("w", [TPC, 8], F32),
            ("xsend", [L, 3584], BF16), ("xr", [L, 3584], BF16),
            ("besend", [L, 1], F32), ("ber", [L, 1], F32),
            ("o", [L, DV], F32), ("orecv", [L, DV], F32),
            ("moe0", [TPC, D], F32),
            ("x1", [TPC, D], F32), ("xn1fm", [D, TPC], BF16),
            ("Qfm", [128, TPC], BF16), ("Kfm", [128, TPC], BF16),
            ("Vtm", [TPC, 128], BF16), ("qn", [1, TPC], F32), ("kn", [1, TPC], F32),
            ("w1", [TPC, 8], F32), ("kmax", [1, 2], F32),
            ("KG", [1024, TPC], BF16), ("VG", [8 * TPC, 128], BF16), ("kmR", [1, 2], F32),
            ("s1", [TPC, D], F32)]
    for nm, shape, dt in ints:
        io[nm] = nc.dram_tensor(nm, shape, dt, kind="Internal").ap()

    with ExitStack() as stack:
        P = Prog(nc, stack)
        Y, G, T, V, S = P.Y, P.G, P.T, P.V, P.S
        ccsems = [stack.enter_context(nc.semaphore(f"s_cc{i}")) for i in range(6)]
        ncc = [0]

        def coll(kind, op, ins_, outs_):
            sem = ccsems[ncc[0]]
            ncc[0] += 1

            def fn(e, k=kind, o=op, i=ins_, u=outs_):
                with allow_ici_all_to_all():
                    return e.collective_compute(
                        k, o, replica_groups=[list(range(8))], ins=[i], outs=[u])
            return G.op_sem(fn, sem, 1, inc=1)
        P.push_phase()
        emit_A(P, io)
        P.pop_phase()
        P.barrier()
        with allow_ici_all_to_all():
            t_cc1 = coll("AllToAll", ALU.bypass, io["xsend"], io["xr"])
            t_cc2 = coll("AllToAll", ALU.bypass, io["besend"], io["ber"])
        P.push_phase()
        emit_C(P, io)
        P.pop_phase()
        P.barrier()
        Y.wait(t_cc1)
        Y.wait(t_cc2)
        G.wait(t_cc1)
        G.wait(t_cc2)
        P.push_phase()
        emit_B(P, io)
        P.pop_phase()
        P.barrier()
        G.wait(G.dtok())
        with allow_ici_all_to_all():
            t_cc3 = coll("AllToAll", ALU.bypass, io["o"], io["orecv"])
        Y.wait(t_cc3)
        Y.wait(G.dtok())
        P.push_phase()
        emit_D(P, io)
        P.pop_phase()
        P.barrier()
        G.wait(G.dtok())
        t_cc4 = coll("AllGather", ALU.bypass, io["Kfm"], io["KG"])
        t_cc5 = coll("AllGather", ALU.bypass, io["Vtm"], io["VG"])
        t_cc6 = coll("AllReduce", ALU.max, io["kmax"], io["kmR"])
        for t in (t_cc4, t_cc5, t_cc6):
            Y.wait(t)
            G.wait(t)
        io2 = dict(io)
        io2["Qloc"] = io["Qfm"]
        io2["Kloc"] = io["Kfm"]
        io2["Vloc"] = io["Vtm"]
        io2["qnloc"] = io["qn"]
        P.push_phase()
        emit_E2(P, io2)
        P.pop_phase()
        P.barrier()
        Y.wait(G.dtok())
        P.push_phase()
        emit_F(P, io)
        P.pop_phase()
        P.finish()
    return nc


def host_slotmask(c):
    m = np.zeros((128, 128), dtype=np.float32)
    for bi in range(2):
        for kc in range(64):
            if kc < 8 * c:
                m[:, 64 * bi + kc] = 1
    return m


def kernel(**inputs):
    cc = np.ascontiguousarray
    x = cc(inputs["L0_attn_x"].astype(np.float32, copy=False))
    xs = shard_tokens(x)
    wmap = {"Wq": "L0_attn_Wq", "Wk": "L0_attn_Wk", "Wv": "L0_attn_Wv",
            "Wb": "L0_attn_Wb", "l0g": "L0_ffn_gamma", "l0b": "L0_ffn_beta",
            "Wr": "L0_ffn_Wr", "br": "L0_ffn_br", "W1": "L0_ffn_W1",
            "b1": "L0_ffn_b1", "W2": "L0_ffn_W2", "b2": "L0_ffn_b2",
            "lng": "L0_attn_ln_g", "lnb": "L0_attn_ln_b",
            "Wq1": "L1_attn_Wq", "bq1": "L1_attn_bq", "Wk1": "L1_attn_Wk",
            "bk1": "L1_attn_bk", "Wv1": "L1_attn_Wv", "bv1": "L1_attn_bv",
            "Wo": "L1_attn_Wo", "bo": "L1_attn_bo", "Wg": "L1_ffn_Wg",
            "Wu": "L1_ffn_Wu", "Wd": "L1_ffn_Wd", "W1s": "L1_ffn_W1",
            "W2s": "L1_ffn_W2"}
    base = {k: cc(inputs[v].astype(np.float32, copy=False)) for k, v in wmap.items()}
    nc = build_full()
    in_maps = [dict(base, x=cc(xs[c]), slotmask=host_slotmask(c)) for c in range(NC)]
    res = run_bass_kernel_spmd(nc, in_maps, core_ids=list(range(NC)))
    out = unshard_tokens([res.results[c]["out"] for c in range(NC)])
    return np.ascontiguousarray(out.astype(np.float32))
